# version 04
# baseline (speedup 1.0000x reference)
import math
import numpy as np
import ml_dtypes
from contextlib import ExitStack

import concourse.bass as bass
import concourse.mybir as mybir
from concourse.bass_utils import run_bass_kernel_spmd
from concourse.alu_op_type import AluOpType as OP

F32 = mybir.dt.float32
BF16 = mybir.dt.bfloat16
FP16 = mybir.dt.float16
I8 = mybir.dt.int8
U8 = mybir.dt.uint8
AF = mybir.ActivationFunctionType
AX = mybir.AxisListType

S = 2048
D = 1024
H = 16
HD = 64
DL = 256
NCH = 8
NT = 16
TB = 512
NTB = S // TB
LAM = 0.01
LN32 = math.log(math.sqrt(D))
EPS = 1e-5

ESIZE = {F32: 4, BF16: 2, FP16: 2, I8: 1, U8: 1}

L1_OFFS = [0, 1, 2, 3, 4, 6, 8, 12, 16, 24, 32, 48, 64, 96,
           128, 192, 256, 384, 512, 768, 1024, 1536]
L1_BAND = 1664
L1_WROW = 1792
RNE_C = 12582912.0

ENGINES = ("sync", "gpsimd", "tensor", "vector", "scalar")

N_STAGES = 4


class Sem:
    def __init__(self, h):
        self.h = h
        self.n = 0


class Buf:
    def __init__(self, parts, fs, base_ap, pitch=None, off=0):
        self.parts = parts
        self.fs = fs
        self.pitch = fs if pitch is None else pitch
        self.off = off
        self._base = base_ap

    def ap(self, p0=0, np_=None, f0=0, nf=None):
        np_ = (self.parts - p0) if np_ is None else np_
        nf = (self.fs - f0) if nf is None else nf
        return type(self._base)(self._base.tensor, p0 * self.pitch + self.off + f0,
                                [[self.pitch, np_], [1, nf]])

    def apd(self, offset, dims):
        return type(self._base)(self._base.tensor, self.off + offset, dims)


class Builder:
    def __init__(self, nc):
        self.nc = nc
        self.prog = {e: [] for e in ENGINES}
        self.es = ExitStack()
        self._i = 0

    def sem(self, name=None):
        h = self.es.enter_context(self.nc.semaphore(name or f"s{self._i}"))
        self._i += 1
        return Sem(h)

    def sbuf(self, name, parts, fs, dt, stack=None):
        h = (stack or self.es).enter_context(self.nc.sbuf_tensor(name, [parts, fs], dt))
        return Buf(parts, fs, h.ap())

    def psum(self, name, parts, fs, dt):
        h = self.es.enter_context(self.nc.psum_tensor(name, [parts, fs], dt))
        return Buf(parts, fs, h.ap())

    def em(self, e, f):
        self.prog[e].append(f)

    def wait(self, e, s, v=None):
        v = s.n if v is None else v
        if v > 0:
            self.em(e, lambda E, h=s.h, v=v: E.wait_ge(h, v))

    def run(self, e, f):
        self.em(e, f)

    def inc(self, e, s, f, k=1):
        s.n += k
        self.em(e, lambda E, f=f, h=s.h, k=k: f(E).then_inc(h, k))

    def barrier(self, pb, dummies=None):
        for e in ENGINES:
            pb.n += 1
            if dummies and e in dummies:
                self.em(e, lambda E, h=pb.h, f=dummies[e]: f(E).then_inc(h, 1))
            else:
                self.em(e, lambda E, h=pb.h: E.nop().then_inc(h, 1))
        tgt = pb.n
        for e in ENGINES:
            self.em(e, lambda E, h=pb.h, v=tgt: E.wait_ge(h, v))

    def finalize(self):
        with self.nc.Block() as block:
            for name in ENGINES:
                dec = getattr(block, name)
                prog = self.prog[name]

                def body(E, prog=prog):
                    for f in prog:
                        f(E)
                dec(body)
        self.es.close()


def dram_ap(t, offset, dims):
    base = t if hasattr(t, "tensor") else t.ap()
    return type(base)(base.tensor, offset, dims)


class Prog:
    def __init__(self, n_stages):
        self.n_stages = n_stages
        nc = bass.Bass("TRN2", target_bir_lowering=False)
        self.nc = nc
        B = Builder(nc)
        self.B = B

        self.x_in = nc.dram_tensor("x", [S, D], F32, kind="ExternalInput").ap()
        self.y_out = nc.dram_tensor("y", [S, D], F32, kind="ExternalOutput").ap()
        self.win = {}
        for nm, shp in [
            ("l0a_wq", [D, D]), ("l0a_wk", [D, D]), ("l0a_wv", [D, D]),
            ("l0a_wo", [D, D]), ("l0a_wdkv", [D, DL]),
            ("l0a_wupk", [DL, D]), ("l0a_wupv", [DL, D]),
            ("l0f_wexp", [D, 4 * D]), ("l0f_wcon", [4 * D, D]),
            ("l0f_bexp", [1, 4 * D]), ("l0f_upb", [1, 4 * D]),
            ("l0f_bcon", [1, D]), ("l0f_dnb", [1, D]),
            ("l0f_upw", [12, 8192]), ("l0f_dnw", [12, 8192]),
        ]:
            self.win[nm] = nc.dram_tensor(nm, shp, F32, kind="ExternalInput").ap()
        self.win["l0f_pmc"] = nc.dram_tensor("l0f_pmc", [128, 7 * 128], BF16,
                                             kind="ExternalInput").ap()
        for nm, shp in [
            ("l1a_wqkv", [D, 3 * D]), ("l1a_bqkv", [1, 3 * D]),
            ("l1a_wout", [D, D]), ("l1a_bout", [1, D]),
            ("l1a_wgate", [D, D]), ("l1a_bgate", [1, D]),
            ("l1a_pb", [44, H]),
        ]:
            self.win[nm] = nc.dram_tensor(nm, shp, F32, kind="ExternalInput").ap()
        self.win["l1a_gsel"] = nc.dram_tensor("l1a_gsel", [44, 23], BF16,
                                              kind="ExternalInput").ap()
        self.win["l1a_stc"] = nc.dram_tensor("l1a_stc", [23, S], BF16,
                                             kind="ExternalInput").ap()
        self.c2scr = nc.dram_tensor("c2scr", [H, S], BF16)
        for nm, shp in [
            ("l1f_wg", [2816, D]), ("l1f_wu", [2816, D]), ("l1f_wd", [D, 2816]),
            ("l1f_clipx", [1, 1]), ("l1f_clipd", [1, 1]),
        ]:
            self.win[nm] = nc.dram_tensor(nm, shp, F32, kind="ExternalInput").ap()
        self.hspill = nc.dram_tensor("hspill", [2816, S], F32)
        self.ccxi = nc.dram_tensor("ccxi", [1, 1], F32)
        self.ccxo = nc.dram_tensor("ccxo", [1, 1], F32)
        self.cchi = nc.dram_tensor("cchi", [1, 1], F32)
        self.ccho = nc.dram_tensor("ccho", [1, 1], F32)
        self.wrowscr = nc.dram_tensor("wrowscr", [H, L1_WROW], BF16)
        self.wscr23 = nc.dram_tensor("wscr23", [23 * 16], BF16)
        self.zscr2 = nc.dram_tensor("zscr2", [H, S], BF16)
        self.rowscr = nc.dram_tensor("rowscr", [8, S], F32)
        self.zscr = nc.dram_tensor("zscr", [2, H * S], BF16)
        self.xspill = nc.dram_tensor("xspill", [NCH * 128, S], F32)
        self.xspill2 = nc.dram_tensor("xspill2", [NCH * 128, S], F32)
        self.wescr = nc.dram_tensor("wescr", [8, 128 * 8 * 512], BF16)
        self.gcscr = nc.dram_tensor("gcscr", [128, 32 * 512], BF16)
        self.wcscr = nc.dram_tensor("wcscr", [8, 128 * 32 * 128], BF16)

        self.PB = B.sem("pb")
        self.ld = B.sem("ld")
        self.gl = B.sem("gl")
        self.mm = B.sem("mm")
        self.ev = B.sem("ev")
        self.vc = B.sem("vc")
        self.sc = B.sem("sc")
        self.gc = B.sem("gc")
        self.xs = B.sem("xs")
        self.pv = B.sem("pv")
        self.st = B.sem("st")
        self.lda = B.sem("lda")
        self.ldb = B.sem("ldb")
        self.gb = B.sem("gb")
        self.sta = B.sem("sta")
        self.stb = B.sem("stb")
        self.gwa = B.sem("gwa")
        self.gwb = B.sem("gwb")
        self.ccs = B.sem("ccs")

        self.xbf = B.sbuf("xbf", 128, NCH * S, BF16)
        self.eye32 = B.sbuf("eye32", 128, 128, F32)
        self.maskU = B.sbuf("masku", 128, 128, U8)
        self.pidx = B.sbuf("pidx", 128, 1, F32)
        self.fidx = B.sbuf("fidx", 128, 128, F32)
        self.bias1 = B.sbuf("bias1", 1, 4, F32)
        self.ones128 = B.sbuf("ones128", 128, 1, BF16)
        self.dume = B.sbuf("dume", 1, 4, F32)
        self.ta = B.sbuf("ta", 32, 256, F32)

        self.ps_a = B.psum("ps_a", 128, 2048, F32)
        self.ps_scr = B.psum("ps_scr", 128, 512, F32)
        self.ps_o1 = B.psum("ps_o1", 128, 512, F32)
        self.ps_o2 = B.psum("ps_o2", 128, 512, F32)
        self.ps_s = B.psum("ps_s", 128, 512, F32)
        self.pa4 = [self.ps_a.ap(0, 128, b * 512, 512) for b in range(4)]
        self._lni = 0
        self._xfi = 0
        self.bank_ev = {}

    def carve(self, byte_off, parts, nf, dt):
        bc = self.xf._base.bitcast(dt)
        row = bc.ap[0][0]
        return Buf(parts, nf, bc, pitch=row, off=byte_off // ESIZE[dt])

    def alloc_xf(self):
        self.s_xf = ExitStack()
        self.xf = self.B.sbuf(f"xf{self._xfi}", 128, NCH * S, F32, stack=self.s_xf)
        self._xfi += 1

    def barrier(self):
        B = self.B
        dummies = {
            "tensor": lambda E: E.matmul(self.ps_s.ap(0, 1, 0, 1),
                                         self.ones128.ap(0, 1, 0, 1),
                                         self.ones128.ap(0, 1, 0, 1), start=True, stop=True),
            "vector": lambda E: E.memset(self.dume.ap(0, 1, 0, 1), 0.0),
            "scalar": lambda E: E.activation(self.dume.ap(0, 1, 1, 1), self.dume.ap(0, 1, 1, 1), AF.Copy),
            "gpsimd": lambda E: E.memset(self.dume.ap(0, 1, 2, 1), 0.0),
        }
        B.wait("gpsimd", self.gc, self.gc.n)
        B.wait("gpsimd", self.gl, self.gl.n)
        B.wait("gpsimd", self.gb, self.gb.n)
        B.wait("gpsimd", self.st, self.st.n)
        B.wait("gpsimd", self.sta, self.sta.n)
        B.wait("gpsimd", self.stb, self.stb.n)
        B.wait("gpsimd", self.gwa, self.gwa.n)
        B.wait("gpsimd", self.gwb, self.gwb.n)
        B.wait("sync", self.ld, self.ld.n)
        B.wait("sync", self.lda, self.lda.n)
        B.wait("sync", self.ldb, self.ldb.n)
        B.barrier(self.PB, dummies)

    def mm_pipe(self, groups, psum_aps, evacs, evac_engine="scalar"):
        B = self.B
        NB = len(psum_aps)
        B.wait("tensor", self.ev, self.ev.n)
        for g, pairs in enumerate(groups):
            pt = psum_aps[g % NB]
            bank = pt.offset // 512
            prev = self.bank_ev.get(bank)
            if prev:
                B.wait("tensor", self.ev, prev)
            n = len(pairs)
            for i, (lh, rh) in enumerate(pairs):
                st_ = (i == 0)
                if i == n - 1:
                    B.inc("tensor", self.mm,
                          lambda E, lh=lh, rh=rh, pt=pt, st_=st_: E.matmul(pt, lh, rh, start=st_, stop=True))
                else:
                    B.run("tensor",
                          lambda E, lh=lh, rh=rh, pt=pt, st_=st_: E.matmul(pt, lh, rh, start=st_, stop=False))
            B.wait(evac_engine, self.mm)
            fns = evacs[g]
            for j, f in enumerate(fns):
                if j == len(fns) - 1:
                    B.inc(evac_engine, self.ev, lambda E, f=f, pt=pt: f(E, pt))
                else:
                    B.run(evac_engine, lambda E, f=f, pt=pt: f(E, pt))
            self.bank_ev[bank] = self.ev.n

    def cast_load_w(self, dst, dram, rows, cols):
        B = self.B
        for c in range(rows // 128):
            B.inc("gpsimd", self.gl, lambda E, c=c, dst=dst, dram=dram, cols=cols: E.dma_start(
                out=dst.ap(0, 128, c * cols, cols),
                in_=dram_ap(dram, c * 128 * cols, [[cols, 128], [1, cols]])), 16)

    def row_bcast(self, dram_off, n_elems, dst):
        B = self.B
        B.inc("sync", self.ld, lambda E, dram_off=dram_off, n_elems=n_elems, dst=dst: E.dma_start(
            out=dst.ap(0, 128, 0, n_elems),
            in_=dram_ap(self.rowscr, dram_off, [[0, 128], [1, n_elems]])), 16)

    def phase0(self):
        B = self.B
        s_p0 = ExitStack()
        xtok = [B.sbuf(f"xtok{i}", 128, D, F32, stack=s_p0) for i in range(2)]
        B.inc("gpsimd", self.gc, lambda E: E.iota(self.pidx.ap(), [[0, 1]], base=0,
                                         channel_multiplier=1, allow_small_or_imprecise_dtypes=True))
        B.inc("gpsimd", self.gc, lambda E: E.iota(self.fidx.ap(), [[1, 128]], base=0,
                                         channel_multiplier=0, allow_small_or_imprecise_dtypes=True))
        B.wait("vector", self.gc, 2)
        B.run("vector", lambda E: E.tensor_scalar(self.eye32.ap(), self.fidx.ap(), self.pidx.ap(), None, OP.is_equal))
        B.run("vector", lambda E: E.tensor_scalar(self.maskU.ap(), self.fidx.ap(), self.pidx.ap(), None, OP.is_lt))
        B.run("vector", lambda E: E.memset(self.dume.ap(0, 1, 1, 1), 0.0))
        B.run("vector", lambda E: E.memset(self.bias1.ap(0, 1, 0, 1), -LN32))
        B.run("vector", lambda E: E.memset(self.bias1.ap(0, 1, 1, 1), EPS))
        B.inc("vector", self.vc, lambda E: E.memset(self.ones128.ap(), 1.0))

        for t in range(NT):
            buf = xtok[t % 2]
            par = self.lda if t % 2 == 0 else self.ldb
            if t >= 2:
                B.wait("sync", self.vc, self.vc.n)
            B.inc("sync", par, lambda E, buf=buf, t=t: E.dma_start(
                out=buf.ap(), in_=dram_ap(self.x_in, t * 128 * D, [[D, 128], [1, D]])), 16)
            B.wait("tensor", par, par.n)
            if t == 0:
                B.wait("tensor", self.vc, 1)
            for hf in range(2):
                if t > 0 or hf > 0:
                    B.wait("tensor", self.vc, self.vc.n)
                for j in range(4):
                    fn = lambda E, buf=buf, hf=hf, j=j: E.transpose(
                        self.ps_a.ap(0, 128, hf * 1024 + j * 128, 128),
                        buf.ap(0, 128, (hf * 4 + j) * 128, 128), self.eye32.ap())
                    if j == 3:
                        B.inc("tensor", self.mm, fn)
                    else:
                        B.run("tensor", fn)
                B.wait("vector", self.mm, self.mm.n)

                def evx(E, t=t, hf=hf):
                    srcap = self.ps_a.apd(hf * 1024, [[2048, 128], [128, 4], [1, 128]])
                    dstap = self.xf.apd(4 * hf * S + t * 128, [[NCH * S, 128], [S, 4], [1, 128]])
                    return E.tensor_copy(dstap, srcap)
                B.inc("vector", self.vc, evx)
        B.wait("scalar", self.vc, self.vc.n)
        for c in range(NCH):
            B.inc("scalar", self.sc, lambda E, c=c: E.activation(
                self.xbf.ap(0, 128, c * S, S), self.xf.ap(0, 128, c * S, S), AF.Copy))
        self.barrier()
        s_p0.close()

    def ln_phase(self):
        B = self.B
        self._lni += 1
        s_ln = ExitStack()
        sqb = B.sbuf(f"sqb{self._lni}", 128, S, BF16, stack=s_ln)
        repa = B.sbuf(f"repa{self._lni}", 128, S, F32, stack=s_ln)
        repb = B.sbuf(f"repb{self._lni}", 128, S, F32, stack=s_ln)
        srow = B.sbuf(f"srow{self._lni}", 1, 4 * S, F32, stack=s_ln)
        for c in range(NCH):
            if c > 0:
                B.wait("scalar", self.mm, self.mm.n)
            B.inc("scalar", self.sc, lambda E, c=c: E.activation(
                self.xbf.ap(0, 128, c * S, S), self.xf.ap(0, 128, c * S, S), AF.Copy))
            B.inc("scalar", self.sc, lambda E, c=c, sqb=sqb: E.activation(
                sqb.ap(), self.xf.ap(0, 128, c * S, S), AF.Square))
            B.wait("tensor", self.sc, self.sc.n)
            for tb in range(NTB):
                B.run("tensor", lambda E, c=c, tb=tb: E.matmul(
                    self.ps_a.ap(0, 1, tb * TB, TB), self.ones128.ap(),
                    self.xbf.ap(0, 128, c * S + tb * TB, TB),
                    start=(c == 0), stop=(c == NCH - 1)))
                fn = lambda E, c=c, tb=tb, sqb=sqb: E.matmul(
                    self.ps_a.ap(32, 1, tb * TB, TB), self.ones128.ap(),
                    sqb.ap(0, 128, tb * TB, TB),
                    start=(c == 0), stop=(c == NCH - 1))
                if tb == NTB - 1:
                    B.inc("tensor", self.mm, fn)
                else:
                    B.run("tensor", fn)
        sr = srow
        B.wait("vector", self.mm, self.mm.n)
        B.inc("vector", self.vc, lambda E, sr=sr: E.tensor_scalar(
            sr.ap(0, 1, 2 * S, S), self.ps_a.apd(0, [[2048, 1], [512, 4], [1, 512]]),
            1.0 / D, None, OP.mult))
        B.inc("vector", self.vc, lambda E, sr=sr: E.tensor_scalar(
            sr.ap(0, 1, 3 * S, S), self.ps_a.apd(32 * 2048, [[2048, 1], [512, 4], [1, 512]]),
            1.0 / D, None, OP.mult))
        B.wait("vector", self.vc, self.vc.n)
        B.inc("vector", self.vc, lambda E, sr=sr: E.tensor_tensor(
            sr.ap(0, 1, 0, S), sr.ap(0, 1, 2 * S, S), sr.ap(0, 1, 2 * S, S), OP.mult))
        B.wait("vector", self.vc, self.vc.n)
        B.inc("vector", self.vc, lambda E, sr=sr: E.tensor_tensor(
            sr.ap(0, 1, 3 * S, S), sr.ap(0, 1, 3 * S, S), sr.ap(0, 1, 0, S), OP.subtract))
        B.wait("scalar", self.vc, self.vc.n)
        B.inc("scalar", self.sc, lambda E, sr=sr: E.activation(
            sr.ap(0, 1, 0, S), sr.ap(0, 1, 3 * S, S), AF.Sqrt,
            bias=self.bias1.ap(0, 1, 1, 1)))
        B.wait("vector", self.sc, self.sc.n)
        B.inc("vector", self.vc, lambda E, sr=sr: E.reciprocal(sr.ap(0, 1, 0, S), sr.ap(0, 1, 0, S)))
        B.wait("vector", self.vc, self.vc.n)
        B.inc("vector", self.vc, lambda E, sr=sr: E.tensor_tensor(
            sr.ap(0, 1, 1 * S, S), sr.ap(0, 1, 2 * S, S), sr.ap(0, 1, 0, S), OP.mult))
        B.wait("vector", self.vc, self.vc.n)
        B.inc("vector", self.vc, lambda E, sr=sr: E.tensor_scalar(
            sr.ap(0, 1, 1 * S, S), sr.ap(0, 1, 1 * S, S), -1.0, None, OP.mult))
        B.wait("gpsimd", self.vc, self.vc.n)
        B.inc("gpsimd", self.gb, lambda E, sr=sr: E.dma_start(
            out=dram_ap(self.rowscr, 0, [[1, 2 * S]]), in_=sr.ap(0, 1, 0, 2 * S)), 16)
        B.wait("sync", self.gb, self.gb.n)
        self.row_bcast(0, S, repa)
        self.row_bcast(S, S, repb)
        B.wait("vector", self.ld, self.ld.n)
        for c in range(NCH):
            B.inc("vector", self.vc, lambda E, c=c, repa=repa: E.tensor_tensor(
                self.xf.ap(0, 128, c * S, S), self.xf.ap(0, 128, c * S, S),
                repa.ap(), OP.mult))
            B.wait("vector", self.vc, self.vc.n)
            B.inc("vector", self.vc, lambda E, c=c, repb=repb: E.tensor_tensor(
                self.xf.ap(0, 128, c * S, S), self.xf.ap(0, 128, c * S, S),
                repb.ap(), OP.add))
            B.wait("scalar", self.vc, self.vc.n)
            B.inc("scalar", self.sc, lambda E, c=c: E.activation(
                self.xbf.ap(0, 128, c * S, S), self.xf.ap(0, 128, c * S, S), AF.Copy))
        self.barrier()
        s_ln.close()

    def l0_attn(self):
        B = self.B
        s_ot = ExitStack()
        OTu = B.sbuf("otu", 128, NCH * S, BF16, stack=s_ot)
        s1 = ExitStack()
        wupk = B.sbuf("wupk", 128, 2 * D, BF16, stack=s1)
        wupv = B.sbuf("wupv", 128, 2 * D, BF16, stack=s1)
        etile = [B.sbuf(f"etile{i}", 128, S, BF16, stack=s1) for i in range(4)]
        latT = B.sbuf("latT", 128, 2 * S, BF16, stack=s1)
        self.cast_load_w(wupk, self.win["l0a_wupk"], DL, D)
        self.cast_load_w(wupv, self.win["l0a_wupv"], DL, D)

        s_ef = ExitStack()
        irow = B.sbuf("irow", 1, S, F32, stack=s_ef)
        erow = B.sbuf("erow", 1, S, F32, stack=s_ef)
        etf = B.sbuf("etf", 128, S, F32, stack=s_ef)
        B.inc("gpsimd", self.gc, lambda E, irow=irow: E.iota(irow.ap(0, 1, 0, S), [[1, S]], base=0,
                                                    channel_multiplier=0, allow_small_or_imprecise_dtypes=True))
        B.wait("scalar", self.vc, 1)
        B.wait("scalar", self.gc, self.gc.n)
        for i, (sgn, use_b) in enumerate([(+1, True), (-1, True), (+1, False), (-1, False)]):
            if i > 0:
                B.wait("scalar", self.gb, self.gb.n)
            if use_b:
                B.inc("scalar", self.sc, lambda E, sgn=sgn, erow=erow: E.activation(
                    erow.ap(0, 1, 0, S), irow.ap(0, 1, 0, S), AF.Exp,
                    bias=self.bias1.ap(0, 1, 0, 1), scale=sgn * LAM))
            else:
                B.inc("scalar", self.sc, lambda E, sgn=sgn, erow=erow: E.activation(
                    erow.ap(0, 1, 0, S), irow.ap(0, 1, 0, S), AF.Exp, scale=sgn * LAM))
            B.wait("gpsimd", self.sc, self.sc.n)
            B.inc("gpsimd", self.gb, lambda E, i=i, erow=erow: E.dma_start(
                out=dram_ap(self.rowscr, 2 * S + i * S, [[1, S]]), in_=erow.ap(0, 1, 0, S)), 16)
        B.wait("sync", self.gb, self.gb.n)
        for i in range(4):
            self.row_bcast(2 * S + i * S, S, etf)
            B.wait("vector", self.ld, self.ld.n)
            B.inc("vector", self.vc, lambda E, i=i, etile=etile, etf=etf: E.tensor_copy(
                etile[i].ap(), etf.ap()))
            B.wait("sync", self.vc, self.vc.n)
        self.barrier()
        s_ef.close()

        s_w1 = ExitStack()
        wk = B.sbuf("wk", 128, NCH * D, BF16, stack=s_w1)
        wv = B.sbuf("wv", 128, NCH * D, BF16, stack=s_w1)
        wdkv = B.sbuf("wdkv", 128, NCH * DL, BF16, stack=s_w1)
        self.cast_load_w(wk, self.win["l0a_wk"], D, D)
        self.cast_load_w(wv, self.win["l0a_wv"], D, D)
        self.cast_load_w(wdkv, self.win["l0a_wdkv"], D, DL)
        s_kv = ExitStack()
        QTB = 256
        kQ = B.sbuf("kQ", 128, NCH * QTB, BF16, stack=s_kv)
        vQ = B.sbuf("vQ", 128, NCH * QTB, BF16, stack=s_kv)
        B.wait("tensor", self.gl, self.gl.n)
        for tq in range(S // QTB):
            tok0 = tq * QTB
            for dstbuf, wbuf in ((kQ, wk), (vQ, wv)):
                groups, evacs = [], []
                for m in range(NCH):
                    groups.append([(wbuf.ap(0, 128, c * D + m * 128, 128),
                                    self.xbf.ap(0, 128, c * S + tok0, QTB)) for c in range(NCH)])
                    evacs.append([lambda E, pt, dstbuf=dstbuf, m=m: E.activation(
                        dstbuf.ap(0, 128, m * QTB, QTB), pt, AF.Copy)])
                self.mm_pipe(groups, [self.ps_a.ap(0, 128, b * 512, QTB) for b in range(4)], evacs)
            groups, evacs = [], []
            for m in range(2):
                pairs = [(wdkv.ap(0, 128, c * DL + m * 128, 128),
                          kQ.ap(0, 128, c * QTB, QTB)) for c in range(NCH)]
                pairs += [(wdkv.ap(0, 128, c * DL + m * 128, 128),
                           vQ.ap(0, 128, c * QTB, QTB)) for c in range(NCH)]
                groups.append(pairs)
                evacs.append([lambda E, pt, m=m, tok0=tok0, latT=latT: E.activation(
                    latT.ap(0, 128, m * S + tok0, QTB), pt, AF.Copy)])
            self.mm_pipe(groups, [self.ps_a.ap(0, 128, b * 512, QTB) for b in range(4)], evacs)
        self.barrier()
        s_kv.close()
        s_w1.close()

        s_qT = ExitStack()
        qT = B.sbuf("qT", 128, 4 * S, BF16, stack=s_qT)
        wq = B.sbuf("wq", 128, 4 * D, BF16, stack=s_qT)
        zrow = B.sbuf("zrowr", 65, 1024, BF16, stack=s_qT)

        for c in range(NCH):
            B.inc("gpsimd", self.gl, lambda E, c=c: E.dma_start(
                out=dram_ap(self.xspill, c * 128 * S, [[S, 128], [1, S]]),
                in_=self.xf.ap(0, 128, c * S, S)), 16)
        B.wait("gpsimd", self.gl, self.gl.n)
        B.inc("gpsimd", self.gc, lambda E: E.nop())
        for e in ENGINES:
            B.wait(e, self.gc, self.gc.n)

        KB = 1024
        krecT = self.carve(0, 128, 4 * S, BF16)
        vaug = self.carve(16 * KB, 128, NT * KB, BF16)
        k1c = self.carve(48 * KB, 128, S, BF16)
        k2c = self.carve(52 * KB, 128, S, BF16)
        q1b = self.carve(56 * KB, 128, TB, BF16)
        q2b = self.carve(57 * KB, 128, TB, BF16)
        ptile = [self.carve(58 * KB, 128, KB, BF16), self.carve(60 * KB, 128, KB, BF16)]

        for t in range(NT):
            B.run("vector", lambda E, t=t, vaug=vaug: E.memset(
                vaug.apd(t * KB + 64, [[vaug.pitch, 128], [256, 4], [1, 64]]), 1.0))
            fn = lambda E, t=t, vaug=vaug: E.memset(
                vaug.apd(t * KB + 128, [[vaug.pitch, 128], [256, 4], [1, 64]]), 1.0)
            if t == NT - 1:
                B.inc("vector", self.vc, fn)
            else:
                B.run("vector", fn)

        for half in range(2):
            B.wait("gpsimd", self.mm, self.mm.n)
            for c in range(NCH):
                B.inc("gpsimd", self.gl, lambda E, c=c, half=half, wq=wq: E.dma_start(
                    out=wq.ap(0, 128, c * 512, 512),
                    in_=dram_ap(self.win["l0a_wq"], c * 128 * D + half * 512, [[D, 128], [1, 512]])), 16)
            B.wait("tensor", self.gl, self.gl.n)
            groups, evacs = [], []
            for m in range(4):
                for tb in range(NTB):
                    groups.append([(wq.ap(0, 128, c * 512 + m * 128, 128),
                                    self.xbf.ap(0, 128, c * S + tb * TB, TB)) for c in range(NCH)])
                    evacs.append([lambda E, pt, m=m, tb=tb, qT=qT: E.activation(
                        qT.ap(0, 128, m * S + tb * TB, TB), pt, AF.Copy)])
            self.mm_pipe(groups, self.pa4, evacs)
            groups, evacs = [], []
            for m in range(4):
                C = 4 * half + m
                for tb in range(NTB):
                    groups.append([(wupk.ap(0, 128, dl * D + C * 128, 128),
                                    latT.ap(0, 128, dl * S + tb * TB, TB)) for dl in range(2)])
                    evacs.append([lambda E, pt, m=m, tb=tb, krecT=krecT: E.activation(
                        krecT.ap(0, 128, m * S + tb * TB, TB), pt, AF.Copy)])
            self.mm_pipe(groups, self.pa4, evacs)

            B.wait("tensor", self.vc, self.vc.n)
            groups, evacs = [], []
            for t in range(NT):
                groups.append([(latT.ap(0, 128, dl * S + t * 128, 128),
                                wupv.ap(0, 128, dl * D + half * 512, 512)) for dl in range(2)])

                def ev_even(E, pt, t=t, vaug=vaug):
                    src = type(pt)(pt.tensor, pt.offset, [[2048, 128], [128, 4], [1, 64]])
                    dst = vaug.apd(t * KB, [[vaug.pitch, 128], [256, 4], [1, 64]])
                    return E.tensor_copy(dst, src)

                def ev_odd(E, pt, t=t, vaug=vaug):
                    src = type(pt)(pt.tensor, pt.offset + 64, [[2048, 128], [128, 4], [1, 64]])
                    dst = vaug.apd(t * KB + 192, [[vaug.pitch, 128], [256, 4], [1, 64]])
                    return E.tensor_copy(dst, src)
                evacs.append([ev_even, ev_odd])
            self.mm_pipe(groups, self.pa4, evacs, evac_engine="vector")

            for mc in range(4):
                C = 4 * half + mc
                B.wait("vector", self.ev, self.ev.n)
                B.wait("vector", self.pv, self.pv.n)
                B.run("vector", lambda E, mc=mc, k1c=k1c, krecT=krecT, etile=etile: E.tensor_tensor(
                    k1c.ap(), krecT.ap(0, 128, mc * S, S), etile[0].ap(), OP.mult))
                B.inc("vector", self.vc, lambda E, mc=mc, k2c=k2c, krecT=krecT, etile=etile: E.tensor_tensor(
                    k2c.ap(), krecT.ap(0, 128, mc * S, S), etile[1].ap(), OP.mult))
                for h2 in range(2):
                    hg = 2 * C + h2
                    pr = 64 * h2
                    hh = hg & 7
                    for qb in range(NTB):
                        B.wait("vector", self.pv, self.pv.n)
                        B.run("vector", lambda E, pr=pr, mc=mc, qb=qb, q1b=q1b, qT=qT, etile=etile: E.tensor_tensor(
                            q1b.ap(pr, 64, 0, TB), qT.ap(pr, 64, mc * S + qb * TB, TB),
                            etile[3].ap(pr, 64, qb * TB, TB), OP.mult))
                        B.inc("vector", self.vc, lambda E, pr=pr, mc=mc, qb=qb, q2b=q2b, qT=qT, etile=etile: E.tensor_tensor(
                            q2b.ap(pr, 64, 0, TB), qT.ap(pr, 64, mc * S + qb * TB, TB),
                            etile[2].ap(pr, 64, qb * TB, TB), OP.mult))
                        B.wait("tensor", self.vc, self.vc.n)
                        xs_base = self.xs.n
                        pv_base = self.pv.n
                        for g in range(8):
                            hf = g % 2
                            if g >= 2:
                                B.wait("tensor", self.xs, xs_base + (g - 1))
                            diag_slots = []
                            mmq = []
                            for kk in range(2):
                                kc = 2 * g + kk
                                kcr = kc - 4 * qb
                                pslice = self.ps_a.ap(0, 128, hf * 1024 + kk * 512, 512)
                                if kcr < 0:
                                    mmq.append((None, lambda E, pr=pr, kc=kc, pslice=pslice, k1c=k1c, q1b=q1b: E.matmul(
                                        pslice, k1c.ap(pr, 64, kc * 128, 128), q1b.ap(pr, 64, 0, TB),
                                        start=True, stop=True)))
                                elif kcr > 3:
                                    mmq.append((None, lambda E, pr=pr, kc=kc, pslice=pslice, k2c=k2c, q2b=q2b: E.matmul(
                                        pslice, k2c.ap(pr, 64, kc * 128, 128), q2b.ap(pr, 64, 0, TB),
                                        start=True, stop=True)))
                                else:
                                    for j in range(4):
                                        psub = self.ps_a.ap(0, 128, hf * 1024 + kk * 512 + j * 128, 128)
                                        if kcr < j:
                                            mmq.append((None, lambda E, pr=pr, kc=kc, j=j, psub=psub, k1c=k1c, q1b=q1b: E.matmul(
                                                psub, k1c.ap(pr, 64, kc * 128, 128),
                                                q1b.ap(pr, 64, j * 128, 128), start=True, stop=True)))
                                        elif kcr > j:
                                            mmq.append((None, lambda E, pr=pr, kc=kc, j=j, psub=psub, k2c=k2c, q2b=q2b: E.matmul(
                                                psub, k2c.ap(pr, 64, kc * 128, 128),
                                                q2b.ap(pr, 64, j * 128, 128), start=True, stop=True)))
                                        else:
                                            nds = len(diag_slots)
                                            mmq.append(("dwait", None))
                                            mmq.append((None, lambda E, pr=pr, kc=kc, j=j, psub=psub, k1c=k1c, q1b=q1b: E.matmul(
                                                psub, k1c.ap(pr, 64, kc * 128, 128),
                                                q1b.ap(pr, 64, j * 128, 128), start=True, stop=True)))
                                            mmq.append((None, lambda E, pr=pr, kc=kc, j=j, nds=nds, k2c=k2c, q2b=q2b: E.matmul(
                                                self.ps_scr.ap(0, 128, nds * 128, 128),
                                                k2c.ap(pr, 64, kc * 128, 128),
                                                q2b.ap(pr, 64, j * 128, 128), start=True, stop=True)))
                                            diag_slots.append(psub)
                            last_i = max(i for i, (tag, f) in enumerate(mmq) if f is not None)
                            for i, (tag, f) in enumerate(mmq):
                                if tag == "dwait":
                                    if self.vc.n > 0:
                                        B.wait("tensor", self.vc, self.vc.n)
                                elif i == last_i:
                                    B.inc("tensor", self.mm, f)
                                else:
                                    B.run("tensor", f)
                            if diag_slots:
                                B.wait("vector", self.mm, self.mm.n)
                                for nds, psub in enumerate(diag_slots):
                                    B.inc("vector", self.vc, lambda E, nds=nds, psub=psub: E.copy_predicated(
                                        psub, self.maskU.ap(), self.ps_scr.ap(0, 128, nds * 128, 128)))
                                B.wait("scalar", self.vc, self.vc.n)
                            else:
                                B.wait("scalar", self.mm, self.mm.n)
                            if g >= 2:
                                B.wait("scalar", self.pv, pv_base + (g - 1))
                            B.inc("scalar", self.xs, lambda E, hf=hf, ptile=ptile: E.activation(
                                ptile[hf].ap(), self.ps_a.ap(0, 128, hf * 1024, 1024), AF.Exp))
                            B.wait("tensor", self.xs, self.xs.n)
                            for kk in range(2):
                                kc = 2 * g + kk
                                pso = self.ps_o1 if h2 == 0 else self.ps_o2
                                if h2 == 0:
                                    lh = vaug.ap(0, 128, kc * KB + hh * 128, 65)
                                else:
                                    lh = vaug.ap(0, 128, kc * KB + hh * 128, 128)
                                nout = 65 if h2 == 0 else 128
                                if kk == 1:
                                    B.inc("tensor", self.pv, lambda E, lh=lh, hf=hf, kk=kk, pso=pso, kc=kc, nout=nout, ptile=ptile: E.matmul(
                                        pso.ap(0, nout, 0, 512), lh, ptile[hf].ap(0, 128, kk * 512, 512),
                                        start=(kc == 0), stop=(kc == 15)))
                                else:
                                    B.run("tensor", lambda E, lh=lh, hf=hf, kk=kk, pso=pso, kc=kc, nout=nout, ptile=ptile: E.matmul(
                                        pso.ap(0, nout, 0, 512), lh, ptile[hf].ap(0, 128, kk * 512, 512),
                                        start=(kc == 0), stop=(kc == 15)))
                        pso = self.ps_o1 if h2 == 0 else self.ps_o2
                        zslot = qb % 2
                        zpar = self.sta if zslot == 0 else self.stb
                        B.wait("vector", self.pv, self.pv.n)
                        B.wait("vector", zpar, zpar.n)
                        zr = 64 if h2 == 0 else 0
                        B.inc("vector", self.vc, lambda E, pso=pso, zslot=zslot, zr=zr, zrow=zrow: E.tensor_copy(
                            zrow.ap(zr, 1, zslot * 512, 512), pso.ap(zr, 1, 0, 512)))
                        B.wait("scalar", self.pv, self.pv.n)
                        if h2 == 0:
                            B.inc("scalar", self.sc, lambda E, pso=pso, C=C, qb=qb, OTu=OTu: E.activation(
                                OTu.ap(0, 64, C * S + qb * TB, TB), pso.ap(0, 64, 0, 512), AF.Copy))
                            zoff = hg // 2 * S + qb * 512
                        else:
                            B.inc("scalar", self.sc, lambda E, pso=pso, C=C, qb=qb, OTu=OTu: E.activation(
                                OTu.ap(64, 64, C * S + qb * TB, TB), pso.ap(64, 64, 0, 512), AF.Copy))
                            zoff = H * S + hg // 2 * S + qb * 512
                        B.wait("gpsimd", self.vc, self.vc.n)
                        B.inc("gpsimd", zpar, lambda E, zr=zr, zslot=zslot, zoff=zoff, zrow=zrow: E.dma_start(
                            out=dram_ap(self.zscr, zoff, [[1, 512]]),
                            in_=zrow.ap(zr, 1, zslot * 512, 512)), 16)
                        B.wait("tensor", self.sc, self.sc.n)
                        B.wait("tensor", self.vc, self.vc.n)
        self.barrier()
        s_qT.close()
        s1.close()

        s_n = ExitStack()
        zrep = B.sbuf("zrep", 128, S, BF16, stack=s_n)
        rzt = B.sbuf("rzt", 128, S, F32, stack=s_n)
        wo = B.sbuf("wo", 128, NCH * D, BF16, stack=s_n)
        B.wait("sync", self.sta, self.sta.n)
        B.wait("sync", self.stb, self.stb.n)
        for C in range(NCH):
            B.inc("sync", self.ld, lambda E, C=C, zrep=zrep: E.dma_start(
                out=zrep.ap(), in_=dram_ap(self.zscr, C * S, [[H * S, 2], [0, 64], [1, S]])), 16)
            B.wait("vector", self.ld, self.ld.n)
            B.inc("vector", self.vc, lambda E, zrep=zrep, rzt=rzt: E.reciprocal(rzt.ap(), zrep.ap()))
            B.wait("vector", self.vc, self.vc.n)
            B.inc("vector", self.vc, lambda E, C=C, OTu=OTu, rzt=rzt: E.tensor_tensor(
                OTu.ap(0, 128, C * S, S), OTu.ap(0, 128, C * S, S), rzt.ap(), OP.mult))
            B.wait("sync", self.vc, self.vc.n)

        B.wait("gpsimd", self.vc, self.vc.n)
        for c in range(NCH):
            B.inc("gpsimd", self.gl, lambda E, c=c: E.dma_start(
                out=self.xf.ap(0, 128, c * S, S),
                in_=dram_ap(self.xspill, c * 128 * S, [[S, 128], [1, S]])), 16)
        self.cast_load_w(wo, self.win["l0a_wo"], D, D)
        B.wait("tensor", self.gl, self.gl.n)
        B.wait("tensor", self.vc, self.vc.n)
        groups, evacs = [], []
        for m in range(NCH):
            for tb in range(NTB):
                groups.append([(wo.ap(0, 128, c * D + m * 128, 128),
                                OTu.ap(0, 128, c * S + tb * TB, TB)) for c in range(NCH)])
                evacs.append([lambda E, pt, m=m, tb=tb: E.tensor_tensor(
                    self.xf.ap(0, 128, m * S + tb * TB, TB), pt,
                    self.xf.ap(0, 128, m * S + tb * TB, TB), OP.add)])
        self.mm_pipe(groups, self.pa4, evacs, evac_engine="vector")
        self.barrier()
        s_n.close()
        s_ot.close()

        self.ln_phase()


    def l0_ffn(self):
        B = self.B
        FD = 4 * D
        NFC = 32
        QT = 512
        for c in range(NCH):
            B.inc("gpsimd", self.gl, lambda E, c=c: E.dma_start(
                out=dram_ap(self.xspill, c * 128 * S, [[S, 128], [1, S]]),
                in_=self.xf.ap(0, 128, c * S, S)), 16)
        B.wait("gpsimd", self.gl, self.gl.n)
        B.inc("gpsimd", self.gc, lambda E: E.nop())
        for e in ENGINES:
            B.wait(e, self.gc, self.gc.n)
        self.s_xf.close()

        s_f = ExitStack()
        bsets = [B.sbuf(f"bset{i}", 128, NFC * 128, BF16, stack=s_f) for i in range(4)]
        pmats = B.sbuf("pmats", 128, 7 * 128, BF16, stack=s_f)
        eyebf = B.sbuf("eyebf", 128, 128, BF16, stack=s_f)
        vab = B.sbuf("vab", 128, 64, BF16, stack=s_f)
        wsl = B.sbuf("wsl", 32, 512, F32, stack=s_f)
        bexpL1 = B.sbuf("bexpL1", 128, 32, F32, stack=s_f)
        upbL2 = B.sbuf("upbL2", 128, 32, F32, stack=s_f)
        b2t = B.sbuf("b2t", 128, 16, F32, stack=s_f)
        pcol = B.sbuf("pcol", 128, 4, F32, stack=s_f)
        xstage = B.sbuf("xstage", 128, 2 * QT, F32, stack=s_f)
        rstage = B.sbuf("rstage", 128, 2 * QT, F32, stack=s_f)
        tstage = B.sbuf("tstage", 128, QT, F32, stack=s_f)
        s_fb = ExitStack()
        tsc = B.sbuf("tsc", 128, NFC * 128, BF16, stack=s_fb)
        da_all = B.sbuf("da_all", 128, NFC * 128, BF16, stack=s_fb)
        db_all = B.sbuf("db_all", 128, NFC * 128, BF16, stack=s_fb)
        ws2 = B.sbuf("ws2", 32, 8068, F32, stack=s_fb)

        STRIDES = [1, 2, 4, 8, 16, 32, 64]
        PIDX = {st: i for i, st in enumerate(STRIDES)}

        B.inc("sync", self.ld, lambda E: E.dma_start(
            out=wsl.ap(0, 32, 0, 128), in_=dram_ap(self.win["l0f_bexp"], 0, [[128, 32], [1, 128]])), 16)
        B.inc("sync", self.ld, lambda E: E.dma_start(
            out=upbL2.ap(0, 128, 0, 32), in_=dram_ap(self.win["l0f_upb"], 0, [[32, 128], [1, 32]])), 16)
        B.inc("sync", self.ld, lambda E: E.dma_start(
            out=wsl.ap(0, 8, 128, 128), in_=dram_ap(self.win["l0f_bcon"], 0, [[128, 8], [1, 128]])), 16)
        B.inc("sync", self.ld, lambda E: E.dma_start(
            out=wsl.ap(0, 8, 256, 128), in_=dram_ap(self.win["l0f_dnb"], 0, [[128, 8], [1, 128]])), 16)
        B.wait("tensor", self.ld, self.ld.n)
        B.wait("tensor", self.ev, self.ev.n)
        B.run("tensor", lambda E: E.transpose(
            self.ps_scr.ap(0, 128, 0, 32), wsl.ap(0, 32, 0, 128), self.eye32.ap(0, 32, 0, 32)))
        B.run("tensor", lambda E: E.transpose(
            self.ps_scr.ap(0, 128, 32, 8), wsl.ap(0, 8, 128, 128), self.eye32.ap(0, 8, 0, 8)))
        B.inc("tensor", self.mm, lambda E: E.transpose(
            self.ps_scr.ap(0, 128, 40, 8), wsl.ap(0, 8, 256, 128), self.eye32.ap(0, 8, 0, 8)))
        B.wait("vector", self.mm, self.mm.n)
        B.run("vector", lambda E: E.tensor_copy(bexpL1.ap(), self.ps_scr.ap(0, 128, 0, 32)))
        B.inc("vector", self.vc, lambda E: E.tensor_copy(
            b2t.ap(0, 128, 0, 8), self.ps_scr.ap(0, 128, 32, 8)))
        B.wait("vector", self.vc, self.vc.n)
        B.inc("vector", self.vc, lambda E: E.tensor_tensor(
            b2t.ap(0, 128, 0, 8), b2t.ap(0, 128, 0, 8), self.ps_scr.ap(0, 128, 40, 8), OP.add))

        B.run("vector", lambda E: E.tensor_scalar(eyebf.ap(), self.fidx.ap(), self.pidx.ap(), None, OP.is_equal))
        B.wait("sync", self.ld, self.ld.n)
        B.inc("sync", self.ld, lambda E: E.dma_start(
            out=pmats.ap(), in_=dram_ap(self.win["l0f_pmc"], 0, [[7 * 128, 128], [1, 7 * 128]])), 16)
        B.wait("vector", self.ld, self.ld.n)

        for set_i, (wdram, stages, grid) in enumerate([
                (self.win["l0f_upw"], list(range(7)), "L1"),
                (self.win["l0f_upw"], list(range(7, 12)), "L2"),
                (self.win["l0f_dnw"], list(range(7)), "L1"),
                (self.win["l0f_dnw"], list(range(7, 12)), "L2")]):
            dest = bsets[set_i]
            n_st = len(stages)
            for idx, s in enumerate(reversed(stages)):
                B.wait("sync", self.vc, self.vc.n)
                B.wait("sync", self.ld, self.ld.n)
                if grid == "L1":
                    B.inc("sync", self.ld, lambda E, s=s, wdram=wdram: E.dma_start(
                        out=wsl.ap(0, 32, 0, 256),
                        in_=dram_ap(wdram, s * 8192, [[256, 32], [1, 256]])), 16)
                    stt = 2 ** s
                    ap_a = ([4 * stt, 2 ** (6 - s)], [4, stt])
                else:
                    B.inc("sync", self.ld, lambda E, s=s, wdram=wdram, ws2=ws2: E.dma_start(
                        out=ws2.ap(0, 32, 0, 8068),
                        in_=dram_ap(wdram, s * 8192, [[4, 32], [1, 8068]])), 16)
                    stt = 2 ** s
                    ap_a = ([4 * stt, 2 ** (11 - s)], [128, 2 ** (s - 5)])
                B.wait("vector", self.ld, self.ld.n)
                srcbuf = wsl if grid == "L1" else ws2
                self._bf_regather(srcbuf, ap_a)
                B.wait("tensor", self.vc, self.vc.n)
                B.wait("tensor", self.ev, self.ev.n)
                B.run("tensor", lambda E: E.transpose(
                    self.ps_scr.ap(0, 128, 64, 32), self.ta.ap(0, 32, 0, 128), self.eye32.ap(0, 32, 0, 32)))
                B.inc("tensor", self.mm, lambda E: E.transpose(
                    self.ps_scr.ap(0, 128, 96, 32), self.ta.ap(0, 32, 128, 128), self.eye32.ap(0, 32, 0, 32)))
                B.wait("vector", self.mm, self.mm.n)
                B.inc("vector", self.vc, lambda E, vab=vab: E.tensor_copy(
                    vab.ap(0, 128, 0, 64), self.ps_scr.ap(0, 128, 64, 64)))
                B.wait("vector", self.vc, self.vc.n)
                eye_rep = eyebf.apd(0, [[128, 128], [0, 32], [1, 128]])
                va_rep = vab.apd(0, [[64, 128], [1, 32], [0, 128]])
                pm = pmats.apd(PIDX[stt if grid == "L1" else stt // 32] * 128,
                               [[7 * 128, 128], [0, 32], [1, 128]])
                vb_rep = vab.apd(32, [[64, 128], [1, 32], [0, 128]])
                da_ap = da_all.apd(0, [[NFC * 128, 128], [128, 32], [1, 128]])
                db_ap = db_all.apd(0, [[NFC * 128, 128], [128, 32], [1, 128]])
                B.run("vector", lambda E, eye_rep=eye_rep, va_rep=va_rep, da_ap=da_ap: E.tensor_tensor(
                    da_ap, eye_rep, va_rep, OP.mult))
                B.inc("vector", self.vc, lambda E, pm=pm, vb_rep=vb_rep, db_ap=db_ap: E.tensor_tensor(
                    db_ap, pm, vb_rep, OP.mult))
                B.wait("tensor", self.vc, self.vc.n)
                B.wait("tensor", self.ev, self.ev.n)
                out_is_dest = ((n_st - 1 - idx) % 2 == 0)
                outbuf = dest if out_is_dest else tsc
                if idx == 0:
                    rhsbuf = None
                else:
                    rhsbuf = tsc if out_is_dest else dest
                stage_ev = self.ev.n
                for c in range(NFC):
                    slot = c % 16
                    if c >= 16 and c % 4 == 0:
                        B.wait("tensor", self.ev, stage_ev + (c // 4 - 4) + 1)
                    if rhsbuf is None:
                        rhs = eyebf.ap()
                    else:
                        rhs = rhsbuf.ap(0, 128, c * 128, 128)
                    B.run("tensor", lambda E, c=c, slot=slot, da_all=da_all, rhs=rhs: E.matmul(
                        self.ps_a.ap(0, 128, slot * 128, 128),
                        da_all.ap(0, 128, c * 128, 128), rhs, start=True, stop=False))
                    fn = lambda E, c=c, slot=slot, db_all=db_all, rhs=rhs: E.matmul(
                        self.ps_a.ap(0, 128, slot * 128, 128),
                        db_all.ap(0, 128, c * 128, 128), rhs, start=False, stop=True)
                    if c % 4 == 3:
                        B.inc("tensor", self.mm, fn)
                        B.wait("vector", self.mm, self.mm.n)
                        grp = c // 4
                        B.inc("vector", self.ev, lambda E, grp=grp, outbuf=outbuf: E.tensor_copy(
                            outbuf.ap(0, 128, grp * 512, 512),
                            self.ps_a.ap(0, 128, (grp % 4) * 512, 512)))
                    else:
                        B.run("tensor", fn)
        B.wait("tensor", self.ev, self.ev.n)
        self.barrier()
        s_fb.close()

        s_fh = ExitStack()
        hA = B.sbuf("hA", 128, NFC * QT, BF16, stack=s_fh)
        hB = B.sbuf("hB", 128, NFC * QT, BF16, stack=s_fh)
        webuf = [B.sbuf(f"webuf{i}", 128, 8 * 512, BF16, stack=s_fh) for i in range(2)]
        wcbuf = [B.sbuf(f"wcbuf{i}", 128, NFC * 128, BF16, stack=s_fh) for i in range(2)]

        for tq in range(4):
            tok0 = tq * QT
            for g8 in range(8):
                buf = webuf[g8 % 2]
                B.wait("gpsimd", self.mm, self.mm.n)
                spar = self.gwa if g8 % 2 == 0 else self.gwb
                B.wait("gpsimd", spar, spar.n)
                B.inc("gpsimd", spar, lambda E, g8=g8, buf=buf: E.dma_start(
                    out=buf.ap(),
                    in_=dram_ap(self.win["l0f_wexp"], g8 * 512,
                                [[4096, 128], [128 * 4096, 8], [1, 512]])), 16)
                B.wait("tensor", spar, spar.n)
                groups, evacs = [], []
                for i4 in range(4):
                    fc = g8 * 4 + i4
                    groups.append([(buf.ap(0, 128, c * 512 + i4 * 128, 128),
                                    self.xbf.ap(0, 128, c * S + tok0, QT)) for c in range(NCH)])
                    evacs.append([lambda E, pt, fc=fc, hA=hA: E.activation(
                        hA.ap(0, 128, fc * QT, QT), pt, AF.Identity, bias=bexpL1.ap(0, 128, fc, 1))])
                self.mm_pipe(groups, self.pa4, evacs)
            groups, evacs = [], []
            for fc in range(NFC):
                groups.append([(bsets[0].ap(0, 128, fc * 128, 128), hA.ap(0, 128, fc * QT, QT))])
                evacs.append([lambda E, pt, fc=fc, hA=hA: E.tensor_copy(
                    hA.ap(0, 128, fc * QT, QT), pt)])
            self.mm_pipe(groups, self.pa4, evacs, evac_engine="vector")
            self._grid_change(hA, hB, QT, fwd=True)
            groups, evacs = [], []
            for fc in range(NFC):
                groups.append([(bsets[1].ap(0, 128, fc * 128, 128), hB.ap(0, 128, fc * QT, QT))])
                evacs.append([lambda E, pt, fc=fc, hB=hB: E.activation(
                    hB.ap(0, 128, fc * QT, QT), pt, AF.Gelu, bias=upbL2.ap(0, 128, fc, 1))])
            self.mm_pipe(groups, self.pa4, evacs)
            self._grid_change(hB, hA, QT, fwd=False)
            groups, evacs = [], []
            for fc in range(NFC):
                groups.append([(bsets[2].ap(0, 128, fc * 128, 128), hA.ap(0, 128, fc * QT, QT))])
                evacs.append([lambda E, pt, fc=fc, hA=hA: E.tensor_copy(
                    hA.ap(0, 128, fc * QT, QT), pt)])
            self.mm_pipe(groups, self.pa4, evacs, evac_engine="vector")
            self._grid_change(hA, hB, QT, fwd=True)
            groups, evacs = [], []
            for fc in range(NFC):
                groups.append([(bsets[3].ap(0, 128, fc * 128, 128), hB.ap(0, 128, fc * QT, QT))])
                evacs.append([lambda E, pt, fc=fc, hB=hB: E.tensor_copy(
                    hB.ap(0, 128, fc * QT, QT), pt)])
            self.mm_pipe(groups, self.pa4, evacs, evac_engine="vector")
            groups, evacs = [], []
            for m in range(NCH):
                buf = wcbuf[m % 2]
                groups.append([(buf.ap(0, 128, fc * 128, 128),
                                hB.ap(0, 128, fc * QT, QT)) for fc in range(NFC)])

                def ev_con(E, pt, m=m, tstage=tstage):
                    return E.tensor_scalar(tstage.ap(0, 128, 0, QT), pt,
                                           b2t.ap(0, 128, m, 1), None, OP.add)

                def ev_con2(E, pt, m=m, xstage=xstage, rstage=rstage, tstage=tstage):
                    return E.tensor_tensor(rstage.ap(0, 128, (m % 2) * QT, QT),
                                           tstage.ap(0, 128, 0, QT),
                                           xstage.ap(0, 128, (m % 2) * QT, QT), OP.add)
                evacs.append([ev_con, ev_con2])
            self._wcon_pipe(groups, evacs, tq, xstage, rstage, wcbuf)
        B.wait("gpsimd", self.sta, self.sta.n)
        B.wait("gpsimd", self.stb, self.stb.n)
        B.wait("gpsimd", self.vc, self.vc.n)
        self.barrier()
        s_fh.close()
        s_f.close()
        self.alloc_xf()
        for c in range(NCH):
            B.inc("gpsimd", self.gl, lambda E, c=c: E.dma_start(
                out=self.xf.ap(0, 128, c * S, S),
                in_=dram_ap(self.xspill2, c * 128 * S, [[S, 128], [1, S]])), 16)
        B.wait("vector", self.gl, self.gl.n)
        B.wait("scalar", self.gl, self.gl.n)
        B.wait("tensor", self.gl, self.gl.n)
        self.ln_phase()

    def l1_attn(self):
        B = self.B
        AFE = AF.Exp
        for c in range(NCH):
            B.inc("gpsimd", self.gl, lambda E, c=c: E.dma_start(
                out=dram_ap(self.xspill, c * 128 * S, [[S, 128], [1, S]]),
                in_=self.xf.ap(0, 128, c * S, S)), 16)
        B.wait("gpsimd", self.gl, self.gl.n)
        B.inc("gpsimd", self.gc, lambda E: E.nop())
        for e in ENGINES:
            B.wait(e, self.gc, self.gc.n)
        self.s_xf.close()

        s_a = ExitStack()
        ogbf = B.sbuf("ogbf", 128, NCH * S, BF16, stack=s_a)
        qpair = B.sbuf("qpair", 128, S, BF16, stack=s_a)
        kpair = B.sbuf("kpair", 128, S, BF16, stack=s_a)
        gatep = B.sbuf("gatep", 128, S, BF16, stack=s_a)
        vtmp = B.sbuf("vtmp", 128, S, F32, stack=s_a)
        vtokp = B.sbuf("vtokp", 128, 16 * 130, BF16, stack=s_a)
        gathp = B.sbuf("gathp", 128, S, BF16, stack=s_a)
        wbuf = [B.sbuf(f"l1wb{i}", 128, 1024, BF16, stack=s_a) for i in range(2)]
        mkraw = B.sbuf("mkraw", 128, L1_BAND, BF16, stack=s_a)
        mk2 = B.sbuf("mk2", 128, L1_BAND, BF16, stack=s_a)
        ptil = [B.sbuf(f"ptil{i}", 128, S, BF16, stack=s_a) for i in range(2)]
        zsb = B.sbuf("zsb", 1, 2 * S, F32, stack=s_a)
        zrep = B.sbuf("zrep2", 128, S, BF16, stack=s_a)
        rzf = B.sbuf("rzf", 128, S, F32, stack=s_a)
        og1 = B.sbuf("og1", 128, S, F32, stack=s_a)
        c2l = B.sbuf("c2l", 1, 2 * S, BF16, stack=s_a)
        bqt = B.sbuf("bqt", 128, 32, F32, stack=s_a)
        bot = B.sbuf("bot", 128, 8, F32, stack=s_a)
        revb = B.sbuf("revb", 128, 128, BF16, stack=s_a)
        p127 = B.sbuf("p127", 128, 1, F32, stack=s_a)
        s_prep = ExitStack()
        pbx = B.sbuf("pbx", 44, 16, F32, stack=s_prep)
        epbb = B.sbuf("epbb", 44, 16, BF16, stack=s_prep)
        gselb = B.sbuf("gselb", 44, 23, BF16, stack=s_prep)
        wsb = B.sbuf("wsb", 23, 16, F32, stack=s_prep)
        wsbb = B.sbuf("wsbb", 23, 16, BF16, stack=s_prep)
        stcb = B.sbuf("stcb", 23, S, BF16, stack=s_prep)
        mz = B.sbuf("mz", 16, L1_WROW, BF16, stack=s_prep)

        B.wait("sync", self.ld, self.ld.n)
        B.inc("sync", self.ld, lambda E: E.dma_start(
            out=pbx.ap(), in_=dram_ap(self.win["l1a_pb"], 0, [[16, 44], [1, 16]])), 16)
        B.wait("sync", self.ld, self.ld.n)
        B.inc("sync", self.ld, lambda E: E.dma_start(
            out=gselb.ap(), in_=dram_ap(self.win["l1a_gsel"], 0, [[23, 44], [1, 23]])), 16)
        B.wait("sync", self.ld, self.ld.n)
        B.inc("sync", self.ld, lambda E: E.dma_start(
            out=bqt.ap(0, 128, 0, 24),
            in_=dram_ap(self.win["l1a_bqkv"], 0, [[1, 128], [128, 24]]), allow_slow_non_contiguous=True), 16)
        B.wait("sync", self.ld, self.ld.n)
        B.inc("sync", self.ld, lambda E: E.dma_start(
            out=bqt.ap(0, 128, 24, 8),
            in_=dram_ap(self.win["l1a_bgate"], 0, [[1, 128], [128, 8]]), allow_slow_non_contiguous=True), 16)
        B.wait("sync", self.ld, self.ld.n)
        B.inc("sync", self.ld, lambda E: E.dma_start(
            out=bot.ap(), in_=dram_ap(self.win["l1a_bout"], 0, [[1, 128], [128, 8]]), allow_slow_non_contiguous=True), 16)

        B.inc("vector", self.vc, lambda E: E.tensor_scalar(
            p127.ap(), self.pidx.ap(), -1.0, 127.0, OP.mult, OP.add))
        B.wait("vector", self.vc, self.vc.n)
        B.inc("vector", self.vc, lambda E: E.tensor_scalar(
            revb.ap(), self.fidx.ap(), p127.ap(), None, OP.is_equal))
        B.run("vector", lambda E: E.memset(mz.ap(), 0.0))
        B.inc("vector", self.vc, lambda E: E.memset(
            vtokp.apd(64, [[16 * 130, 128], [65, 32], [1, 1]]), 1.0))

        B.wait("scalar", self.ld, self.ld.n)
        B.inc("scalar", self.sc, lambda E: E.activation(epbb.ap(), pbx.ap(), AF.Exp))
        B.wait("tensor", self.sc, self.sc.n)
        B.wait("tensor", self.ld, self.ld.n)
        B.wait("tensor", self.vc, self.vc.n)
        B.inc("tensor", self.mm, lambda E: E.matmul(
            self.ps_scr.ap(0, 23, 0, 16), gselb.ap(), epbb.ap(), start=True, stop=True))
        B.wait("vector", self.mm, self.mm.n)
        B.inc("vector", self.vc, lambda E: E.tensor_copy(wsb.ap(), self.ps_scr.ap(0, 23, 0, 16)))
        B.wait("vector", self.vc, self.vc.n)
        B.inc("vector", self.vc, lambda E: E.tensor_copy(wsbb.ap(), wsb.ap()))
        B.wait("scalar", self.vc, self.vc.n)
        B.wait("sync", self.ld, self.ld.n)
        B.inc("sync", self.ld, lambda E: E.dma_start(
            out=stcb.ap(), in_=dram_ap(self.win["l1a_stc"], 0, [[S, 23], [1, S]])), 16)
        B.wait("tensor", self.ld, self.ld.n)
        B.wait("tensor", self.vc, self.vc.n)
        for bb in range(4):
            B.inc("tensor", self.mm, lambda E, bb=bb: E.matmul(
                self.ps_a.ap(0, 16, bb * 512, 512), wsbb.ap(),
                stcb.ap(0, 23, bb * 512, 512), start=True, stop=True))
        B.wait("vector", self.mm, self.mm.n)
        B.inc("vector", self.vc, lambda E: E.tensor_copy(
            stcb.ap(0, 16, 0, S), self.ps_a.ap(0, 16, 0, S)))
        B.wait("gpsimd", self.vc, self.vc.n)
        B.inc("gpsimd", self.gb, lambda E: E.dma_start(
            out=dram_ap(self.c2scr, 0, [[S, 16], [1, S]]),
            in_=stcb.ap(0, 16, 0, S)), 16)
        B.wait("gpsimd", self.gb, self.gb.n)
        B.wait("gpsimd", self.vc, self.vc.n)
        B.inc("gpsimd", self.gb, lambda E: E.dma_start(
            out=dram_ap(self.wrowscr, 0, [[L1_WROW, 16], [1, L1_WROW]]), in_=mz.ap()), 16)
        B.wait("gpsimd", self.gb, self.gb.n)
        B.inc("gpsimd", self.gb, lambda E: E.dma_start(
            out=dram_ap(self.wscr23, 0, [[16, 23], [1, 16]]), in_=wsb.ap()), 16)
        B.wait("gpsimd", self.gb, self.gb.n)
        for oi, o in enumerate(L1_OFFS):
            B.inc("gpsimd", self.gb, lambda E, oi=oi, o=o: E.dma_start(
                out=dram_ap(self.wrowscr, 127 + o, [[L1_WROW, 16], [1, 1]]),
                in_=dram_ap(self.wscr23, oi * 16, [[1, 16]]),
                allow_slow_non_contiguous=True), 16)
            B.wait("gpsimd", self.gb, self.gb.n)
        B.inc("gpsimd", self.gc, lambda E: E.nop())
        for e in ENGINES:
            B.wait(e, self.gc, self.gc.n)
        s_prep.close()

        po = [self.ps_o1, self.ps_o2]
        po_ev = [0, 0]
        po_i = [0]

        def piece_mm(mms, evac):
            i = po_i[0] & 1
            pso = po[i]
            B.wait("tensor", self.sc, po_ev[i])
            mms(pso)
            B.wait("scalar", self.mm, self.mm.n)
            evac(pso)
            po_ev[i] = self.sc.n
            po_i[0] += 1

        psa_sc = [0]
        psa_vc = [0]
        gath_vc = [0]
        mk_vc = [0]
        vtok_mm = [0]
        wb_mm = [0, 0]
        par_ld = [self.lda, self.ldb]
        par_gw = [self.gwa, self.gwb]

        WQKV = self.win["l1a_wqkv"]
        WGATE = self.win["l1a_wgate"]

        zvc = [0]
        for pr in range(NCH):
            hA, hB2 = 2 * pr, 2 * pr + 1
            B.wait("sync", self.vc, zvc[0])
            B.wait("sync", self.ld, self.ld.n)
            B.inc("sync", self.ld, lambda E, pr=pr: E.dma_start(
                out=c2l.ap(0, 1, 0, 2 * S),
                in_=dram_ap(self.c2scr, 2 * pr * S, [[1, 2 * S]])), 16)
            projs = [
                (WQKV, 3 * D, pr * 128, pr, AF.Identity, qpair, BF16),
                (WQKV, 3 * D, D + pr * 128, 8 + pr, AF.Identity, kpair, BF16),
                (WQKV, 3 * D, 2 * D + pr * 128, 16 + pr, AF.Identity, vtmp, F32),
                (WGATE, D, pr * 128, 24 + pr, AF.Sigmoid, gatep, BF16),
            ]
            for pi, (wd, wpitch, c0, bcol, func, dst, ddt) in enumerate(projs):
                wi = pi & 1
                par = par_gw[wi]
                B.wait("gpsimd", self.mm, wb_mm[wi])
                B.wait("gpsimd", par, par.n)
                B.inc("gpsimd", par, lambda E, wd=wd, wpitch=wpitch, c0=c0, wi=wi: E.dma_start(
                    out=wbuf[wi].ap(),
                    in_=dram_ap(wd, c0, [[wpitch, 128], [128 * wpitch, 8], [1, 128]])), 16)
                B.wait("tensor", par, par.n)
                if pi == 2:
                    B.wait("scalar", self.vc, gath_vc[0])
                if pi == 3:
                    B.wait("scalar", self.vc, gath_vc[0])
                for k in range(4):
                    def mms(pso, k=k, wi=wi):
                        for t in range(NCH):
                            fn = lambda E, pso=pso, t=t, k=k, wi=wi: E.matmul(
                                pso.ap(0, 128, 0, 512),
                                wbuf[wi].ap(0, 128, t * 128, 128),
                                self.xbf.ap(0, 128, t * S + k * 512, 512),
                                start=(t == 0), stop=(t == NCH - 1))
                            if t == NCH - 1:
                                B.inc("tensor", self.mm, fn)
                            else:
                                B.run("tensor", fn)

                    def evac(pso, k=k, func=func, dst=dst, bcol=bcol):
                        B.inc("scalar", self.sc, lambda E, pso=pso, k=k, func=func,
                              dst=dst, bcol=bcol: E.activation(
                                  dst.ap(0, 128, k * 512, 512), pso.ap(0, 128, 0, 512),
                                  func, bias=bqt.ap(0, 128, bcol, 1)))
                    piece_mm(mms, evac)
                wb_mm[wi] = self.mm.n

            B.wait("tensor", self.sc, self.sc.n)
            B.wait("tensor", self.vc, self.vc.n)
            B.wait("tensor", self.mm, vtok_mm[0])
            tp_ev = {}
            tps = [self.ps_scr, self.ps_o1, self.ps_o2]
            for mb in range(16):
                ps = tps[mb % 3]
                if mb >= 3:
                    B.wait("tensor", self.vc, tp_ev[mb - 3])
                B.inc("tensor", self.mm, lambda E, ps=ps, mb=mb: E.transpose(
                    ps.ap(0, 128, 0, 128), vtmp.ap(0, 128, mb * 128, 128),
                    self.eye32.ap()))
                B.wait("vector", self.mm, self.mm.n)
                B.run("vector", lambda E, ps=ps, mb=mb: E.tensor_copy(
                    vtokp.ap(0, 128, mb * 130, 64), ps.ap(0, 128, 0, 64)))
                B.inc("vector", self.vc, lambda E, ps=ps, mb=mb: E.tensor_copy(
                    vtokp.ap(0, 128, mb * 130 + 65, 64), ps.ap(0, 128, 64, 64)))
                tp_ev[mb] = self.vc.n
            B.wait("tensor", self.vc, self.vc.n)
            po_ev[0] = self.sc.n
            po_ev[1] = self.sc.n

            for hi, h in enumerate((hA, hB2)):
                B.wait("sync", self.mm, self.mm.n)
                par = par_ld[hi]
                B.wait("sync", par, par.n)
                B.inc("sync", par, lambda E, h=h: E.dma_start(
                    out=mkraw.ap(),
                    in_=dram_ap(self.wrowscr, h * L1_WROW, [[1, 128], [1, L1_BAND]])), 16)
                B.wait("tensor", par, par.n)
                B.wait("scalar", self.vc, mk_vc[0])
                for k in range(4):
                    w = min(512, L1_BAND - k * 512)

                    def mms(pso, k=k, w=w):
                        B.inc("tensor", self.mm, lambda E, pso=pso, k=k, w=w: E.matmul(
                            pso.ap(0, 128, 0, w), revb.ap(),
                            mkraw.ap(0, 128, k * 512, w), start=True, stop=True))

                    def evac(pso, k=k, w=w):
                        B.inc("scalar", self.sc, lambda E, pso=pso, k=k, w=w: E.activation(
                            mk2.ap(0, 128, k * 512, w), pso.ap(0, 128, 0, w), AF.Copy))
                    piece_mm(mms, evac)

                B.wait("tensor", self.sc, psa_sc[0])
                B.wait("tensor", self.vc, psa_vc[0])
                pt_mm = [0, 0]
                for mb in range(16):
                    m0 = 128 * mb
                    W = min(L1_BAND, S - m0)
                    pb_ = mb & 1
                    ptb = ptil[pb_]
                    npz = (W + 511) // 512
                    for k in range(npz):
                        n0 = m0 + k * 512
                        w = min(512, m0 + W - n0)

                        def mms(pso, h=h, hi_=h & 1, m0=m0, n0=n0, w=w):
                            B.wait("tensor", self.mm, pt_mm[pb_])
                            B.inc("tensor", self.mm, lambda E, pso=pso, hi_=hi_, m0=m0,
                                  n0=n0, w=w: E.matmul(
                                      pso.ap(0, 128, 0, w),
                                      kpair.ap(64 * hi_, 64, m0, 128),
                                      qpair.ap(64 * hi_, 64, n0, w),
                                      start=True, stop=True))

                        def evac(pso, n0=n0, w=w, ptb=ptb):
                            B.inc("scalar", self.sc, lambda E, pso=pso, n0=n0, w=w,
                                  ptb=ptb: E.activation(
                                      ptb.ap(0, 128, n0, w), pso.ap(0, 128, 0, w),
                                      AF.Exp, scale=0.125))
                        piece_mm(mms, evac)
                    if mb == 0:
                        B.inc("vector", self.vc, lambda E, ptb=ptb: E.memset(
                            ptb.ap(0, 128, L1_BAND, S - L1_BAND), 0.0))
                    B.wait("vector", self.sc, self.sc.n)
                    B.inc("vector", self.vc, lambda E, ptb=ptb, m0=m0, W=W: E.tensor_tensor(
                        ptb.ap(0, 128, m0, W), mk2.ap(0, 128, 0, W),
                        ptb.ap(0, 128, m0, W), OP.mult))
                    mk_vc[0] = self.vc.n
                    B.wait("tensor", self.vc, self.vc.n)
                    for b in range(4):
                        if mb == 0:
                            nlo, nhi = 512 * b, 512 * b + 512
                        else:
                            nlo = max(512 * b, m0)
                            nhi = min(512 * b + 512, m0 + W)
                            if nlo >= nhi:
                                continue
                        last = (mb == min(15, (512 * (b + 1) - 1) // 128))
                        fn = lambda E, b=b, nlo=nlo, nhi=nhi, mb=mb, hi_=h & 1, last=last: E.matmul(
                            self.ps_a.ap(0, 65, nlo, nhi - nlo),
                            vtokp.ap(0, 128, mb * 130 + 65 * hi_, 65),
                            ptil[mb & 1].ap(0, 128, nlo, nhi - nlo),
                            start=(mb == 0), stop=last)
                        if b == 3:
                            B.inc("tensor", self.mm, fn)
                        else:
                            B.run("tensor", fn)
                    pt_mm[pb_] = self.mm.n
                vtok_mm[0] = self.mm.n
                B.wait("scalar", self.mm, self.mm.n)
                for b in range(4):
                    B.inc("scalar", self.sc, lambda E, b=b, hi=hi: E.activation(
                        gathp.ap(64 * hi, 64, b * 512, 512),
                        self.ps_a.ap(0, 64, b * 512, 512), AF.Copy))
                B.wait("vector", self.mm, self.mm.n)
                B.wait("vector", self.ld, self.ld.n)
                B.inc("vector", self.vc, lambda E, hi=hi: E.tensor_tensor(
                    zsb.ap(0, 1, hi * S, S), self.ps_a.ap(64, 1, 0, S),
                    c2l.ap(0, 1, hi * S, S), OP.add))
                zvc[0] = self.vc.n
                psa_sc[0] = self.sc.n
                psa_vc[0] = self.vc.n

            stp = self.sta if pr % 2 == 0 else self.stb
            B.wait("gpsimd", self.sc, self.sc.n)
            B.wait("gpsimd", self.vc, self.vc.n)
            B.wait("gpsimd", stp, stp.n)
            B.inc("gpsimd", stp, lambda E, pr=pr: E.dma_start(
                out=dram_ap(self.zscr2, 2 * pr * S, [[S, 2], [1, S]]),
                in_=zsb.ap(0, 1, 0, 2 * S)), 16)
            B.wait("sync", stp, stp.n)
            B.wait("sync", self.ld, self.ld.n)
            B.inc("sync", self.ld, lambda E, pr=pr: E.dma_start(
                out=zrep.ap(),
                in_=dram_ap(self.zscr2, 2 * pr * S, [[S, 2], [0, 64], [1, S]])), 16)
            B.wait("vector", self.ld, self.ld.n)
            B.wait("vector", self.sc, self.sc.n)
            B.inc("vector", self.vc, lambda E: E.reciprocal(rzf.ap(), zrep.ap()))
            B.wait("vector", self.vc, self.vc.n)
            B.inc("vector", self.vc, lambda E: E.tensor_tensor(
                og1.ap(), gathp.ap(), gatep.ap(), OP.mult))
            B.wait("vector", self.vc, self.vc.n)
            B.inc("vector", self.vc, lambda E, pr=pr: E.tensor_tensor(
                ogbf.ap(0, 128, pr * S, S), og1.ap(), rzf.ap(), OP.mult))
            gath_vc[0] = self.vc.n

        s_t = ExitStack()
        xstage = B.sbuf("xstg1", 128, 2 * S, F32, stack=s_t)
        rstage = B.sbuf("rstg1", 128, 2 * S, F32, stack=s_t)
        tstage = B.sbuf("tstg1", 128, S, F32, stack=s_t)
        WOUT = self.win["l1a_wout"]
        ev2_at = {}
        st_at = {}
        B.wait("tensor", self.vc, self.vc.n)
        for co in range(NCH):
            wi = co & 1
            par = par_gw[wi]
            parx = par_ld[wi]
            B.wait("gpsimd", self.mm, wb_mm[wi])
            B.wait("gpsimd", par, par.n)
            B.inc("gpsimd", par, lambda E, co=co: E.dma_start(
                out=wbuf[co & 1].ap(),
                in_=dram_ap(WOUT, co * 128, [[D, 128], [128 * D, 8], [1, 128]])), 16)
            if co >= 2:
                B.wait("sync", self.vc, ev2_at[co - 2])
            B.wait("sync", parx, parx.n)
            B.inc("sync", parx, lambda E, co=co: E.dma_start(
                out=xstage.ap(0, 128, (co & 1) * S, S),
                in_=dram_ap(self.xspill, co * 128 * S, [[S, 128], [1, S]])), 16)
            B.wait("tensor", par, par.n)
            B.wait("vector", parx, parx.n)
            for k in range(4):
                def mms(pso, co=co, k=k):
                    for t in range(NCH):
                        fn = lambda E, pso=pso, co=co, t=t, k=k: E.matmul(
                            pso.ap(0, 128, 0, 512),
                            wbuf[co & 1].ap(0, 128, t * 128, 128),
                            ogbf.ap(0, 128, t * S + k * 512, 512),
                            start=(t == 0), stop=(t == NCH - 1))
                        if t == NCH - 1:
                            B.inc("tensor", self.mm, fn)
                        else:
                            B.run("tensor", fn)

                def evac(pso, co=co, k=k):
                    B.inc("scalar", self.sc, lambda E, pso=pso, co=co, k=k: E.activation(
                        tstage.ap(0, 128, k * 512, 512), pso.ap(0, 128, 0, 512),
                        AF.Identity, bias=bot.ap(0, 128, co, 1)))
                piece_mm(mms, evac)
            wb_mm[wi] = self.mm.n
            B.wait("vector", self.sc, self.sc.n)
            if co >= 2:
                B.wait("vector", st_at[co - 2][0], st_at[co - 2][1])
            B.inc("vector", self.vc, lambda E, co=co: E.tensor_tensor(
                rstage.ap(0, 128, (co & 1) * S, S), tstage.ap(),
                xstage.ap(0, 128, (co & 1) * S, S), OP.add))
            ev2_at[co] = self.vc.n
            stp = self.sta if co % 2 == 0 else self.stb
            B.wait("gpsimd", self.vc, self.vc.n)
            B.wait("gpsimd", stp, stp.n)
            B.inc("gpsimd", stp, lambda E, co=co: E.dma_start(
                out=dram_ap(self.xspill2, co * 128 * S, [[S, 128], [1, S]]),
                in_=rstage.ap(0, 128, (co & 1) * S, S)), 16)
            st_at[co] = (stp, stp.n)
        B.wait("gpsimd", self.sta, self.sta.n)
        B.wait("gpsimd", self.stb, self.stb.n)
        B.wait("gpsimd", self.vc, self.vc.n)
        B.wait("gpsimd", self.mm, self.mm.n)
        B.wait("gpsimd", self.sc, self.sc.n)
        self.barrier()
        s_t.close()
        s_a.close()
        self.alloc_xf()
        for c in range(NCH):
            B.inc("gpsimd", self.gl, lambda E, c=c: E.dma_start(
                out=self.xf.ap(0, 128, c * S, S),
                in_=dram_ap(self.xspill2, c * 128 * S, [[S, 128], [1, S]])), 16)
        B.wait("vector", self.gl, self.gl.n)
        B.wait("scalar", self.gl, self.gl.n)
        B.wait("tensor", self.gl, self.gl.n)
        self.ln_phase()

    def l1_ffn(self):
        B = self.B
        NIC = 22
        QMAXI = 1.0 / 127.0
        RG = [[0, 1, 2, 3, 4, 5, 6, 7]]

        s_m = ExitStack()
        macc = B.sbuf("macc", 128, 4, F32, stack=s_m)
        gsc = B.sbuf("gsc", 1, 16, F32, stack=s_m)
        B.wait("vector", self.vc, self.vc.n)
        B.inc("vector", self.vc, lambda E: E.tensor_reduce(
            macc.ap(0, 128, 0, 1), self.xf.ap(), AX.X, OP.max, apply_absolute_value=True))
        B.wait("tensor", self.vc, self.vc.n)
        B.inc("tensor", self.mm, lambda E: E.transpose(
            self.ps_scr.ap(0, 1, 0, 128), macc.ap(0, 128, 0, 1), self.eye32.ap()))
        B.wait("vector", self.mm, self.mm.n)
        B.inc("vector", self.vc, lambda E: E.tensor_reduce(
            gsc.ap(0, 1, 0, 1), self.ps_scr.ap(0, 1, 0, 128), AX.X, OP.max))
        B.wait("gpsimd", self.vc, self.vc.n)
        B.inc("gpsimd", self.gb, lambda E: E.dma_start(
            out=dram_ap(self.ccxi, 0, [[1, 1]]), in_=gsc.ap(0, 1, 0, 1)), 16)
        B.wait("gpsimd", self.gb, self.gb.n)
        B.inc("gpsimd", self.ccs, lambda E: E.collective_compute(
            "AllReduce", OP.max, replica_groups=RG,
            ins=[self.ccxi.ap().opt()], outs=[self.ccxo.ap().opt()]), 1)
        B.wait("gpsimd", self.ccs, self.ccs.n)
        B.inc("gpsimd", self.gb, lambda E: E.dma_start(
            out=gsc.ap(0, 1, 1, 1), in_=dram_ap(self.ccxo, 0, [[1, 1]])), 16)
        B.wait("sync", self.ld, self.ld.n)
        B.inc("sync", self.ld, lambda E: E.dma_start(
            out=gsc.ap(0, 1, 2, 1), in_=dram_ap(self.win["l1f_clipx"], 0, [[1, 1]])), 16)
        B.wait("sync", self.ld, self.ld.n)
        B.inc("sync", self.ld, lambda E: E.dma_start(
            out=gsc.ap(0, 1, 3, 1), in_=dram_ap(self.win["l1f_clipd"], 0, [[1, 1]])), 16)
        B.wait("scalar", self.ld, self.ld.n)
        B.inc("scalar", self.sc, lambda E: E.activation(
            gsc.ap(0, 1, 4, 2), gsc.ap(0, 1, 2, 2), AF.Sigmoid))
        B.wait("vector", self.sc, self.sc.n)
        B.wait("vector", self.gb, self.gb.n)
        B.inc("vector", self.vc, lambda E: E.tensor_tensor(
            gsc.ap(0, 1, 6, 1), gsc.ap(0, 1, 4, 1), gsc.ap(0, 1, 1, 1), OP.mult))
        B.wait("vector", self.vc, self.vc.n)
        B.inc("vector", self.vc, lambda E: E.tensor_scalar(
            gsc.ap(0, 1, 6, 1), gsc.ap(0, 1, 6, 1), QMAXI, None, OP.mult))
        B.wait("vector", self.vc, self.vc.n)
        B.inc("vector", self.vc, lambda E: E.reciprocal(
            gsc.ap(0, 1, 7, 1), gsc.ap(0, 1, 6, 1)))
        B.wait("gpsimd", self.vc, self.vc.n)
        B.inc("gpsimd", self.gb, lambda E: E.dma_start(
            out=dram_ap(self.rowscr, 0, [[1, 8]]), in_=gsc.ap(0, 1, 0, 8)), 16)

        for c in range(NCH):
            B.inc("gpsimd", self.gl, lambda E, c=c: E.dma_start(
                out=dram_ap(self.xspill, c * 128 * S, [[S, 128], [1, S]]),
                in_=self.xf.ap(0, 128, c * S, S)), 16)
        B.wait("gpsimd", self.gl, self.gl.n)
        B.inc("gpsimd", self.gc, lambda E: E.nop())
        for e in ENGINES:
            B.wait(e, self.gc, self.gc.n)
        s_m.close()
        self.s_xf.close()

        s_f = ExitStack()
        wdqT = B.sbuf("wdqT", 128, NIC * 8 * 128, BF16, stack=s_f)
        hqc = B.sbuf("hqc", 128, NIC * 512, BF16, stack=s_f)
        wstg = [B.sbuf(f"wstg{i}", 128, 2816, F32, stack=s_f) for i in range(2)]
        quf = B.sbuf("quf", 128, 2816, F32, stack=s_f)
        qf = B.sbuf("qf", 128, 2816, F32, stack=s_f)
        wglT = B.sbuf("wglT", 128, 1024, BF16, stack=s_f)
        wulT = B.sbuf("wulT", 128, 1024, BF16, stack=s_f)
        silg = B.sbuf("silg", 128, 512, F32, stack=s_f)
        hstg = [B.sbuf(f"hstg{i}", 128, S, F32, stack=s_f) for i in range(2)]
        swg = B.sbuf("swg", 128, 96, F32, stack=s_f)
        scl = B.sbuf("scl", 128, 12, F32, stack=s_f)
        hmc = B.sbuf("hmc", 128, NIC + 2, F32, stack=s_f)
        gsc2 = B.sbuf("gsc2", 1, 16, F32, stack=s_f)

        B.wait("sync", self.gb, self.gb.n)
        B.wait("sync", self.ld, self.ld.n)
        B.inc("sync", self.ld, lambda E: E.dma_start(
            out=scl.ap(0, 128, 0, 8), in_=dram_ap(self.rowscr, 0, [[0, 128], [1, 8]])), 16)
        B.wait("scalar", self.ld, self.ld.n)
        B.wait("vector", self.ld, self.ld.n)

        for c in range(NCH):
            B.wait("scalar", self.vc, self.vc.n)
            B.inc("scalar", self.sc, lambda E, c=c: E.activation(
                quf.ap(0, 128, 0, S), self.xf.ap(0, 128, c * S, S),
                AF.Copy, scale=scl.ap(0, 128, 7, 1)))
            B.wait("vector", self.sc, self.sc.n)
            B.inc("vector", self.vc, lambda E: E.tensor_scalar(
                qf.ap(0, 128, 0, S), quf.ap(0, 128, 0, S), RNE_C, RNE_C, OP.add, OP.subtract))
            B.wait("vector", self.vc, self.vc.n)
            B.inc("vector", self.vc, lambda E: E.tensor_scalar(
                qf.ap(0, 128, 0, S), qf.ap(0, 128, 0, S), 127.0, -128.0, OP.min, OP.max))
            B.wait("vector", self.vc, self.vc.n)
            B.inc("vector", self.vc, lambda E, c=c: E.tensor_copy(
                self.xbf.ap(0, 128, c * S, S), qf.ap(0, 128, 0, S)))

        po = [self.ps_o1, self.ps_o2]
        psev = {id(self.ps_o1): (self.sc, 0), id(self.ps_o2): (self.sc, 0),
                id(self.ps_scr): (self.sc, 0)}
        po_i = [0]

        def piece(mms, evacs, eng="scalar"):
            i = po_i[0] & 1
            pso = po[i]
            sem, cnt = psev[id(pso)]
            B.wait("tensor", sem, cnt)
            mms(pso)
            B.wait(eng, self.mm, self.mm.n)
            evacs(pso)
            psev[id(pso)] = (self.sc, self.sc.n) if eng == "scalar" else (self.vc, self.vc.n)
            po_i[0] += 1

        par_ld = [self.lda, self.ldb]
        par_gw = [self.gwa, self.gwb]
        wld_gate = [0, 0]
        qf_mm = [0]
        silg_vc = [0]

        def quant_weight(wi, blk, dram, cols, scol):
            par = par_gw[wi & 1]
            B.wait("gpsimd", self.vc, wld_gate[wi & 1])
            B.wait("gpsimd", par, par.n)
            B.inc("gpsimd", par, lambda E, wi=wi, blk=blk, dram=dram, cols=cols: E.dma_start(
                out=wstg[wi & 1].ap(0, 128, 0, cols),
                in_=dram_ap(dram, blk * 128 * cols, [[cols, 128], [1, cols]])), 16)
            B.wait("vector", par, par.n)
            B.inc("vector", self.vc, lambda E, wi=wi, cols=cols, scol=scol: E.tensor_reduce(
                swg.ap(0, 128, scol, 1), wstg[wi & 1].ap(0, 128, 0, cols), AX.X, OP.max, apply_absolute_value=True))
            B.wait("vector", self.vc, self.vc.n)
            B.inc("vector", self.vc, lambda E, scol=scol: E.tensor_scalar(
                swg.ap(0, 128, scol, 1), swg.ap(0, 128, scol, 1), QMAXI, None, OP.mult))
            B.wait("vector", self.vc, self.vc.n)
            B.inc("vector", self.vc, lambda E, scol=scol: E.reciprocal(
                scl.ap(0, 128, 8, 1), swg.ap(0, 128, scol, 1)))
            B.wait("scalar", self.vc, self.vc.n)
            B.inc("scalar", self.sc, lambda E, wi=wi, cols=cols: E.activation(
                quf.ap(0, 128, 0, cols), wstg[wi & 1].ap(0, 128, 0, cols),
                AF.Copy, scale=scl.ap(0, 128, 8, 1)))
            B.wait("vector", self.sc, self.sc.n)
            B.wait("vector", self.mm, qf_mm[0])
            B.inc("vector", self.vc, lambda E, cols=cols: E.tensor_scalar(
                qf.ap(0, 128, 0, cols), quf.ap(0, 128, 0, cols), RNE_C, RNE_C,
                OP.add, OP.subtract))
            wld_gate[wi & 1] = self.vc.n
            B.wait("vector", self.vc, self.vc.n)
            B.inc("vector", self.vc, lambda E, cols=cols: E.tensor_scalar(
                qf.ap(0, 128, 0, cols), qf.ap(0, 128, 0, cols), 127.0, -128.0,
                OP.min, OP.max))

        tps = [self.ps_scr, self.ps_o1, self.ps_o2]

        def transpose_to(dstbuf, coloff, nblk, colstep):
            B.wait("tensor", self.vc, self.vc.n)
            for tb in range(nblk):
                ps = tps[tb % 3]
                sem, cnt = psev[id(ps)]
                B.wait("tensor", sem, cnt)
                B.inc("tensor", self.mm, lambda E, ps=ps, tb=tb: E.transpose(
                    ps.ap(0, 128, 0, 128), qf.ap(0, 128, tb * 128, 128), self.eye32.ap()))
                B.wait("vector", self.mm, self.mm.n)
                B.inc("vector", self.vc, lambda E, ps=ps, tb=tb, dstbuf=dstbuf,
                      coloff=coloff, colstep=colstep: E.tensor_copy(
                          dstbuf.ap(0, 128, coloff + tb * colstep, 128), ps.ap(0, 128, 0, 128)))
                psev[id(ps)] = (self.vc, self.vc.n)
            qf_mm[0] = self.mm.n

        hst_st = {}
        for ic in range(NIC):
            quant_weight(2 * ic, ic, self.win["l1f_wg"], D, ic)
            transpose_to(wglT, 0, 8, 128)
            B.wait("vector", self.ld, self.ld.n)
            B.inc("vector", self.vc, lambda E, ic=ic: E.tensor_scalar(
                swg.ap(0, 128, 44 + ic, 1), swg.ap(0, 128, ic, 1),
                scl.ap(0, 128, 6, 1), None, OP.mult))
            quant_weight(2 * ic + 1, ic, self.win["l1f_wu"], D, 22 + ic)
            transpose_to(wulT, 0, 8, 128)
            B.inc("vector", self.vc, lambda E, ic=ic: E.tensor_scalar(
                swg.ap(0, 128, 66 + ic, 1), swg.ap(0, 128, 22 + ic, 1),
                scl.ap(0, 128, 6, 1), None, OP.mult))
            fold_vc = self.vc.n
            hbuf = hstg[ic & 1]
            if hst_st.get(ic - 2) is not None:
                B.wait("vector", hst_st[ic - 2][0], hst_st[ic - 2][1])
            for np_ in range(4):
                def mms_g(pso, np_=np_):
                    for dc in range(NCH):
                        fn = lambda E, pso=pso, dc=dc, np_=np_: E.matmul(
                            pso.ap(0, 128, 0, 512), wglT.ap(0, 128, dc * 128, 128),
                            self.xbf.ap(0, 128, dc * S + np_ * 512, 512),
                            start=(dc == 0), stop=(dc == NCH - 1))
                        if dc == NCH - 1:
                            B.inc("tensor", self.mm, fn)
                        else:
                            B.run("tensor", fn)

                def ev_g(pso, ic=ic, fold_vc=fold_vc):
                    B.wait("scalar", self.vc, max(silg_vc[0], fold_vc))
                    B.inc("scalar", self.sc, lambda E, pso=pso, ic=ic: E.activation(
                        silg.ap(), pso.ap(0, 128, 0, 512),
                        AF.Silu, scale=swg.ap(0, 128, 44 + ic, 1)))
                piece(mms_g, ev_g)

                def mms_u(pso, np_=np_):
                    for dc in range(NCH):
                        fn = lambda E, pso=pso, dc=dc, np_=np_: E.matmul(
                            pso.ap(0, 128, 0, 512), wulT.ap(0, 128, dc * 128, 128),
                            self.xbf.ap(0, 128, dc * S + np_ * 512, 512),
                            start=(dc == 0), stop=(dc == NCH - 1))
                        if dc == NCH - 1:
                            B.inc("tensor", self.mm, fn)
                        else:
                            B.run("tensor", fn)

                def ev_u(pso, ic=ic, np_=np_, hbuf=hbuf):
                    B.wait("vector", self.sc, self.sc.n)
                    B.wait("vector", self.vc, self.vc.n)
                    B.inc("vector", self.vc, lambda E, pso=pso, ic=ic, np_=np_,
                          hbuf=hbuf: E.tensor_scalar(
                              hbuf.ap(0, 128, np_ * 512, 512), pso.ap(0, 128, 0, 512),
                              swg.ap(0, 128, 66 + ic, 1), None, OP.mult))
                    B.wait("vector", self.vc, self.vc.n)
                    B.inc("vector", self.vc, lambda E, np_=np_, hbuf=hbuf: E.tensor_tensor(
                        hbuf.ap(0, 128, np_ * 512, 512), silg.ap(),
                        hbuf.ap(0, 128, np_ * 512, 512), OP.mult))
                    silg_vc[0] = self.vc.n
                piece(mms_u, ev_u, eng="vector")
            B.wait("vector", self.vc, self.vc.n)
            B.inc("vector", self.vc, lambda E, ic=ic, hbuf=hbuf: E.tensor_reduce(
                hmc.ap(0, 128, ic, 1), hbuf.ap(), AX.X, OP.max, apply_absolute_value=True))
            stp = self.sta if ic % 2 == 0 else self.stb
            B.wait("gpsimd", self.vc, self.vc.n)
            B.wait("gpsimd", stp, stp.n)
            B.inc("gpsimd", stp, lambda E, ic=ic, hbuf=hbuf: E.dma_start(
                out=dram_ap(self.hspill, ic * 128 * S, [[S, 128], [1, S]]),
                in_=hbuf.ap()), 16)
            hst_st[ic] = (stp, stp.n)

        B.wait("vector", self.vc, self.vc.n)
        B.inc("vector", self.vc, lambda E: E.tensor_reduce(
            hmc.ap(0, 128, NIC, 1), hmc.ap(0, 128, 0, NIC), AX.X, OP.max))
        B.wait("tensor", self.vc, self.vc.n)
        B.wait("tensor", self.sc, self.sc.n)
        B.inc("tensor", self.mm, lambda E: E.transpose(
            self.ps_scr.ap(0, 1, 0, 128), hmc.ap(0, 128, NIC, 1), self.eye32.ap()))
        B.wait("vector", self.mm, self.mm.n)
        B.inc("vector", self.vc, lambda E: E.tensor_reduce(
            gsc2.ap(0, 1, 0, 1), self.ps_scr.ap(0, 1, 0, 128), AX.X, OP.max))
        B.wait("gpsimd", self.vc, self.vc.n)
        B.inc("gpsimd", self.gb, lambda E: E.dma_start(
            out=dram_ap(self.cchi, 0, [[1, 1]]), in_=gsc2.ap(0, 1, 0, 1)), 16)
        B.wait("gpsimd", self.gb, self.gb.n)
        B.inc("gpsimd", self.ccs, lambda E: E.collective_compute(
            "AllReduce", OP.max, replica_groups=RG,
            ins=[self.cchi.ap().opt()], outs=[self.ccho.ap().opt()]), 1)
        B.wait("gpsimd", self.ccs, self.ccs.n)
        B.inc("gpsimd", self.gb, lambda E: E.dma_start(
            out=gsc2.ap(0, 1, 1, 1), in_=dram_ap(self.ccho, 0, [[1, 1]])), 16)
        B.wait("vector", self.gb, self.gb.n)
        B.inc("vector", self.vc, lambda E: E.tensor_tensor(
            gsc2.ap(0, 1, 2, 1), gsc2.ap(0, 1, 1, 1), scl.ap(0, 1, 5, 1), OP.mult))
        B.wait("vector", self.vc, self.vc.n)
        B.inc("vector", self.vc, lambda E: E.tensor_scalar(
            gsc2.ap(0, 1, 2, 1), gsc2.ap(0, 1, 2, 1), QMAXI, None, OP.mult))
        B.wait("vector", self.vc, self.vc.n)
        B.inc("vector", self.vc, lambda E: E.reciprocal(
            gsc2.ap(0, 1, 3, 1), gsc2.ap(0, 1, 2, 1)))
        B.wait("gpsimd", self.vc, self.vc.n)
        B.inc("gpsimd", self.gb, lambda E: E.dma_start(
            out=dram_ap(self.rowscr, 8, [[1, 4]]), in_=gsc2.ap(0, 1, 0, 4)), 16)
        B.wait("sync", self.gb, self.gb.n)
        B.wait("sync", self.ld, self.ld.n)
        B.inc("sync", self.ld, lambda E: E.dma_start(
            out=scl.ap(0, 128, 9, 2), in_=dram_ap(self.rowscr, 10, [[0, 128], [1, 2]])), 16)
        B.wait("scalar", self.ld, self.ld.n)
        B.wait("vector", self.ld, self.ld.n)

        for dc in range(NCH):
            quant_weight(dc, dc, self.win["l1f_wd"], 2816, 88 + dc)
            transpose_to(wdqT, dc * 128, NIC, 8 * 128)

        s_t = ExitStack()
        xstage = B.sbuf("xstg2", 128, 2 * S, F32, stack=s_t)
        tst2 = B.sbuf("tst2", 128, 512, F32, stack=s_t)
        rst2 = B.sbuf("rst2", 128, 2 * S, F32, stack=s_t)
        h16 = B.sbuf("h16", 128, 512, F32, stack=s_t)
        B.wait("sync", self.sta, self.sta.n)
        B.wait("sync", self.stb, self.stb.n)
        xst_at = {}
        st2_at = {}
        hq_vc = {}
        for np_ in range(4):
            if np_ > 0:
                B.wait("vector", self.mm, self.mm.n)
            for ic in range(NIC):
                par = par_ld[ic & 1]
                B.wait("sync", par, par.n)
                B.wait("sync", self.vc, hq_vc.get(ic - 2, 0))
                B.inc("sync", par, lambda E, ic=ic, np_=np_: E.dma_start(
                    out=hstg[0].ap(0, 128, (ic & 1) * 512, 512),
                    in_=dram_ap(self.hspill, ic * 128 * S + np_ * 512, [[S, 128], [1, 512]])), 16)
                B.wait("scalar", par, par.n)
                B.wait("scalar", self.vc, self.vc.n)
                B.inc("scalar", self.sc, lambda E, ic=ic: E.activation(
                    h16.ap(), hstg[0].ap(0, 128, (ic & 1) * 512, 512),
                    AF.Copy, scale=scl.ap(0, 128, 10, 1)))
                B.wait("vector", self.sc, self.sc.n)
                B.inc("vector", self.vc, lambda E: E.tensor_scalar(
                    h16.ap(), h16.ap(), RNE_C, RNE_C, OP.add, OP.subtract))
                B.wait("vector", self.vc, self.vc.n)
                B.inc("vector", self.vc, lambda E: E.tensor_scalar(
                    h16.ap(), h16.ap(), 127.0, -128.0, OP.min, OP.max))
                B.wait("vector", self.vc, self.vc.n)
                B.inc("vector", self.vc, lambda E, ic=ic: E.tensor_copy(
                    hqc.ap(0, 128, ic * 512, 512), h16.ap()))
                hq_vc[ic] = self.vc.n
            B.wait("tensor", self.vc, self.vc.n)
            for dc in range(NCH):
                parx = par_ld[dc & 1]
                if xst_at.get((np_, dc - 2)) is not None:
                    B.wait("sync", self.vc, xst_at[(np_, dc - 2)])
                B.wait("sync", parx, parx.n)
                B.inc("sync", parx, lambda E, dc=dc, np_=np_: E.dma_start(
                    out=xstage.ap(0, 128, (dc & 1) * S + np_ * 512, 512),
                    in_=dram_ap(self.xspill, dc * 128 * S + np_ * 512, [[S, 128], [1, 512]])), 16)

                def mms_d(pso, dc=dc):
                    for ic in range(NIC):
                        fn = lambda E, pso=pso, dc=dc, ic=ic: E.matmul(
                            pso.ap(0, 128, 0, 512),
                            wdqT.ap(0, 128, (ic * 8 + dc) * 128, 128),
                            hqc.ap(0, 128, ic * 512, 512),
                            start=(ic == 0), stop=(ic == NIC - 1))
                        if ic == NIC - 1:
                            B.inc("tensor", self.mm, fn)
                        else:
                            B.run("tensor", fn)

                def ev_d(pso, dc=dc):
                    B.wait("scalar", self.vc, self.vc.n)
                    B.inc("scalar", self.sc, lambda E, pso=pso, dc=dc: E.activation(
                        tst2.ap(), pso.ap(0, 128, 0, 512),
                        AF.Identity, scale=swg.ap(0, 128, 88 + dc, 1)))
                piece(mms_d, ev_d)
                B.wait("vector", self.sc, self.sc.n)
                B.wait("vector", parx, parx.n)
                if st2_at.get((np_, dc - 2)) is not None:
                    B.wait("vector", st2_at[(np_, dc - 2)][0], st2_at[(np_, dc - 2)][1])
                B.inc("vector", self.vc, lambda E: E.tensor_scalar(
                    tst2.ap(), tst2.ap(), scl.ap(0, 128, 9, 1), None, OP.mult))
                B.wait("vector", self.vc, self.vc.n)
                B.inc("vector", self.vc, lambda E, dc=dc, np_=np_: E.tensor_tensor(
                    rst2.ap(0, 128, (dc & 1) * S + np_ * 512, 512), tst2.ap(),
                    xstage.ap(0, 128, (dc & 1) * S + np_ * 512, 512), OP.add))
                xst_at[(np_, dc)] = self.vc.n
                stp = self.sta if dc % 2 == 0 else self.stb
                B.wait("gpsimd", self.vc, self.vc.n)
                B.wait("gpsimd", stp, stp.n)
                B.inc("gpsimd", stp, lambda E, dc=dc, np_=np_: E.dma_start(
                    out=dram_ap(self.xspill2, dc * 128 * S + np_ * 512, [[S, 128], [1, 512]]),
                    in_=rst2.ap(0, 128, (dc & 1) * S + np_ * 512, 512)), 16)
                st2_at[(np_, dc)] = (stp, stp.n)
        B.wait("gpsimd", self.sta, self.sta.n)
        B.wait("gpsimd", self.stb, self.stb.n)
        B.wait("gpsimd", self.vc, self.vc.n)
        B.wait("gpsimd", self.mm, self.mm.n)
        B.wait("gpsimd", self.sc, self.sc.n)
        self.barrier()
        s_t.close()
        s_f.close()
        self.alloc_xf()
        for c in range(NCH):
            B.inc("gpsimd", self.gl, lambda E, c=c: E.dma_start(
                out=self.xf.ap(0, 128, c * S, S),
                in_=dram_ap(self.xspill2, c * 128 * S, [[S, 128], [1, S]])), 16)
        B.wait("vector", self.gl, self.gl.n)
        B.wait("scalar", self.gl, self.gl.n)
        B.wait("tensor", self.gl, self.gl.n)
        self.ln_phase()

    def _bf_regather(self, srcbuf, ap_a):
        B = self.B
        hi, lo = ap_a
        sapA = type(srcbuf._base)(srcbuf._base.tensor, srcbuf.off,
                                  [[srcbuf.pitch, 32], list(hi), [3, 2], list(lo)])
        sapB = type(srcbuf._base)(srcbuf._base.tensor, srcbuf.off + 1,
                                  [[srcbuf.pitch, 32], list(hi), [1, 2], list(lo)])
        B.run("vector", lambda E, sapA=sapA: E.tensor_copy(
            self.ta.apd(0, [[self.ta.pitch, 32], [1, 128]]), sapA))
        B.inc("vector", self.vc, lambda E, sapB=sapB: E.tensor_copy(
            self.ta.apd(128, [[self.ta.pitch, 32], [1, 128]]), sapB))

    def _grid_change(self, src, dst, N, fwd):
        B = self.B
        W = 32 * N
        B.wait("sync", self.ev, self.ev.n)
        B.wait("sync", self.ld, self.ld.n)
        wsap = src.ap(0, 128, 0, W)
        if fwd:
            wdap = dram_ap(self.gcscr, 0, [[W, 4], [N, 32], [4 * W, 32], [1, N]])
        else:
            wdap = dram_ap(self.gcscr, 0, [[N, 32], [32 * W, 4], [W, 32], [1, N]])
        B.inc("sync", self.ld, lambda E, wsap=wsap, wdap=wdap: E.dma_start(out=wdap, in_=wsap), 16)
        B.wait("sync", self.ld, self.ld.n)
        rs = dst.ap(0, 128, 0, W)
        rd = dram_ap(self.gcscr, 0, [[W, 128], [1, W]])
        B.inc("sync", self.ld, lambda E, rs=rs, rd=rd: E.dma_start(out=rs, in_=rd), 16)
        B.wait("tensor", self.ld, self.ld.n)

    def _wcon_pipe(self, groups, evacs, tq, xstage, rstage, wcbuf):
        B = self.B
        NB = 4
        QT = 512
        B.wait("tensor", self.ev, self.ev.n)
        evcon2_at = {}
        mm_at = {}
        for g, pairs in enumerate(groups):
            m = g
            par = self.lda if m % 2 == 0 else self.ldb
            parw = self.gwa if m % 2 == 0 else self.gwb
            if m >= 2:
                B.wait("sync", self.ev, evcon2_at[m - 2])
                B.wait("gpsimd", self.mm, mm_at[m - 2])
            B.wait("sync", par, par.n)
            B.inc("sync", par, lambda E, m=m, tq=tq, xstage=xstage: E.dma_start(
                out=xstage.ap(0, 128, (m % 2) * QT, QT),
                in_=dram_ap(self.xspill, m * 128 * S + tq * QT, [[S, 128], [1, QT]])), 16)
            B.wait("gpsimd", parw, parw.n)
            B.inc("gpsimd", parw, lambda E, m=m, wcbuf=wcbuf: E.dma_start(
                out=wcbuf[m % 2].ap(),
                in_=dram_ap(self.win["l0f_wcon"], m * 128,
                            [[32 * 1024, 128], [1024, 32], [1, 128]])), 16)
            pt = self.pa4[g % NB]
            bank = pt.offset // 512
            prev = self.bank_ev.get(bank)
            if prev:
                B.wait("tensor", self.ev, prev)
            B.wait("tensor", par, par.n)
            B.wait("tensor", parw, parw.n)
            n = len(pairs)
            for i, (lh, rh) in enumerate(pairs):
                st_ = (i == 0)
                if i == n - 1:
                    B.inc("tensor", self.mm,
                          lambda E, lh=lh, rh=rh, pt=pt, st_=st_: E.matmul(pt, lh, rh, start=st_, stop=True))
                else:
                    B.run("tensor",
                          lambda E, lh=lh, rh=rh, pt=pt, st_=st_: E.matmul(pt, lh, rh, start=st_, stop=False))
            mm_at[m] = self.mm.n
            B.wait("vector", self.mm, self.mm.n)
            spar = self.sta if m % 2 == 0 else self.stb
            fns = evacs[g]
            B.wait("vector", spar, spar.n)
            B.wait("vector", self.ev, self.ev.n)
            B.inc("vector", self.vc, lambda E, f=fns[0], pt=pt: f(E, pt))
            B.wait("vector", self.vc, self.vc.n)
            B.wait("vector", par, par.n)
            B.inc("vector", self.ev, lambda E, f=fns[1], pt=pt: f(E, pt))
            evcon2_at[m] = self.ev.n
            self.bank_ev[bank] = self.ev.n
            B.wait("gpsimd", self.ev, self.ev.n)
            B.inc("gpsimd", spar, lambda E, m=m, tq=tq, rstage=rstage: E.dma_start(
                out=dram_ap(self.xspill2, m * 128 * S + tq * QT, [[S, 128], [1, QT]]),
                in_=rstage.ap(0, 128, (m % 2) * QT, QT)), 16)

    def emit_output(self):
        B = self.B
        s_o = ExitStack()
        otok = B.sbuf("otok", 128, D, F32, stack=s_o)
        for t in range(NT):
            for hf in range(2):
                if t > 0 or hf > 0:
                    B.wait("tensor", self.vc, self.vc.n)
                for j in range(4):
                    c = hf * 4 + j
                    fn = lambda E, c=c, j=j, hf=hf, t=t: E.transpose(
                        self.ps_a.ap(0, 128, hf * 1024 + j * 128, 128),
                        self.xf.ap(0, 128, c * S + t * 128, 128), self.eye32.ap())
                    if j == 3:
                        B.inc("tensor", self.mm, fn)
                    else:
                        B.run("tensor", fn)
                B.wait("vector", self.mm, self.mm.n)
                if t > 0:
                    B.wait("vector", self.st, self.st.n)
                B.inc("vector", self.vc, lambda E, hf=hf, otok=otok: E.tensor_copy(
                    otok.apd(hf * 512, [[D, 128], [128, 4], [1, 128]]),
                    self.ps_a.apd(hf * 1024, [[2048, 128], [128, 4], [1, 128]])))
            B.wait("gpsimd", self.vc, self.vc.n)
            B.inc("gpsimd", self.st, lambda E, t=t, otok=otok: E.dma_start(
                out=dram_ap(self.y_out, t * 128 * D, [[D, 128], [1, D]]), in_=otok.ap()), 16)
        B.wait("gpsimd", self.st, self.st.n)
        s_o.close()

    def build(self):
        self.alloc_xf()
        self.phase0()
        if self.n_stages >= 1:
            self.l0_attn()
        if self.n_stages >= 2:
            self.l0_ffn()
        if self.n_stages >= 3:
            self.l1_attn()
        if self.n_stages >= 4:
            self.l1_ffn()
        self.emit_output()
        self.s_xf.close()
        self.B.finalize()
        return self.nc


def build_program(n_stages=N_STAGES):
    return Prog(n_stages).build()


def _gsel_const():
    g = np.zeros((44, 23), dtype=ml_dtypes.bfloat16)
    for j in range(11):
        for tau in range(4):
            o = (1 << j) * tau
            slot = 4 * j + tau
            if o >= S:
                g[slot, 22] = 1.0
            else:
                g[slot, L1_OFFS.index(o)] = 1.0
    return g


def _stc_const():
    st = np.zeros((23, S), dtype=ml_dtypes.bfloat16)
    n = np.arange(S)
    for i, o in enumerate(L1_OFFS):
        st[i, n < o] = 1.0
    st[22] = 1.0
    return st


def _pmc_const():
    pm = np.zeros((128, 7 * 128), dtype=ml_dtypes.bfloat16)
    for i in range(7):
        st = 1 << i
        p = np.arange(128)
        pm[p, i * 128 + (p ^ st)] = 1.0
    return pm


def kernel(**inputs):
    nc = build_program(N_STAGES)
    x = np.ascontiguousarray(inputs["L0_attn_x"], dtype=np.float32)
    common = {
        "l0a_wq": np.ascontiguousarray(inputs["L0_attn_W_q"], dtype=np.float32),
        "l0a_wk": np.ascontiguousarray(inputs["L0_attn_W_k"], dtype=np.float32),
        "l0a_wv": np.ascontiguousarray(inputs["L0_attn_W_v"], dtype=np.float32),
        "l0a_wo": np.ascontiguousarray(inputs["L0_attn_W_o"], dtype=np.float32),
        "l0a_wdkv": np.ascontiguousarray(inputs["L0_attn_W_down_kv"], dtype=np.float32),
        "l0a_wupk": np.ascontiguousarray(inputs["L0_attn_W_up_k"], dtype=np.float32),
        "l0a_wupv": np.ascontiguousarray(inputs["L0_attn_W_up_v"], dtype=np.float32),
        "l0f_wexp": np.ascontiguousarray(inputs["L0_ffn_W_exp"], dtype=np.float32),
        "l0f_wcon": np.ascontiguousarray(inputs["L0_ffn_W_con"], dtype=np.float32),
        "l0f_bexp": np.ascontiguousarray(inputs["L0_ffn_b_exp"], dtype=np.float32).reshape(1, 4096),
        "l0f_upb": np.ascontiguousarray(inputs["L0_ffn_up_bias"], dtype=np.float32).reshape(1, 4096),
        "l0f_bcon": np.ascontiguousarray(inputs["L0_ffn_b_con"], dtype=np.float32).reshape(1, 1024),
        "l0f_dnb": np.ascontiguousarray(inputs["L0_ffn_down_bias"], dtype=np.float32).reshape(1, 1024),
        "l0f_upw": np.ascontiguousarray(inputs["L0_ffn_up_weights"], dtype=np.float32).reshape(12, 8192),
        "l0f_dnw": np.ascontiguousarray(inputs["L0_ffn_down_weights"], dtype=np.float32).reshape(12, 8192),
        "l0f_pmc": _pmc_const(),
        "l1a_wqkv": np.ascontiguousarray(inputs["L1_attn_w_qkv"], dtype=np.float32),
        "l1a_bqkv": np.ascontiguousarray(inputs["L1_attn_b_qkv"], dtype=np.float32).reshape(1, 3072),
        "l1a_wout": np.ascontiguousarray(inputs["L1_attn_w_out"], dtype=np.float32),
        "l1a_bout": np.ascontiguousarray(inputs["L1_attn_b_out"], dtype=np.float32).reshape(1, 1024),
        "l1a_wgate": np.ascontiguousarray(inputs["L1_attn_w_gate"], dtype=np.float32),
        "l1a_bgate": np.ascontiguousarray(inputs["L1_attn_b_gate"], dtype=np.float32).reshape(1, 1024),
        "l1a_pb": np.ascontiguousarray(inputs["L1_attn_pos_bias"], dtype=np.float32),
        "l1a_gsel": _gsel_const(),
        "l1a_stc": _stc_const(),
        "l1f_wg": np.ascontiguousarray(inputs["L1_ffn_w_gate"], dtype=np.float32),
        "l1f_wu": np.ascontiguousarray(inputs["L1_ffn_w_up"], dtype=np.float32),
        "l1f_wd": np.ascontiguousarray(inputs["L1_ffn_w_down"], dtype=np.float32),
        "l1f_clipx": np.ascontiguousarray(inputs["L1_ffn_clip_x"], dtype=np.float32).reshape(1, 1),
        "l1f_clipd": np.ascontiguousarray(inputs["L1_ffn_clip_down"], dtype=np.float32).reshape(1, 1),
    }
    in_maps = [dict(common, x=x[b]) for b in range(8)]
    res = run_bass_kernel_spmd(nc, in_maps, core_ids=list(range(8)))
    return np.stack([res.results[b]["y"] for b in range(8)], axis=0)


# version 06 (final)
# speedup vs baseline: 1.0392x; 1.0392x over previous
import math
import numpy as np
import ml_dtypes
from contextlib import ExitStack

import concourse.bass as bass
import concourse.mybir as mybir
from concourse.bass_utils import run_bass_kernel_spmd
from concourse.alu_op_type import AluOpType as OP

F32 = mybir.dt.float32
BF16 = mybir.dt.bfloat16
FP16 = mybir.dt.float16
I8 = mybir.dt.int8
U8 = mybir.dt.uint8
AF = mybir.ActivationFunctionType
AX = mybir.AxisListType

S = 2048
D = 1024
H = 16
HD = 64
DL = 256
NCH = 8
NT = 16
TB = 512
NTB = S // TB
LAM = 0.01
LN32 = math.log(math.sqrt(D))
EPS = 1e-5

ESIZE = {F32: 4, BF16: 2, FP16: 2, I8: 1, U8: 1}

L1_OFFS = [0, 1, 2, 3, 4, 6, 8, 12, 16, 24, 32, 48, 64, 96,
           128, 192, 256, 384, 512, 768, 1024, 1536]
L1_BAND = 1664
L1_WROW = 1792
RNE_C = 12582912.0

ENGINES = ("sync", "gpsimd", "tensor", "vector", "scalar")

N_STAGES = 4


class Sem:
    def __init__(self, h):
        self.h = h
        self.n = 0


class Buf:
    def __init__(self, parts, fs, base_ap, pitch=None, off=0):
        self.parts = parts
        self.fs = fs
        self.pitch = fs if pitch is None else pitch
        self.off = off
        self._base = base_ap

    def ap(self, p0=0, np_=None, f0=0, nf=None):
        np_ = (self.parts - p0) if np_ is None else np_
        nf = (self.fs - f0) if nf is None else nf
        return type(self._base)(self._base.tensor, p0 * self.pitch + self.off + f0,
                                [[self.pitch, np_], [1, nf]])

    def apd(self, offset, dims):
        return type(self._base)(self._base.tensor, self.off + offset, dims)


class Builder:
    def __init__(self, nc):
        self.nc = nc
        self.prog = {e: [] for e in ENGINES}
        self.es = ExitStack()
        self._i = 0

    def sem(self, name=None):
        h = self.es.enter_context(self.nc.semaphore(name or f"s{self._i}"))
        self._i += 1
        return Sem(h)

    def sbuf(self, name, parts, fs, dt, stack=None):
        h = (stack or self.es).enter_context(self.nc.sbuf_tensor(name, [parts, fs], dt))
        return Buf(parts, fs, h.ap())

    def psum(self, name, parts, fs, dt):
        h = self.es.enter_context(self.nc.psum_tensor(name, [parts, fs], dt))
        return Buf(parts, fs, h.ap())

    def em(self, e, f):
        self.prog[e].append(f)

    def wait(self, e, s, v=None):
        v = s.n if v is None else v
        if v > 0:
            self.em(e, lambda E, h=s.h, v=v: E.wait_ge(h, v))

    def run(self, e, f):
        self.em(e, f)

    def inc(self, e, s, f, k=1):
        s.n += k
        self.em(e, lambda E, f=f, h=s.h, k=k: f(E).then_inc(h, k))

    def barrier(self, pb, dummies=None):
        for e in ENGINES:
            pb.n += 1
            if dummies and e in dummies:
                self.em(e, lambda E, h=pb.h, f=dummies[e]: f(E).then_inc(h, 1))
            else:
                self.em(e, lambda E, h=pb.h: E.nop().then_inc(h, 1))
        tgt = pb.n
        for e in ENGINES:
            self.em(e, lambda E, h=pb.h, v=tgt: E.wait_ge(h, v))

    def finalize(self):
        with self.nc.Block() as block:
            for name in ENGINES:
                dec = getattr(block, name)
                prog = self.prog[name]

                def body(E, prog=prog):
                    for f in prog:
                        f(E)
                dec(body)
        self.es.close()


def dram_ap(t, offset, dims):
    base = t if hasattr(t, "tensor") else t.ap()
    return type(base)(base.tensor, offset, dims)


class Prog:
    def __init__(self, n_stages):
        self.n_stages = n_stages
        nc = bass.Bass("TRN2", target_bir_lowering=False)
        self.nc = nc
        B = Builder(nc)
        self.B = B

        self.x_in = nc.dram_tensor("x", [S, D], F32, kind="ExternalInput").ap()
        self.y_out = nc.dram_tensor("y", [S, D], F32, kind="ExternalOutput").ap()
        self.win = {}
        for nm, shp in [
            ("l0a_wq", [D, D]), ("l0a_wk", [D, D]), ("l0a_wv", [D, D]),
            ("l0a_wo", [D, D]), ("l0a_wdkv", [D, DL]),
            ("l0a_wupk", [DL, D]), ("l0a_wupv", [DL, D]),
            ("l0f_wexp", [D, 4 * D]), ("l0f_wcon", [4 * D, D]),
            ("l0f_bexp", [1, 4 * D]), ("l0f_upb", [1, 4 * D]),
            ("l0f_bcon", [1, D]), ("l0f_dnb", [1, D]),
            ("l0f_upw", [12, 8192]), ("l0f_dnw", [12, 8192]),
        ]:
            self.win[nm] = nc.dram_tensor(nm, shp, F32, kind="ExternalInput").ap()
        self.win["l0f_pmc"] = nc.dram_tensor("l0f_pmc", [128, 7 * 128], BF16,
                                             kind="ExternalInput").ap()
        for nm, shp in [
            ("l1a_wqkv", [D, 3 * D]), ("l1a_bqkv", [1, 3 * D]),
            ("l1a_wout", [D, D]), ("l1a_bout", [1, D]),
            ("l1a_wgate", [D, D]), ("l1a_bgate", [1, D]),
            ("l1a_pb", [44, H]),
        ]:
            self.win[nm] = nc.dram_tensor(nm, shp, F32, kind="ExternalInput").ap()
        self.win["l1a_gsel"] = nc.dram_tensor("l1a_gsel", [44, 23], BF16,
                                              kind="ExternalInput").ap()
        self.win["l1a_stc"] = nc.dram_tensor("l1a_stc", [23, S], BF16,
                                             kind="ExternalInput").ap()
        self.c2scr = nc.dram_tensor("c2scr", [H, S], BF16)
        for nm, shp in [
            ("l1f_wg", [2816, D]), ("l1f_wu", [2816, D]), ("l1f_wd", [D, 2816]),
            ("l1f_clipx", [1, 1]), ("l1f_clipd", [1, 1]),
        ]:
            self.win[nm] = nc.dram_tensor(nm, shp, F32, kind="ExternalInput").ap()
        self.hspill = nc.dram_tensor("hspill", [2816, S], F32)
        self.ccxi = nc.dram_tensor("ccxi", [1, 1], F32)
        self.ccxo = nc.dram_tensor("ccxo", [1, 1], F32)
        self.cchi = nc.dram_tensor("cchi", [1, 1], F32)
        self.ccho = nc.dram_tensor("ccho", [1, 1], F32)
        self.wrowscr = nc.dram_tensor("wrowscr", [H, L1_WROW], BF16)
        self.wscr23 = nc.dram_tensor("wscr23", [23 * 16], BF16)
        self.zscr2 = nc.dram_tensor("zscr2", [H, S], BF16)
        self.rowscr = nc.dram_tensor("rowscr", [8, S], F32)
        self.zscr = nc.dram_tensor("zscr", [2, H * S], BF16)
        self.xspill = nc.dram_tensor("xspill", [NCH * 128, S], F32)
        self.xspill2 = nc.dram_tensor("xspill2", [NCH * 128, S], F32)
        self.wescr = nc.dram_tensor("wescr", [8, 128 * 8 * 512], BF16)
        self.gcscr = nc.dram_tensor("gcscr", [128, 32 * 512], BF16)
        self.wcscr = nc.dram_tensor("wcscr", [8, 128 * 32 * 128], BF16)

        self.PB = B.sem("pb")
        self.ld = B.sem("ld")
        self.gl = B.sem("gl")
        self.mm = B.sem("mm")
        self.ev = B.sem("ev")
        self.vc = B.sem("vc")
        self.sc = B.sem("sc")
        self.gc = B.sem("gc")
        self.xs = B.sem("xs")
        self.pv = B.sem("pv")
        self.st = B.sem("st")
        self.lda = B.sem("lda")
        self.ldb = B.sem("ldb")
        self.gb = B.sem("gb")
        self.sta = B.sem("sta")
        self.stb = B.sem("stb")
        self.gwa = B.sem("gwa")
        self.gwb = B.sem("gwb")
        self.ccs = B.sem("ccs")
        self.wsa = B.sem("wsa")
        self.wsb = B.sem("wsb")

        self.xbf = B.sbuf("xbf", 128, NCH * S, BF16)
        self.eye32 = B.sbuf("eye32", 128, 128, F32)
        self.maskU = B.sbuf("masku", 128, 128, U8)
        self.pidx = B.sbuf("pidx", 128, 1, F32)
        self.fidx = B.sbuf("fidx", 128, 128, F32)
        self.bias1 = B.sbuf("bias1", 1, 4, F32)
        self.ones128 = B.sbuf("ones128", 128, 1, BF16)
        self.dume = B.sbuf("dume", 1, 4, F32)
        self.ta = B.sbuf("ta", 32, 256, F32)

        self.ps_a = B.psum("ps_a", 128, 2048, F32)
        self.ps_scr = B.psum("ps_scr", 128, 512, F32)
        self.ps_o1 = B.psum("ps_o1", 128, 512, F32)
        self.ps_o2 = B.psum("ps_o2", 128, 512, F32)
        self.ps_s = B.psum("ps_s", 128, 512, F32)
        self.pa4 = [self.ps_a.ap(0, 128, b * 512, 512) for b in range(4)]
        self._lni = 0
        self._xfi = 0
        self.bank_ev = {}

    def carve(self, byte_off, parts, nf, dt):
        bc = self.xf._base.bitcast(dt)
        row = bc.ap[0][0]
        return Buf(parts, nf, bc, pitch=row, off=byte_off // ESIZE[dt])

    def alloc_xf(self):
        self.s_xf = ExitStack()
        self.xf = self.B.sbuf(f"xf{self._xfi}", 128, NCH * S, F32, stack=self.s_xf)
        self._xfi += 1

    def barrier(self):
        B = self.B
        dummies = {
            "tensor": lambda E: E.matmul(self.ps_s.ap(0, 1, 0, 1),
                                         self.ones128.ap(0, 1, 0, 1),
                                         self.ones128.ap(0, 1, 0, 1), start=True, stop=True),
            "vector": lambda E: E.memset(self.dume.ap(0, 1, 0, 1), 0.0),
            "scalar": lambda E: E.activation(self.dume.ap(0, 1, 1, 1), self.dume.ap(0, 1, 1, 1), AF.Copy),
            "gpsimd": lambda E: E.memset(self.dume.ap(0, 1, 2, 1), 0.0),
        }
        B.wait("gpsimd", self.gc, self.gc.n)
        B.wait("gpsimd", self.gl, self.gl.n)
        B.wait("gpsimd", self.gb, self.gb.n)
        B.wait("gpsimd", self.st, self.st.n)
        B.wait("gpsimd", self.sta, self.sta.n)
        B.wait("gpsimd", self.stb, self.stb.n)
        B.wait("gpsimd", self.gwa, self.gwa.n)
        B.wait("gpsimd", self.gwb, self.gwb.n)
        B.wait("sync", self.ld, self.ld.n)
        B.wait("sync", self.lda, self.lda.n)
        B.wait("sync", self.ldb, self.ldb.n)
        B.wait("sync", self.wsa, self.wsa.n)
        B.wait("sync", self.wsb, self.wsb.n)
        B.barrier(self.PB, dummies)

    def mm_pipe(self, groups, psum_aps, evacs, evac_engine="scalar"):
        B = self.B
        NB = len(psum_aps)
        B.wait("tensor", self.ev, self.ev.n)
        for g, pairs in enumerate(groups):
            pt = psum_aps[g % NB]
            bank = pt.offset // 512
            prev = self.bank_ev.get(bank)
            if prev:
                B.wait("tensor", self.ev, prev)
            n = len(pairs)
            for i, (lh, rh) in enumerate(pairs):
                st_ = (i == 0)
                if i == n - 1:
                    B.inc("tensor", self.mm,
                          lambda E, lh=lh, rh=rh, pt=pt, st_=st_: E.matmul(pt, lh, rh, start=st_, stop=True))
                else:
                    B.run("tensor",
                          lambda E, lh=lh, rh=rh, pt=pt, st_=st_: E.matmul(pt, lh, rh, start=st_, stop=False))
            B.wait(evac_engine, self.mm)
            fns = evacs[g]
            for j, f in enumerate(fns):
                if j == len(fns) - 1:
                    B.inc(evac_engine, self.ev, lambda E, f=f, pt=pt: f(E, pt))
                else:
                    B.run(evac_engine, lambda E, f=f, pt=pt: f(E, pt))
            self.bank_ev[bank] = self.ev.n

    def cast_load_w(self, dst, dram, rows, cols):
        B = self.B
        for c in range(rows // 128):
            B.inc("gpsimd", self.gl, lambda E, c=c, dst=dst, dram=dram, cols=cols: E.dma_start(
                out=dst.ap(0, 128, c * cols, cols),
                in_=dram_ap(dram, c * 128 * cols, [[cols, 128], [1, cols]])), 16)

    def row_bcast(self, dram_off, n_elems, dst):
        B = self.B
        B.inc("sync", self.ld, lambda E, dram_off=dram_off, n_elems=n_elems, dst=dst: E.dma_start(
            out=dst.ap(0, 128, 0, n_elems),
            in_=dram_ap(self.rowscr, dram_off, [[0, 128], [1, n_elems]])), 16)

    def phase0(self):
        B = self.B
        s_p0 = ExitStack()
        xtok = [B.sbuf(f"xtok{i}", 128, D, F32, stack=s_p0) for i in range(2)]
        B.inc("gpsimd", self.gc, lambda E: E.iota(self.pidx.ap(), [[0, 1]], base=0,
                                         channel_multiplier=1, allow_small_or_imprecise_dtypes=True))
        B.inc("gpsimd", self.gc, lambda E: E.iota(self.fidx.ap(), [[1, 128]], base=0,
                                         channel_multiplier=0, allow_small_or_imprecise_dtypes=True))
        B.wait("vector", self.gc, 2)
        B.run("vector", lambda E: E.tensor_scalar(self.eye32.ap(), self.fidx.ap(), self.pidx.ap(), None, OP.is_equal))
        B.run("vector", lambda E: E.tensor_scalar(self.maskU.ap(), self.fidx.ap(), self.pidx.ap(), None, OP.is_lt))
        B.run("vector", lambda E: E.memset(self.dume.ap(0, 1, 1, 1), 0.0))
        B.run("vector", lambda E: E.memset(self.bias1.ap(0, 1, 0, 1), -LN32))
        B.run("vector", lambda E: E.memset(self.bias1.ap(0, 1, 1, 1), EPS))
        B.inc("vector", self.vc, lambda E: E.memset(self.ones128.ap(), 1.0))

        for t in range(NT):
            buf = xtok[t % 2]
            par = self.lda if t % 2 == 0 else self.ldb
            if t >= 2:
                B.wait("sync", self.vc, self.vc.n)
            B.inc("sync", par, lambda E, buf=buf, t=t: E.dma_start(
                out=buf.ap(), in_=dram_ap(self.x_in, t * 128 * D, [[D, 128], [1, D]])), 16)
            B.wait("tensor", par, par.n)
            if t == 0:
                B.wait("tensor", self.vc, 1)
            for hf in range(2):
                if t > 0 or hf > 0:
                    B.wait("tensor", self.vc, self.vc.n)
                for j in range(4):
                    fn = lambda E, buf=buf, hf=hf, j=j: E.transpose(
                        self.ps_a.ap(0, 128, hf * 1024 + j * 128, 128),
                        buf.ap(0, 128, (hf * 4 + j) * 128, 128), self.eye32.ap())
                    if j == 3:
                        B.inc("tensor", self.mm, fn)
                    else:
                        B.run("tensor", fn)
                B.wait("vector", self.mm, self.mm.n)

                def evx(E, t=t, hf=hf):
                    srcap = self.ps_a.apd(hf * 1024, [[2048, 128], [128, 4], [1, 128]])
                    dstap = self.xf.apd(4 * hf * S + t * 128, [[NCH * S, 128], [S, 4], [1, 128]])
                    return E.tensor_copy(dstap, srcap)
                B.inc("vector", self.vc, evx)
        B.wait("scalar", self.vc, self.vc.n)
        for c in range(NCH):
            B.inc("scalar", self.sc, lambda E, c=c: E.activation(
                self.xbf.ap(0, 128, c * S, S), self.xf.ap(0, 128, c * S, S), AF.Copy))
        self.barrier()
        s_p0.close()

    def ln_phase(self):
        B = self.B
        self._lni += 1
        s_ln = ExitStack()
        sqb = B.sbuf(f"sqb{self._lni}", 128, S, BF16, stack=s_ln)
        repa = B.sbuf(f"repa{self._lni}", 128, S, F32, stack=s_ln)
        repb = B.sbuf(f"repb{self._lni}", 128, S, F32, stack=s_ln)
        srow = B.sbuf(f"srow{self._lni}", 1, 4 * S, F32, stack=s_ln)
        for c in range(NCH):
            if c > 0:
                B.wait("scalar", self.mm, self.mm.n)
            B.inc("scalar", self.sc, lambda E, c=c: E.activation(
                self.xbf.ap(0, 128, c * S, S), self.xf.ap(0, 128, c * S, S), AF.Copy))
            B.inc("scalar", self.sc, lambda E, c=c, sqb=sqb: E.activation(
                sqb.ap(), self.xf.ap(0, 128, c * S, S), AF.Square))
            B.wait("tensor", self.sc, self.sc.n)
            for tb in range(NTB):
                B.run("tensor", lambda E, c=c, tb=tb: E.matmul(
                    self.ps_a.ap(0, 1, tb * TB, TB), self.ones128.ap(),
                    self.xbf.ap(0, 128, c * S + tb * TB, TB),
                    start=(c == 0), stop=(c == NCH - 1)))
                fn = lambda E, c=c, tb=tb, sqb=sqb: E.matmul(
                    self.ps_a.ap(32, 1, tb * TB, TB), self.ones128.ap(),
                    sqb.ap(0, 128, tb * TB, TB),
                    start=(c == 0), stop=(c == NCH - 1))
                if tb == NTB - 1:
                    B.inc("tensor", self.mm, fn)
                else:
                    B.run("tensor", fn)
        sr = srow
        B.wait("vector", self.mm, self.mm.n)
        B.inc("vector", self.vc, lambda E, sr=sr: E.tensor_scalar(
            sr.ap(0, 1, 2 * S, S), self.ps_a.apd(0, [[2048, 1], [512, 4], [1, 512]]),
            1.0 / D, None, OP.mult))
        B.inc("vector", self.vc, lambda E, sr=sr: E.tensor_scalar(
            sr.ap(0, 1, 3 * S, S), self.ps_a.apd(32 * 2048, [[2048, 1], [512, 4], [1, 512]]),
            1.0 / D, None, OP.mult))
        B.wait("vector", self.vc, self.vc.n)
        B.inc("vector", self.vc, lambda E, sr=sr: E.tensor_tensor(
            sr.ap(0, 1, 0, S), sr.ap(0, 1, 2 * S, S), sr.ap(0, 1, 2 * S, S), OP.mult))
        B.wait("vector", self.vc, self.vc.n)
        B.inc("vector", self.vc, lambda E, sr=sr: E.tensor_tensor(
            sr.ap(0, 1, 3 * S, S), sr.ap(0, 1, 3 * S, S), sr.ap(0, 1, 0, S), OP.subtract))
        B.wait("scalar", self.vc, self.vc.n)
        B.inc("scalar", self.sc, lambda E, sr=sr: E.activation(
            sr.ap(0, 1, 0, S), sr.ap(0, 1, 3 * S, S), AF.Sqrt,
            bias=self.bias1.ap(0, 1, 1, 1)))
        B.wait("vector", self.sc, self.sc.n)
        B.inc("vector", self.vc, lambda E, sr=sr: E.reciprocal(sr.ap(0, 1, 0, S), sr.ap(0, 1, 0, S)))
        B.wait("vector", self.vc, self.vc.n)
        B.inc("vector", self.vc, lambda E, sr=sr: E.tensor_tensor(
            sr.ap(0, 1, 1 * S, S), sr.ap(0, 1, 2 * S, S), sr.ap(0, 1, 0, S), OP.mult))
        B.wait("vector", self.vc, self.vc.n)
        B.inc("vector", self.vc, lambda E, sr=sr: E.tensor_scalar(
            sr.ap(0, 1, 1 * S, S), sr.ap(0, 1, 1 * S, S), -1.0, None, OP.mult))
        B.wait("gpsimd", self.vc, self.vc.n)
        B.inc("gpsimd", self.gb, lambda E, sr=sr: E.dma_start(
            out=dram_ap(self.rowscr, 0, [[1, 2 * S]]), in_=sr.ap(0, 1, 0, 2 * S)), 16)
        B.wait("sync", self.gb, self.gb.n)
        self.row_bcast(0, S, repa)
        self.row_bcast(S, S, repb)
        B.wait("vector", self.ld, self.ld.n)
        for c in range(NCH):
            B.inc("vector", self.vc, lambda E, c=c, repa=repa: E.tensor_tensor(
                self.xf.ap(0, 128, c * S, S), self.xf.ap(0, 128, c * S, S),
                repa.ap(), OP.mult))
            B.wait("vector", self.vc, self.vc.n)
            B.inc("vector", self.vc, lambda E, c=c, repb=repb: E.tensor_tensor(
                self.xf.ap(0, 128, c * S, S), self.xf.ap(0, 128, c * S, S),
                repb.ap(), OP.add))
            B.wait("scalar", self.vc, self.vc.n)
            B.inc("scalar", self.sc, lambda E, c=c: E.activation(
                self.xbf.ap(0, 128, c * S, S), self.xf.ap(0, 128, c * S, S), AF.Copy))
        self.barrier()
        s_ln.close()

    def l0_attn(self):
        B = self.B
        s_ot = ExitStack()
        OTu = B.sbuf("otu", 128, NCH * S, BF16, stack=s_ot)
        s1 = ExitStack()
        wupk = B.sbuf("wupk", 128, 2 * D, BF16, stack=s1)
        wupv = B.sbuf("wupv", 128, 2 * D, BF16, stack=s1)
        etile = [B.sbuf(f"etile{i}", 128, S, BF16, stack=s1) for i in range(4)]
        latT = B.sbuf("latT", 128, 2 * S, BF16, stack=s1)
        self.cast_load_w(wupk, self.win["l0a_wupk"], DL, D)
        self.cast_load_w(wupv, self.win["l0a_wupv"], DL, D)

        s_ef = ExitStack()
        irow = B.sbuf("irow", 1, S, F32, stack=s_ef)
        erow = B.sbuf("erow", 1, S, F32, stack=s_ef)
        etf = B.sbuf("etf", 128, S, F32, stack=s_ef)
        B.inc("gpsimd", self.gc, lambda E, irow=irow: E.iota(irow.ap(0, 1, 0, S), [[1, S]], base=0,
                                                    channel_multiplier=0, allow_small_or_imprecise_dtypes=True))
        B.wait("scalar", self.vc, 1)
        B.wait("scalar", self.gc, self.gc.n)
        for i, (sgn, use_b) in enumerate([(+1, True), (-1, True), (+1, False), (-1, False)]):
            if i > 0:
                B.wait("scalar", self.gb, self.gb.n)
            if use_b:
                B.inc("scalar", self.sc, lambda E, sgn=sgn, erow=erow: E.activation(
                    erow.ap(0, 1, 0, S), irow.ap(0, 1, 0, S), AF.Exp,
                    bias=self.bias1.ap(0, 1, 0, 1), scale=sgn * LAM))
            else:
                B.inc("scalar", self.sc, lambda E, sgn=sgn, erow=erow: E.activation(
                    erow.ap(0, 1, 0, S), irow.ap(0, 1, 0, S), AF.Exp, scale=sgn * LAM))
            B.wait("gpsimd", self.sc, self.sc.n)
            B.inc("gpsimd", self.gb, lambda E, i=i, erow=erow: E.dma_start(
                out=dram_ap(self.rowscr, 2 * S + i * S, [[1, S]]), in_=erow.ap(0, 1, 0, S)), 16)
        B.wait("sync", self.gb, self.gb.n)
        for i in range(4):
            self.row_bcast(2 * S + i * S, S, etf)
            B.wait("vector", self.ld, self.ld.n)
            B.inc("vector", self.vc, lambda E, i=i, etile=etile, etf=etf: E.tensor_copy(
                etile[i].ap(), etf.ap()))
            B.wait("sync", self.vc, self.vc.n)
        self.barrier()
        s_ef.close()

        s_w1 = ExitStack()
        wk = B.sbuf("wk", 128, NCH * D, BF16, stack=s_w1)
        wv = B.sbuf("wv", 128, NCH * D, BF16, stack=s_w1)
        wdkv = B.sbuf("wdkv", 128, NCH * DL, BF16, stack=s_w1)
        self.cast_load_w(wk, self.win["l0a_wk"], D, D)
        self.cast_load_w(wv, self.win["l0a_wv"], D, D)
        self.cast_load_w(wdkv, self.win["l0a_wdkv"], D, DL)
        s_kv = ExitStack()
        QTB = 256
        kQ = B.sbuf("kQ", 128, NCH * QTB, BF16, stack=s_kv)
        vQ = B.sbuf("vQ", 128, NCH * QTB, BF16, stack=s_kv)
        B.wait("tensor", self.gl, self.gl.n)
        for tq in range(S // QTB):
            tok0 = tq * QTB
            for dstbuf, wbuf in ((kQ, wk), (vQ, wv)):
                groups, evacs = [], []
                for m in range(NCH):
                    groups.append([(wbuf.ap(0, 128, c * D + m * 128, 128),
                                    self.xbf.ap(0, 128, c * S + tok0, QTB)) for c in range(NCH)])
                    evacs.append([lambda E, pt, dstbuf=dstbuf, m=m: E.activation(
                        dstbuf.ap(0, 128, m * QTB, QTB), pt, AF.Copy)])
                self.mm_pipe(groups, [self.ps_a.ap(0, 128, b * 512, QTB) for b in range(4)], evacs)
            groups, evacs = [], []
            for m in range(2):
                pairs = [(wdkv.ap(0, 128, c * DL + m * 128, 128),
                          kQ.ap(0, 128, c * QTB, QTB)) for c in range(NCH)]
                pairs += [(wdkv.ap(0, 128, c * DL + m * 128, 128),
                           vQ.ap(0, 128, c * QTB, QTB)) for c in range(NCH)]
                groups.append(pairs)
                evacs.append([lambda E, pt, m=m, tok0=tok0, latT=latT: E.activation(
                    latT.ap(0, 128, m * S + tok0, QTB), pt, AF.Copy)])
            self.mm_pipe(groups, [self.ps_a.ap(0, 128, b * 512, QTB) for b in range(4)], evacs)
        self.barrier()
        s_kv.close()
        s_w1.close()

        s_qT = ExitStack()
        qT = B.sbuf("qT", 128, 4 * S, BF16, stack=s_qT)
        wq = B.sbuf("wq", 128, 4 * D, BF16, stack=s_qT)
        zrow = B.sbuf("zrowr", 65, 1024, BF16, stack=s_qT)

        for c in range(NCH):
            B.inc("gpsimd", self.gl, lambda E, c=c: E.dma_start(
                out=dram_ap(self.xspill, c * 128 * S, [[S, 128], [1, S]]),
                in_=self.xf.ap(0, 128, c * S, S)), 16)
        B.wait("gpsimd", self.gl, self.gl.n)
        B.inc("gpsimd", self.gc, lambda E: E.nop())
        for e in ENGINES:
            B.wait(e, self.gc, self.gc.n)

        KB = 1024
        krecT = self.carve(0, 128, 4 * S, BF16)
        vaug = self.carve(16 * KB, 128, NT * KB, BF16)
        k1c = self.carve(48 * KB, 128, S, BF16)
        k2c = self.carve(52 * KB, 128, S, BF16)
        q1b = self.carve(56 * KB, 128, TB, BF16)
        q2b = self.carve(57 * KB, 128, TB, BF16)
        ptile = [self.carve(58 * KB, 128, KB, BF16), self.carve(60 * KB, 128, KB, BF16)]

        for t in range(NT):
            B.run("vector", lambda E, t=t, vaug=vaug: E.memset(
                vaug.apd(t * KB + 64, [[vaug.pitch, 128], [256, 4], [1, 64]]), 1.0))
            fn = lambda E, t=t, vaug=vaug: E.memset(
                vaug.apd(t * KB + 128, [[vaug.pitch, 128], [256, 4], [1, 64]]), 1.0)
            if t == NT - 1:
                B.inc("vector", self.vc, fn)
            else:
                B.run("vector", fn)

        for half in range(2):
            B.wait("gpsimd", self.mm, self.mm.n)
            for c in range(NCH):
                B.inc("gpsimd", self.gl, lambda E, c=c, half=half, wq=wq: E.dma_start(
                    out=wq.ap(0, 128, c * 512, 512),
                    in_=dram_ap(self.win["l0a_wq"], c * 128 * D + half * 512, [[D, 128], [1, 512]])), 16)
            B.wait("tensor", self.gl, self.gl.n)
            groups, evacs = [], []
            for m in range(4):
                for tb in range(NTB):
                    groups.append([(wq.ap(0, 128, c * 512 + m * 128, 128),
                                    self.xbf.ap(0, 128, c * S + tb * TB, TB)) for c in range(NCH)])
                    evacs.append([lambda E, pt, m=m, tb=tb, qT=qT: E.activation(
                        qT.ap(0, 128, m * S + tb * TB, TB), pt, AF.Copy)])
            self.mm_pipe(groups, self.pa4, evacs)
            groups, evacs = [], []
            for m in range(4):
                C = 4 * half + m
                for tb in range(NTB):
                    groups.append([(wupk.ap(0, 128, dl * D + C * 128, 128),
                                    latT.ap(0, 128, dl * S + tb * TB, TB)) for dl in range(2)])
                    evacs.append([lambda E, pt, m=m, tb=tb, krecT=krecT: E.activation(
                        krecT.ap(0, 128, m * S + tb * TB, TB), pt, AF.Copy)])
            self.mm_pipe(groups, self.pa4, evacs)

            B.wait("tensor", self.vc, self.vc.n)
            groups, evacs = [], []
            for t in range(NT):
                groups.append([(latT.ap(0, 128, dl * S + t * 128, 128),
                                wupv.ap(0, 128, dl * D + half * 512, 512)) for dl in range(2)])

                def ev_even(E, pt, t=t, vaug=vaug):
                    src = type(pt)(pt.tensor, pt.offset, [[2048, 128], [128, 4], [1, 64]])
                    dst = vaug.apd(t * KB, [[vaug.pitch, 128], [256, 4], [1, 64]])
                    return E.tensor_copy(dst, src)

                def ev_odd(E, pt, t=t, vaug=vaug):
                    src = type(pt)(pt.tensor, pt.offset + 64, [[2048, 128], [128, 4], [1, 64]])
                    dst = vaug.apd(t * KB + 192, [[vaug.pitch, 128], [256, 4], [1, 64]])
                    return E.tensor_copy(dst, src)
                evacs.append([ev_even, ev_odd])
            self.mm_pipe(groups, self.pa4, evacs, evac_engine="vector")

            for mc in range(4):
                C = 4 * half + mc
                B.wait("vector", self.ev, self.ev.n)
                B.wait("vector", self.pv, self.pv.n)
                B.run("vector", lambda E, mc=mc, k1c=k1c, krecT=krecT, etile=etile: E.tensor_tensor(
                    k1c.ap(), krecT.ap(0, 128, mc * S, S), etile[0].ap(), OP.mult))
                B.inc("vector", self.vc, lambda E, mc=mc, k2c=k2c, krecT=krecT, etile=etile: E.tensor_tensor(
                    k2c.ap(), krecT.ap(0, 128, mc * S, S), etile[1].ap(), OP.mult))
                for h2 in range(2):
                    hg = 2 * C + h2
                    pr = 64 * h2
                    hh = hg & 7
                    for qb in range(NTB):
                        B.wait("vector", self.pv, self.pv.n)
                        B.run("vector", lambda E, pr=pr, mc=mc, qb=qb, q1b=q1b, qT=qT, etile=etile: E.tensor_tensor(
                            q1b.ap(pr, 64, 0, TB), qT.ap(pr, 64, mc * S + qb * TB, TB),
                            etile[3].ap(pr, 64, qb * TB, TB), OP.mult))
                        B.inc("vector", self.vc, lambda E, pr=pr, mc=mc, qb=qb, q2b=q2b, qT=qT, etile=etile: E.tensor_tensor(
                            q2b.ap(pr, 64, 0, TB), qT.ap(pr, 64, mc * S + qb * TB, TB),
                            etile[2].ap(pr, 64, qb * TB, TB), OP.mult))
                        B.wait("tensor", self.vc, self.vc.n)
                        xs_base = self.xs.n
                        pv_base = self.pv.n
                        for g in range(8):
                            hf = g % 2
                            if g >= 2:
                                B.wait("tensor", self.xs, xs_base + (g - 1))
                            diag_slots = []
                            mmq = []
                            for kk in range(2):
                                kc = 2 * g + kk
                                kcr = kc - 4 * qb
                                pslice = self.ps_a.ap(0, 128, hf * 1024 + kk * 512, 512)
                                if kcr < 0:
                                    mmq.append((None, lambda E, pr=pr, kc=kc, pslice=pslice, k1c=k1c, q1b=q1b: E.matmul(
                                        pslice, k1c.ap(pr, 64, kc * 128, 128), q1b.ap(pr, 64, 0, TB),
                                        start=True, stop=True)))
                                elif kcr > 3:
                                    mmq.append((None, lambda E, pr=pr, kc=kc, pslice=pslice, k2c=k2c, q2b=q2b: E.matmul(
                                        pslice, k2c.ap(pr, 64, kc * 128, 128), q2b.ap(pr, 64, 0, TB),
                                        start=True, stop=True)))
                                else:
                                    for j in range(4):
                                        psub = self.ps_a.ap(0, 128, hf * 1024 + kk * 512 + j * 128, 128)
                                        if kcr < j:
                                            mmq.append((None, lambda E, pr=pr, kc=kc, j=j, psub=psub, k1c=k1c, q1b=q1b: E.matmul(
                                                psub, k1c.ap(pr, 64, kc * 128, 128),
                                                q1b.ap(pr, 64, j * 128, 128), start=True, stop=True)))
                                        elif kcr > j:
                                            mmq.append((None, lambda E, pr=pr, kc=kc, j=j, psub=psub, k2c=k2c, q2b=q2b: E.matmul(
                                                psub, k2c.ap(pr, 64, kc * 128, 128),
                                                q2b.ap(pr, 64, j * 128, 128), start=True, stop=True)))
                                        else:
                                            nds = len(diag_slots)
                                            mmq.append(("dwait", None))
                                            mmq.append((None, lambda E, pr=pr, kc=kc, j=j, psub=psub, k1c=k1c, q1b=q1b: E.matmul(
                                                psub, k1c.ap(pr, 64, kc * 128, 128),
                                                q1b.ap(pr, 64, j * 128, 128), start=True, stop=True)))
                                            mmq.append((None, lambda E, pr=pr, kc=kc, j=j, nds=nds, k2c=k2c, q2b=q2b: E.matmul(
                                                self.ps_scr.ap(0, 128, nds * 128, 128),
                                                k2c.ap(pr, 64, kc * 128, 128),
                                                q2b.ap(pr, 64, j * 128, 128), start=True, stop=True)))
                                            diag_slots.append(psub)
                            last_i = max(i for i, (tag, f) in enumerate(mmq) if f is not None)
                            for i, (tag, f) in enumerate(mmq):
                                if tag == "dwait":
                                    if self.vc.n > 0:
                                        B.wait("tensor", self.vc, self.vc.n)
                                elif i == last_i:
                                    B.inc("tensor", self.mm, f)
                                else:
                                    B.run("tensor", f)
                            if diag_slots:
                                B.wait("vector", self.mm, self.mm.n)
                                for nds, psub in enumerate(diag_slots):
                                    B.inc("vector", self.vc, lambda E, nds=nds, psub=psub: E.copy_predicated(
                                        psub, self.maskU.ap(), self.ps_scr.ap(0, 128, nds * 128, 128)))
                                B.wait("scalar", self.vc, self.vc.n)
                            else:
                                B.wait("scalar", self.mm, self.mm.n)
                            if g >= 2:
                                B.wait("scalar", self.pv, pv_base + (g - 1))
                            B.inc("scalar", self.xs, lambda E, hf=hf, ptile=ptile: E.activation(
                                ptile[hf].ap(), self.ps_a.ap(0, 128, hf * 1024, 1024), AF.Exp))
                            B.wait("tensor", self.xs, self.xs.n)
                            for kk in range(2):
                                kc = 2 * g + kk
                                pso = self.ps_o1 if h2 == 0 else self.ps_o2
                                if h2 == 0:
                                    lh = vaug.ap(0, 128, kc * KB + hh * 128, 65)
                                else:
                                    lh = vaug.ap(0, 128, kc * KB + hh * 128, 128)
                                nout = 65 if h2 == 0 else 128
                                if kk == 1:
                                    B.inc("tensor", self.pv, lambda E, lh=lh, hf=hf, kk=kk, pso=pso, kc=kc, nout=nout, ptile=ptile: E.matmul(
                                        pso.ap(0, nout, 0, 512), lh, ptile[hf].ap(0, 128, kk * 512, 512),
                                        start=(kc == 0), stop=(kc == 15)))
                                else:
                                    B.run("tensor", lambda E, lh=lh, hf=hf, kk=kk, pso=pso, kc=kc, nout=nout, ptile=ptile: E.matmul(
                                        pso.ap(0, nout, 0, 512), lh, ptile[hf].ap(0, 128, kk * 512, 512),
                                        start=(kc == 0), stop=(kc == 15)))
                        pso = self.ps_o1 if h2 == 0 else self.ps_o2
                        zslot = qb % 2
                        zpar = self.sta if zslot == 0 else self.stb
                        B.wait("vector", self.pv, self.pv.n)
                        B.wait("vector", zpar, zpar.n)
                        zr = 64 if h2 == 0 else 0
                        B.inc("vector", self.vc, lambda E, pso=pso, zslot=zslot, zr=zr, zrow=zrow: E.tensor_copy(
                            zrow.ap(zr, 1, zslot * 512, 512), pso.ap(zr, 1, 0, 512)))
                        B.wait("scalar", self.pv, self.pv.n)
                        if h2 == 0:
                            B.inc("scalar", self.sc, lambda E, pso=pso, C=C, qb=qb, OTu=OTu: E.activation(
                                OTu.ap(0, 64, C * S + qb * TB, TB), pso.ap(0, 64, 0, 512), AF.Copy))
                            zoff = hg // 2 * S + qb * 512
                        else:
                            B.inc("scalar", self.sc, lambda E, pso=pso, C=C, qb=qb, OTu=OTu: E.activation(
                                OTu.ap(64, 64, C * S + qb * TB, TB), pso.ap(64, 64, 0, 512), AF.Copy))
                            zoff = H * S + hg // 2 * S + qb * 512
                        B.wait("gpsimd", self.vc, self.vc.n)
                        B.inc("gpsimd", zpar, lambda E, zr=zr, zslot=zslot, zoff=zoff, zrow=zrow: E.dma_start(
                            out=dram_ap(self.zscr, zoff, [[1, 512]]),
                            in_=zrow.ap(zr, 1, zslot * 512, 512)), 16)
                        B.wait("tensor", self.sc, self.sc.n)
                        B.wait("tensor", self.vc, self.vc.n)
        self.barrier()
        s_qT.close()
        s1.close()

        s_n = ExitStack()
        zrep = B.sbuf("zrep", 128, S, BF16, stack=s_n)
        rzt = B.sbuf("rzt", 128, S, F32, stack=s_n)
        wo = B.sbuf("wo", 128, NCH * D, BF16, stack=s_n)
        B.wait("sync", self.sta, self.sta.n)
        B.wait("sync", self.stb, self.stb.n)
        for C in range(NCH):
            B.inc("sync", self.ld, lambda E, C=C, zrep=zrep: E.dma_start(
                out=zrep.ap(), in_=dram_ap(self.zscr, C * S, [[H * S, 2], [0, 64], [1, S]])), 16)
            B.wait("vector", self.ld, self.ld.n)
            B.inc("vector", self.vc, lambda E, zrep=zrep, rzt=rzt: E.reciprocal(rzt.ap(), zrep.ap()))
            B.wait("vector", self.vc, self.vc.n)
            B.inc("vector", self.vc, lambda E, C=C, OTu=OTu, rzt=rzt: E.tensor_tensor(
                OTu.ap(0, 128, C * S, S), OTu.ap(0, 128, C * S, S), rzt.ap(), OP.mult))
            B.wait("sync", self.vc, self.vc.n)

        B.wait("gpsimd", self.vc, self.vc.n)
        for c in range(NCH):
            B.inc("gpsimd", self.gl, lambda E, c=c: E.dma_start(
                out=self.xf.ap(0, 128, c * S, S),
                in_=dram_ap(self.xspill, c * 128 * S, [[S, 128], [1, S]])), 16)
        self.cast_load_w(wo, self.win["l0a_wo"], D, D)
        B.wait("tensor", self.gl, self.gl.n)
        B.wait("tensor", self.vc, self.vc.n)
        groups, evacs = [], []
        for m in range(NCH):
            for tb in range(NTB):
                groups.append([(wo.ap(0, 128, c * D + m * 128, 128),
                                OTu.ap(0, 128, c * S + tb * TB, TB)) for c in range(NCH)])
                evacs.append([lambda E, pt, m=m, tb=tb: E.tensor_tensor(
                    self.xf.ap(0, 128, m * S + tb * TB, TB), pt,
                    self.xf.ap(0, 128, m * S + tb * TB, TB), OP.add)])
        self.mm_pipe(groups, self.pa4, evacs, evac_engine="vector")
        self.barrier()
        s_n.close()
        s_ot.close()

        self.ln_phase()


    def l0_ffn(self):
        B = self.B
        FD = 4 * D
        NFC = 32
        QT = 512
        for c in range(NCH):
            B.inc("gpsimd", self.gl, lambda E, c=c: E.dma_start(
                out=dram_ap(self.xspill, c * 128 * S, [[S, 128], [1, S]]),
                in_=self.xf.ap(0, 128, c * S, S)), 16)
        B.wait("gpsimd", self.gl, self.gl.n)
        B.inc("gpsimd", self.gc, lambda E: E.nop())
        for e in ENGINES:
            B.wait(e, self.gc, self.gc.n)
        self.s_xf.close()

        s_f = ExitStack()
        bsets = [B.sbuf(f"bset{i}", 128, NFC * 128, BF16, stack=s_f) for i in range(4)]
        pmats = B.sbuf("pmats", 128, 7 * 128, BF16, stack=s_f)
        eyebf = B.sbuf("eyebf", 128, 128, BF16, stack=s_f)
        vab = B.sbuf("vab", 128, 64, BF16, stack=s_f)
        wsl = B.sbuf("wsl", 32, 512, F32, stack=s_f)
        bexpL1 = B.sbuf("bexpL1", 128, 32, F32, stack=s_f)
        upbL2 = B.sbuf("upbL2", 128, 32, F32, stack=s_f)
        b2t = B.sbuf("b2t", 128, 16, F32, stack=s_f)
        pcol = B.sbuf("pcol", 128, 4, F32, stack=s_f)
        xstage = B.sbuf("xstage", 128, 2 * QT, F32, stack=s_f)
        rstage = B.sbuf("rstage", 128, 2 * QT, F32, stack=s_f)
        tstage = B.sbuf("tstage", 128, QT, F32, stack=s_f)
        s_fb = ExitStack()
        tsc = B.sbuf("tsc", 128, NFC * 128, BF16, stack=s_fb)
        da_all = B.sbuf("da_all", 128, NFC * 128, BF16, stack=s_fb)
        db_all = B.sbuf("db_all", 128, NFC * 128, BF16, stack=s_fb)
        ws2 = B.sbuf("ws2", 32, 8068, F32, stack=s_fb)

        STRIDES = [1, 2, 4, 8, 16, 32, 64]
        PIDX = {st: i for i, st in enumerate(STRIDES)}

        B.inc("sync", self.ld, lambda E: E.dma_start(
            out=wsl.ap(0, 32, 0, 128), in_=dram_ap(self.win["l0f_bexp"], 0, [[128, 32], [1, 128]])), 16)
        B.inc("sync", self.ld, lambda E: E.dma_start(
            out=upbL2.ap(0, 128, 0, 32), in_=dram_ap(self.win["l0f_upb"], 0, [[32, 128], [1, 32]])), 16)
        B.inc("sync", self.ld, lambda E: E.dma_start(
            out=wsl.ap(0, 8, 128, 128), in_=dram_ap(self.win["l0f_bcon"], 0, [[128, 8], [1, 128]])), 16)
        B.inc("sync", self.ld, lambda E: E.dma_start(
            out=wsl.ap(0, 8, 256, 128), in_=dram_ap(self.win["l0f_dnb"], 0, [[128, 8], [1, 128]])), 16)
        B.wait("tensor", self.ld, self.ld.n)
        B.wait("tensor", self.ev, self.ev.n)
        B.run("tensor", lambda E: E.transpose(
            self.ps_scr.ap(0, 128, 0, 32), wsl.ap(0, 32, 0, 128), self.eye32.ap(0, 32, 0, 32)))
        B.run("tensor", lambda E: E.transpose(
            self.ps_scr.ap(0, 128, 32, 8), wsl.ap(0, 8, 128, 128), self.eye32.ap(0, 8, 0, 8)))
        B.inc("tensor", self.mm, lambda E: E.transpose(
            self.ps_scr.ap(0, 128, 40, 8), wsl.ap(0, 8, 256, 128), self.eye32.ap(0, 8, 0, 8)))
        B.wait("vector", self.mm, self.mm.n)
        B.run("vector", lambda E: E.tensor_copy(bexpL1.ap(), self.ps_scr.ap(0, 128, 0, 32)))
        B.inc("vector", self.vc, lambda E: E.tensor_copy(
            b2t.ap(0, 128, 0, 8), self.ps_scr.ap(0, 128, 32, 8)))
        B.wait("vector", self.vc, self.vc.n)
        B.inc("vector", self.vc, lambda E: E.tensor_tensor(
            b2t.ap(0, 128, 0, 8), b2t.ap(0, 128, 0, 8), self.ps_scr.ap(0, 128, 40, 8), OP.add))

        B.run("vector", lambda E: E.tensor_scalar(eyebf.ap(), self.fidx.ap(), self.pidx.ap(), None, OP.is_equal))
        B.wait("sync", self.ld, self.ld.n)
        B.inc("sync", self.ld, lambda E: E.dma_start(
            out=pmats.ap(), in_=dram_ap(self.win["l0f_pmc"], 0, [[7 * 128, 128], [1, 7 * 128]])), 16)
        B.wait("vector", self.ld, self.ld.n)

        for set_i, (wdram, stages, grid) in enumerate([
                (self.win["l0f_upw"], list(range(7)), "L1"),
                (self.win["l0f_upw"], list(range(7, 12)), "L2"),
                (self.win["l0f_dnw"], list(range(7)), "L1"),
                (self.win["l0f_dnw"], list(range(7, 12)), "L2")]):
            dest = bsets[set_i]
            n_st = len(stages)
            for idx, s in enumerate(reversed(stages)):
                B.wait("sync", self.vc, self.vc.n)
                B.wait("sync", self.ld, self.ld.n)
                if grid == "L1":
                    B.inc("sync", self.ld, lambda E, s=s, wdram=wdram: E.dma_start(
                        out=wsl.ap(0, 32, 0, 256),
                        in_=dram_ap(wdram, s * 8192, [[256, 32], [1, 256]])), 16)
                    stt = 2 ** s
                    ap_a = ([4 * stt, 2 ** (6 - s)], [4, stt])
                else:
                    B.inc("sync", self.ld, lambda E, s=s, wdram=wdram, ws2=ws2: E.dma_start(
                        out=ws2.ap(0, 32, 0, 8068),
                        in_=dram_ap(wdram, s * 8192, [[4, 32], [1, 8068]])), 16)
                    stt = 2 ** s
                    ap_a = ([4 * stt, 2 ** (11 - s)], [128, 2 ** (s - 5)])
                B.wait("vector", self.ld, self.ld.n)
                srcbuf = wsl if grid == "L1" else ws2
                self._bf_regather(srcbuf, ap_a)
                B.wait("tensor", self.vc, self.vc.n)
                B.wait("tensor", self.ev, self.ev.n)
                B.run("tensor", lambda E: E.transpose(
                    self.ps_scr.ap(0, 128, 64, 32), self.ta.ap(0, 32, 0, 128), self.eye32.ap(0, 32, 0, 32)))
                B.inc("tensor", self.mm, lambda E: E.transpose(
                    self.ps_scr.ap(0, 128, 96, 32), self.ta.ap(0, 32, 128, 128), self.eye32.ap(0, 32, 0, 32)))
                B.wait("vector", self.mm, self.mm.n)
                B.inc("vector", self.vc, lambda E, vab=vab: E.tensor_copy(
                    vab.ap(0, 128, 0, 64), self.ps_scr.ap(0, 128, 64, 64)))
                B.wait("vector", self.vc, self.vc.n)
                eye_rep = eyebf.apd(0, [[128, 128], [0, 32], [1, 128]])
                va_rep = vab.apd(0, [[64, 128], [1, 32], [0, 128]])
                pm = pmats.apd(PIDX[stt if grid == "L1" else stt // 32] * 128,
                               [[7 * 128, 128], [0, 32], [1, 128]])
                vb_rep = vab.apd(32, [[64, 128], [1, 32], [0, 128]])
                da_ap = da_all.apd(0, [[NFC * 128, 128], [128, 32], [1, 128]])
                db_ap = db_all.apd(0, [[NFC * 128, 128], [128, 32], [1, 128]])
                B.run("vector", lambda E, eye_rep=eye_rep, va_rep=va_rep, da_ap=da_ap: E.tensor_tensor(
                    da_ap, eye_rep, va_rep, OP.mult))
                B.inc("vector", self.vc, lambda E, pm=pm, vb_rep=vb_rep, db_ap=db_ap: E.tensor_tensor(
                    db_ap, pm, vb_rep, OP.mult))
                B.wait("tensor", self.vc, self.vc.n)
                B.wait("tensor", self.ev, self.ev.n)
                out_is_dest = ((n_st - 1 - idx) % 2 == 0)
                outbuf = dest if out_is_dest else tsc
                if idx == 0:
                    rhsbuf = None
                else:
                    rhsbuf = tsc if out_is_dest else dest
                stage_ev = self.ev.n
                for c in range(NFC):
                    slot = c % 16
                    if c >= 16 and c % 4 == 0:
                        B.wait("tensor", self.ev, stage_ev + (c // 4 - 4) + 1)
                    if rhsbuf is None:
                        rhs = eyebf.ap()
                    else:
                        rhs = rhsbuf.ap(0, 128, c * 128, 128)
                    B.run("tensor", lambda E, c=c, slot=slot, da_all=da_all, rhs=rhs: E.matmul(
                        self.ps_a.ap(0, 128, slot * 128, 128),
                        da_all.ap(0, 128, c * 128, 128), rhs, start=True, stop=False))
                    fn = lambda E, c=c, slot=slot, db_all=db_all, rhs=rhs: E.matmul(
                        self.ps_a.ap(0, 128, slot * 128, 128),
                        db_all.ap(0, 128, c * 128, 128), rhs, start=False, stop=True)
                    if c % 4 == 3:
                        B.inc("tensor", self.mm, fn)
                        B.wait("vector", self.mm, self.mm.n)
                        grp = c // 4
                        B.inc("vector", self.ev, lambda E, grp=grp, outbuf=outbuf: E.tensor_copy(
                            outbuf.ap(0, 128, grp * 512, 512),
                            self.ps_a.ap(0, 128, (grp % 4) * 512, 512)))
                    else:
                        B.run("tensor", fn)
        B.wait("tensor", self.ev, self.ev.n)
        self.barrier()
        s_fb.close()

        s_ws = ExitStack()
        tmpw = [B.sbuf(f"tmpw{i}", 128, FD, BF16, stack=s_ws) for i in range(2)]
        gw = [self.gwa, self.gwb]
        stw = [self.wsa, self.wsb]
        for c in range(NCH):
            par = gw[c % 2]
            spar = stw[c % 2]
            buf = tmpw[c % 2]
            B.wait("gpsimd", spar, spar.n)
            B.wait("gpsimd", par, par.n)
            B.inc("gpsimd", par, lambda E, c=c, buf=buf: E.dma_start(
                out=buf.ap(), in_=dram_ap(self.win["l0f_wexp"], c * 128 * FD, [[FD, 128], [1, FD]])), 16)
            B.wait("sync", par, par.n)
            B.wait("sync", spar, spar.n)
            B.inc("sync", spar, lambda E, c=c, buf=buf: E.dma_start(
                out=dram_ap(self.wescr, c * 512, [[8 * 512, 128], [128 * 8 * 512, 8], [1, 512]]),
                in_=buf.ap(0, 128, 0, FD)), 16)
        for fc in range(NFC):
            par = gw[fc % 2]
            spar = stw[fc % 2]
            buf = tmpw[fc % 2]
            B.wait("gpsimd", spar, spar.n)
            B.wait("gpsimd", par, par.n)
            B.inc("gpsimd", par, lambda E, fc=fc, buf=buf: E.dma_start(
                out=buf.ap(0, 128, 0, D), in_=dram_ap(self.win["l0f_wcon"], fc * D, [[32 * D, 128], [1, D]])), 16)
            B.wait("sync", par, par.n)
            B.wait("sync", spar, spar.n)
            B.inc("sync", spar, lambda E, fc=fc, buf=buf: E.dma_start(
                out=dram_ap(self.wcscr, fc * 128, [[32 * 128, 128], [128 * 32 * 128, 8], [1, 128]]),
                in_=buf.ap(0, 128, 0, D)), 16)
        B.wait("sync", self.wsa, self.wsa.n)
        B.wait("sync", self.wsb, self.wsb.n)
        B.wait("gpsimd", self.gwa, self.gwa.n)
        B.wait("gpsimd", self.gwb, self.gwb.n)
        self.barrier()
        s_ws.close()

        s_fh = ExitStack()
        hA = B.sbuf("hA", 128, NFC * QT, BF16, stack=s_fh)
        hB = B.sbuf("hB", 128, NFC * QT, BF16, stack=s_fh)
        webuf = [B.sbuf(f"webuf{i}", 128, 8 * 512, BF16, stack=s_fh) for i in range(2)]
        wcbuf = [B.sbuf(f"wcbuf{i}", 128, NFC * 128, BF16, stack=s_fh) for i in range(2)]

        for tq in range(4):
            tok0 = tq * QT
            for g8 in range(8):
                buf = webuf[g8 % 2]
                B.wait("sync", self.mm, self.mm.n)
                spar = self.lda if g8 % 2 == 0 else self.ldb
                B.wait("sync", spar, spar.n)
                B.inc("sync", spar, lambda E, g8=g8, buf=buf: E.dma_start(
                    out=buf.ap(), in_=dram_ap(self.wescr, g8 * 128 * 8 * 512, [[8 * 512, 128], [1, 8 * 512]])), 16)
                B.wait("tensor", spar, spar.n)
                groups, evacs = [], []
                for i4 in range(4):
                    fc = g8 * 4 + i4
                    groups.append([(buf.ap(0, 128, c * 512 + i4 * 128, 128),
                                    self.xbf.ap(0, 128, c * S + tok0, QT)) for c in range(NCH)])
                    evacs.append([lambda E, pt, fc=fc, hA=hA: E.activation(
                        hA.ap(0, 128, fc * QT, QT), pt, AF.Identity, bias=bexpL1.ap(0, 128, fc, 1))])
                self.mm_pipe(groups, self.pa4, evacs)
            groups, evacs = [], []
            for fc in range(NFC):
                groups.append([(bsets[0].ap(0, 128, fc * 128, 128), hA.ap(0, 128, fc * QT, QT))])
                evacs.append([lambda E, pt, fc=fc, hA=hA: E.tensor_copy(
                    hA.ap(0, 128, fc * QT, QT), pt)])
            self.mm_pipe(groups, self.pa4, evacs, evac_engine="vector")
            self._grid_change(hA, hB, QT, fwd=True)
            groups, evacs = [], []
            for fc in range(NFC):
                groups.append([(bsets[1].ap(0, 128, fc * 128, 128), hB.ap(0, 128, fc * QT, QT))])
                evacs.append([lambda E, pt, fc=fc, hB=hB: E.activation(
                    hB.ap(0, 128, fc * QT, QT), pt, AF.Gelu, bias=upbL2.ap(0, 128, fc, 1))])
            self.mm_pipe(groups, self.pa4, evacs)
            self._grid_change(hB, hA, QT, fwd=False)
            groups, evacs = [], []
            for fc in range(NFC):
                groups.append([(bsets[2].ap(0, 128, fc * 128, 128), hA.ap(0, 128, fc * QT, QT))])
                evacs.append([lambda E, pt, fc=fc, hA=hA: E.tensor_copy(
                    hA.ap(0, 128, fc * QT, QT), pt)])
            self.mm_pipe(groups, self.pa4, evacs, evac_engine="vector")
            self._grid_change(hA, hB, QT, fwd=True)
            groups, evacs = [], []
            for fc in range(NFC):
                groups.append([(bsets[3].ap(0, 128, fc * 128, 128), hB.ap(0, 128, fc * QT, QT))])
                evacs.append([lambda E, pt, fc=fc, hB=hB: E.tensor_copy(
                    hB.ap(0, 128, fc * QT, QT), pt)])
            self.mm_pipe(groups, self.pa4, evacs, evac_engine="vector")
            groups, evacs = [], []
            for m in range(NCH):
                buf = wcbuf[m % 2]
                groups.append([(buf.ap(0, 128, fc * 128, 128),
                                hB.ap(0, 128, fc * QT, QT)) for fc in range(NFC)])

                def ev_con(E, pt, m=m, tstage=tstage):
                    return E.tensor_scalar(tstage.ap(0, 128, 0, QT), pt,
                                           b2t.ap(0, 128, m, 1), None, OP.add)

                def ev_con2(E, pt, m=m, xstage=xstage, rstage=rstage, tstage=tstage):
                    return E.tensor_tensor(rstage.ap(0, 128, (m % 2) * QT, QT),
                                           tstage.ap(0, 128, 0, QT),
                                           xstage.ap(0, 128, (m % 2) * QT, QT), OP.add)
                evacs.append([ev_con, ev_con2])
            self._wcon_pipe(groups, evacs, tq, xstage, rstage, wcbuf)
        B.wait("gpsimd", self.sta, self.sta.n)
        B.wait("gpsimd", self.stb, self.stb.n)
        B.wait("gpsimd", self.vc, self.vc.n)
        self.barrier()
        s_fh.close()
        s_f.close()
        self.alloc_xf()
        for c in range(NCH):
            B.inc("gpsimd", self.gl, lambda E, c=c: E.dma_start(
                out=self.xf.ap(0, 128, c * S, S),
                in_=dram_ap(self.xspill2, c * 128 * S, [[S, 128], [1, S]])), 16)
        B.wait("vector", self.gl, self.gl.n)
        B.wait("scalar", self.gl, self.gl.n)
        B.wait("tensor", self.gl, self.gl.n)
        self.ln_phase()

    def l1_attn(self):
        B = self.B
        AFE = AF.Exp
        for c in range(NCH):
            B.inc("gpsimd", self.gl, lambda E, c=c: E.dma_start(
                out=dram_ap(self.xspill, c * 128 * S, [[S, 128], [1, S]]),
                in_=self.xf.ap(0, 128, c * S, S)), 16)
        B.wait("gpsimd", self.gl, self.gl.n)
        B.inc("gpsimd", self.gc, lambda E: E.nop())
        for e in ENGINES:
            B.wait(e, self.gc, self.gc.n)
        self.s_xf.close()

        s_a = ExitStack()
        ogbf = B.sbuf("ogbf", 128, NCH * S, BF16, stack=s_a)
        qpair = B.sbuf("qpair", 128, S, BF16, stack=s_a)
        kpair = B.sbuf("kpair", 128, S, BF16, stack=s_a)
        gatep = B.sbuf("gatep", 128, S, BF16, stack=s_a)
        vtmp = B.sbuf("vtmp", 128, S, F32, stack=s_a)
        vtokp = B.sbuf("vtokp", 128, 16 * 130, BF16, stack=s_a)
        gathp = B.sbuf("gathp", 128, S, BF16, stack=s_a)
        wbuf = [B.sbuf(f"l1wb{i}", 128, 1024, BF16, stack=s_a) for i in range(2)]
        mkraw = B.sbuf("mkraw", 128, L1_BAND, BF16, stack=s_a)
        mk2 = B.sbuf("mk2", 128, L1_BAND, BF16, stack=s_a)
        ptil = [B.sbuf(f"ptil{i}", 128, S, BF16, stack=s_a) for i in range(2)]
        zsb = B.sbuf("zsb", 1, 2 * S, F32, stack=s_a)
        zrep = B.sbuf("zrep2", 128, S, BF16, stack=s_a)
        rzf = B.sbuf("rzf", 128, S, F32, stack=s_a)
        og1 = B.sbuf("og1", 128, S, F32, stack=s_a)
        c2l = B.sbuf("c2l", 1, 2 * S, BF16, stack=s_a)
        bqt = B.sbuf("bqt", 128, 32, F32, stack=s_a)
        bot = B.sbuf("bot", 128, 8, F32, stack=s_a)
        revb = B.sbuf("revb", 128, 128, BF16, stack=s_a)
        p127 = B.sbuf("p127", 128, 1, F32, stack=s_a)
        s_prep = ExitStack()
        pbx = B.sbuf("pbx", 44, 16, F32, stack=s_prep)
        epbb = B.sbuf("epbb", 44, 16, BF16, stack=s_prep)
        gselb = B.sbuf("gselb", 44, 23, BF16, stack=s_prep)
        wsb = B.sbuf("wsb", 23, 16, F32, stack=s_prep)
        wsbb = B.sbuf("wsbb", 23, 16, BF16, stack=s_prep)
        stcb = B.sbuf("stcb", 23, S, BF16, stack=s_prep)
        mz = B.sbuf("mz", 16, L1_WROW, BF16, stack=s_prep)

        B.wait("sync", self.ld, self.ld.n)
        B.inc("sync", self.ld, lambda E: E.dma_start(
            out=pbx.ap(), in_=dram_ap(self.win["l1a_pb"], 0, [[16, 44], [1, 16]])), 16)
        B.wait("sync", self.ld, self.ld.n)
        B.inc("sync", self.ld, lambda E: E.dma_start(
            out=gselb.ap(), in_=dram_ap(self.win["l1a_gsel"], 0, [[23, 44], [1, 23]])), 16)
        B.wait("sync", self.ld, self.ld.n)
        B.inc("sync", self.ld, lambda E: E.dma_start(
            out=bqt.ap(0, 128, 0, 24),
            in_=dram_ap(self.win["l1a_bqkv"], 0, [[1, 128], [128, 24]]), allow_slow_non_contiguous=True), 16)
        B.wait("sync", self.ld, self.ld.n)
        B.inc("sync", self.ld, lambda E: E.dma_start(
            out=bqt.ap(0, 128, 24, 8),
            in_=dram_ap(self.win["l1a_bgate"], 0, [[1, 128], [128, 8]]), allow_slow_non_contiguous=True), 16)
        B.wait("sync", self.ld, self.ld.n)
        B.inc("sync", self.ld, lambda E: E.dma_start(
            out=bot.ap(), in_=dram_ap(self.win["l1a_bout"], 0, [[1, 128], [128, 8]]), allow_slow_non_contiguous=True), 16)

        B.inc("vector", self.vc, lambda E: E.tensor_scalar(
            p127.ap(), self.pidx.ap(), -1.0, 127.0, OP.mult, OP.add))
        B.wait("vector", self.vc, self.vc.n)
        B.inc("vector", self.vc, lambda E: E.tensor_scalar(
            revb.ap(), self.fidx.ap(), p127.ap(), None, OP.is_equal))
        B.run("vector", lambda E: E.memset(mz.ap(), 0.0))
        B.inc("vector", self.vc, lambda E: E.memset(
            vtokp.apd(64, [[16 * 130, 128], [65, 32], [1, 1]]), 1.0))

        B.wait("scalar", self.ld, self.ld.n)
        B.inc("scalar", self.sc, lambda E: E.activation(epbb.ap(), pbx.ap(), AF.Exp))
        B.wait("tensor", self.sc, self.sc.n)
        B.wait("tensor", self.ld, self.ld.n)
        B.wait("tensor", self.vc, self.vc.n)
        B.inc("tensor", self.mm, lambda E: E.matmul(
            self.ps_scr.ap(0, 23, 0, 16), gselb.ap(), epbb.ap(), start=True, stop=True))
        B.wait("vector", self.mm, self.mm.n)
        B.inc("vector", self.vc, lambda E: E.tensor_copy(wsb.ap(), self.ps_scr.ap(0, 23, 0, 16)))
        B.wait("vector", self.vc, self.vc.n)
        B.inc("vector", self.vc, lambda E: E.tensor_copy(wsbb.ap(), wsb.ap()))
        B.wait("scalar", self.vc, self.vc.n)
        B.wait("sync", self.ld, self.ld.n)
        B.inc("sync", self.ld, lambda E: E.dma_start(
            out=stcb.ap(), in_=dram_ap(self.win["l1a_stc"], 0, [[S, 23], [1, S]])), 16)
        B.wait("tensor", self.ld, self.ld.n)
        B.wait("tensor", self.vc, self.vc.n)
        for bb in range(4):
            B.inc("tensor", self.mm, lambda E, bb=bb: E.matmul(
                self.ps_a.ap(0, 16, bb * 512, 512), wsbb.ap(),
                stcb.ap(0, 23, bb * 512, 512), start=True, stop=True))
        B.wait("vector", self.mm, self.mm.n)
        B.inc("vector", self.vc, lambda E: E.tensor_copy(
            stcb.ap(0, 16, 0, S), self.ps_a.ap(0, 16, 0, S)))
        B.wait("gpsimd", self.vc, self.vc.n)
        B.inc("gpsimd", self.gb, lambda E: E.dma_start(
            out=dram_ap(self.c2scr, 0, [[S, 16], [1, S]]),
            in_=stcb.ap(0, 16, 0, S)), 16)
        B.wait("gpsimd", self.gb, self.gb.n)
        B.wait("gpsimd", self.vc, self.vc.n)
        B.inc("gpsimd", self.gb, lambda E: E.dma_start(
            out=dram_ap(self.wrowscr, 0, [[L1_WROW, 16], [1, L1_WROW]]), in_=mz.ap()), 16)
        B.wait("gpsimd", self.gb, self.gb.n)
        B.inc("gpsimd", self.gb, lambda E: E.dma_start(
            out=dram_ap(self.wscr23, 0, [[16, 23], [1, 16]]), in_=wsb.ap()), 16)
        B.wait("gpsimd", self.gb, self.gb.n)
        for oi, o in enumerate(L1_OFFS):
            B.inc("gpsimd", self.gb, lambda E, oi=oi, o=o: E.dma_start(
                out=dram_ap(self.wrowscr, 127 + o, [[L1_WROW, 16], [1, 1]]),
                in_=dram_ap(self.wscr23, oi * 16, [[1, 16]]),
                allow_slow_non_contiguous=True), 16)
            B.wait("gpsimd", self.gb, self.gb.n)
        B.inc("gpsimd", self.gc, lambda E: E.nop())
        for e in ENGINES:
            B.wait(e, self.gc, self.gc.n)
        s_prep.close()

        po = [self.ps_o1, self.ps_o2]
        po_ev = [0, 0]
        po_i = [0]

        def piece_mm(mms, evac):
            i = po_i[0] & 1
            pso = po[i]
            B.wait("tensor", self.sc, po_ev[i])
            mms(pso)
            B.wait("scalar", self.mm, self.mm.n)
            evac(pso)
            po_ev[i] = self.sc.n
            po_i[0] += 1

        psa_sc = [0]
        psa_vc = [0]
        gath_vc = [0]
        mk_vc = [0]
        vtok_mm = [0]
        wb_mm = [0, 0]
        par_ld = [self.lda, self.ldb]
        par_gw = [self.gwa, self.gwb]

        WQKV = self.win["l1a_wqkv"]
        WGATE = self.win["l1a_wgate"]

        zvc = [0]
        for pr in range(NCH):
            hA, hB2 = 2 * pr, 2 * pr + 1
            B.wait("sync", self.vc, zvc[0])
            B.wait("sync", self.ld, self.ld.n)
            B.inc("sync", self.ld, lambda E, pr=pr: E.dma_start(
                out=c2l.ap(0, 1, 0, 2 * S),
                in_=dram_ap(self.c2scr, 2 * pr * S, [[1, 2 * S]])), 16)
            projs = [
                (WQKV, 3 * D, pr * 128, pr, AF.Identity, qpair, BF16),
                (WQKV, 3 * D, D + pr * 128, 8 + pr, AF.Identity, kpair, BF16),
                (WQKV, 3 * D, 2 * D + pr * 128, 16 + pr, AF.Identity, vtmp, F32),
                (WGATE, D, pr * 128, 24 + pr, AF.Sigmoid, gatep, BF16),
            ]
            for pi, (wd, wpitch, c0, bcol, func, dst, ddt) in enumerate(projs):
                wi = pi & 1
                par = par_gw[wi]
                B.wait("gpsimd", self.mm, wb_mm[wi])
                B.wait("gpsimd", par, par.n)
                B.inc("gpsimd", par, lambda E, wd=wd, wpitch=wpitch, c0=c0, wi=wi: E.dma_start(
                    out=wbuf[wi].ap(),
                    in_=dram_ap(wd, c0, [[wpitch, 128], [128 * wpitch, 8], [1, 128]])), 16)
                B.wait("tensor", par, par.n)
                if pi == 2:
                    B.wait("scalar", self.vc, gath_vc[0])
                if pi == 3:
                    B.wait("scalar", self.vc, gath_vc[0])
                for k in range(4):
                    def mms(pso, k=k, wi=wi):
                        for t in range(NCH):
                            fn = lambda E, pso=pso, t=t, k=k, wi=wi: E.matmul(
                                pso.ap(0, 128, 0, 512),
                                wbuf[wi].ap(0, 128, t * 128, 128),
                                self.xbf.ap(0, 128, t * S + k * 512, 512),
                                start=(t == 0), stop=(t == NCH - 1))
                            if t == NCH - 1:
                                B.inc("tensor", self.mm, fn)
                            else:
                                B.run("tensor", fn)

                    def evac(pso, k=k, func=func, dst=dst, bcol=bcol):
                        B.inc("scalar", self.sc, lambda E, pso=pso, k=k, func=func,
                              dst=dst, bcol=bcol: E.activation(
                                  dst.ap(0, 128, k * 512, 512), pso.ap(0, 128, 0, 512),
                                  func, bias=bqt.ap(0, 128, bcol, 1)))
                    piece_mm(mms, evac)
                wb_mm[wi] = self.mm.n

            B.wait("tensor", self.sc, self.sc.n)
            B.wait("tensor", self.vc, self.vc.n)
            B.wait("tensor", self.mm, vtok_mm[0])
            tp_ev = {}
            tps = [self.ps_scr, self.ps_o1, self.ps_o2]
            for mb in range(16):
                ps = tps[mb % 3]
                if mb >= 3:
                    B.wait("tensor", self.vc, tp_ev[mb - 3])
                B.inc("tensor", self.mm, lambda E, ps=ps, mb=mb: E.transpose(
                    ps.ap(0, 128, 0, 128), vtmp.ap(0, 128, mb * 128, 128),
                    self.eye32.ap()))
                B.wait("vector", self.mm, self.mm.n)
                B.run("vector", lambda E, ps=ps, mb=mb: E.tensor_copy(
                    vtokp.ap(0, 128, mb * 130, 64), ps.ap(0, 128, 0, 64)))
                B.inc("vector", self.vc, lambda E, ps=ps, mb=mb: E.tensor_copy(
                    vtokp.ap(0, 128, mb * 130 + 65, 64), ps.ap(0, 128, 64, 64)))
                tp_ev[mb] = self.vc.n
            B.wait("tensor", self.vc, self.vc.n)
            po_ev[0] = self.sc.n
            po_ev[1] = self.sc.n

            for hi, h in enumerate((hA, hB2)):
                B.wait("sync", self.mm, self.mm.n)
                par = par_ld[hi]
                B.wait("sync", par, par.n)
                B.inc("sync", par, lambda E, h=h: E.dma_start(
                    out=mkraw.ap(),
                    in_=dram_ap(self.wrowscr, h * L1_WROW, [[1, 128], [1, L1_BAND]])), 16)
                B.wait("tensor", par, par.n)
                B.wait("scalar", self.vc, mk_vc[0])
                for k in range(4):
                    w = min(512, L1_BAND - k * 512)

                    def mms(pso, k=k, w=w):
                        B.inc("tensor", self.mm, lambda E, pso=pso, k=k, w=w: E.matmul(
                            pso.ap(0, 128, 0, w), revb.ap(),
                            mkraw.ap(0, 128, k * 512, w), start=True, stop=True))

                    def evac(pso, k=k, w=w):
                        B.inc("scalar", self.sc, lambda E, pso=pso, k=k, w=w: E.activation(
                            mk2.ap(0, 128, k * 512, w), pso.ap(0, 128, 0, w), AF.Copy))
                    piece_mm(mms, evac)

                B.wait("tensor", self.sc, psa_sc[0])
                B.wait("tensor", self.vc, psa_vc[0])
                pt_mm = [0, 0]
                for mb in range(16):
                    m0 = 128 * mb
                    W = min(L1_BAND, S - m0)
                    pb_ = mb & 1
                    ptb = ptil[pb_]
                    npz = (W + 511) // 512
                    for k in range(npz):
                        n0 = m0 + k * 512
                        w = min(512, m0 + W - n0)

                        def mms(pso, h=h, hi_=h & 1, m0=m0, n0=n0, w=w):
                            B.wait("tensor", self.mm, pt_mm[pb_])
                            B.inc("tensor", self.mm, lambda E, pso=pso, hi_=hi_, m0=m0,
                                  n0=n0, w=w: E.matmul(
                                      pso.ap(0, 128, 0, w),
                                      kpair.ap(64 * hi_, 64, m0, 128),
                                      qpair.ap(64 * hi_, 64, n0, w),
                                      start=True, stop=True))

                        def evac(pso, n0=n0, w=w, ptb=ptb):
                            B.inc("scalar", self.sc, lambda E, pso=pso, n0=n0, w=w,
                                  ptb=ptb: E.activation(
                                      ptb.ap(0, 128, n0, w), pso.ap(0, 128, 0, w),
                                      AF.Exp, scale=0.125))
                        piece_mm(mms, evac)
                    if mb == 0:
                        B.inc("vector", self.vc, lambda E, ptb=ptb: E.memset(
                            ptb.ap(0, 128, L1_BAND, S - L1_BAND), 0.0))
                    B.wait("vector", self.sc, self.sc.n)
                    B.inc("vector", self.vc, lambda E, ptb=ptb, m0=m0, W=W: E.tensor_tensor(
                        ptb.ap(0, 128, m0, W), mk2.ap(0, 128, 0, W),
                        ptb.ap(0, 128, m0, W), OP.mult))
                    mk_vc[0] = self.vc.n
                    B.wait("tensor", self.vc, self.vc.n)
                    for b in range(4):
                        if mb == 0:
                            nlo, nhi = 512 * b, 512 * b + 512
                        else:
                            nlo = max(512 * b, m0)
                            nhi = min(512 * b + 512, m0 + W)
                            if nlo >= nhi:
                                continue
                        last = (mb == min(15, (512 * (b + 1) - 1) // 128))
                        fn = lambda E, b=b, nlo=nlo, nhi=nhi, mb=mb, hi_=h & 1, last=last: E.matmul(
                            self.ps_a.ap(0, 65, nlo, nhi - nlo),
                            vtokp.ap(0, 128, mb * 130 + 65 * hi_, 65),
                            ptil[mb & 1].ap(0, 128, nlo, nhi - nlo),
                            start=(mb == 0), stop=last)
                        if b == 3:
                            B.inc("tensor", self.mm, fn)
                        else:
                            B.run("tensor", fn)
                    pt_mm[pb_] = self.mm.n
                vtok_mm[0] = self.mm.n
                B.wait("scalar", self.mm, self.mm.n)
                for b in range(4):
                    B.inc("scalar", self.sc, lambda E, b=b, hi=hi: E.activation(
                        gathp.ap(64 * hi, 64, b * 512, 512),
                        self.ps_a.ap(0, 64, b * 512, 512), AF.Copy))
                B.wait("vector", self.mm, self.mm.n)
                B.wait("vector", self.ld, self.ld.n)
                B.inc("vector", self.vc, lambda E, hi=hi: E.tensor_tensor(
                    zsb.ap(0, 1, hi * S, S), self.ps_a.ap(64, 1, 0, S),
                    c2l.ap(0, 1, hi * S, S), OP.add))
                zvc[0] = self.vc.n
                psa_sc[0] = self.sc.n
                psa_vc[0] = self.vc.n

            stp = self.sta if pr % 2 == 0 else self.stb
            B.wait("gpsimd", self.sc, self.sc.n)
            B.wait("gpsimd", self.vc, self.vc.n)
            B.wait("gpsimd", stp, stp.n)
            B.inc("gpsimd", stp, lambda E, pr=pr: E.dma_start(
                out=dram_ap(self.zscr2, 2 * pr * S, [[S, 2], [1, S]]),
                in_=zsb.ap(0, 1, 0, 2 * S)), 16)
            B.wait("sync", stp, stp.n)
            B.wait("sync", self.ld, self.ld.n)
            B.inc("sync", self.ld, lambda E, pr=pr: E.dma_start(
                out=zrep.ap(),
                in_=dram_ap(self.zscr2, 2 * pr * S, [[S, 2], [0, 64], [1, S]])), 16)
            B.wait("vector", self.ld, self.ld.n)
            B.wait("vector", self.sc, self.sc.n)
            B.inc("vector", self.vc, lambda E: E.reciprocal(rzf.ap(), zrep.ap()))
            B.wait("vector", self.vc, self.vc.n)
            B.inc("vector", self.vc, lambda E: E.tensor_tensor(
                og1.ap(), gathp.ap(), gatep.ap(), OP.mult))
            B.wait("vector", self.vc, self.vc.n)
            B.inc("vector", self.vc, lambda E, pr=pr: E.tensor_tensor(
                ogbf.ap(0, 128, pr * S, S), og1.ap(), rzf.ap(), OP.mult))
            gath_vc[0] = self.vc.n

        s_t = ExitStack()
        xstage = B.sbuf("xstg1", 128, 2 * S, F32, stack=s_t)
        rstage = B.sbuf("rstg1", 128, 2 * S, F32, stack=s_t)
        tstage = B.sbuf("tstg1", 128, S, F32, stack=s_t)
        WOUT = self.win["l1a_wout"]
        ev2_at = {}
        st_at = {}
        B.wait("tensor", self.vc, self.vc.n)
        for co in range(NCH):
            wi = co & 1
            par = par_gw[wi]
            parx = par_ld[wi]
            B.wait("gpsimd", self.mm, wb_mm[wi])
            B.wait("gpsimd", par, par.n)
            B.inc("gpsimd", par, lambda E, co=co: E.dma_start(
                out=wbuf[co & 1].ap(),
                in_=dram_ap(WOUT, co * 128, [[D, 128], [128 * D, 8], [1, 128]])), 16)
            if co >= 2:
                B.wait("sync", self.vc, ev2_at[co - 2])
            B.wait("sync", parx, parx.n)
            B.inc("sync", parx, lambda E, co=co: E.dma_start(
                out=xstage.ap(0, 128, (co & 1) * S, S),
                in_=dram_ap(self.xspill, co * 128 * S, [[S, 128], [1, S]])), 16)
            B.wait("tensor", par, par.n)
            B.wait("vector", parx, parx.n)
            for k in range(4):
                def mms(pso, co=co, k=k):
                    for t in range(NCH):
                        fn = lambda E, pso=pso, co=co, t=t, k=k: E.matmul(
                            pso.ap(0, 128, 0, 512),
                            wbuf[co & 1].ap(0, 128, t * 128, 128),
                            ogbf.ap(0, 128, t * S + k * 512, 512),
                            start=(t == 0), stop=(t == NCH - 1))
                        if t == NCH - 1:
                            B.inc("tensor", self.mm, fn)
                        else:
                            B.run("tensor", fn)

                def evac(pso, co=co, k=k):
                    B.inc("scalar", self.sc, lambda E, pso=pso, co=co, k=k: E.activation(
                        tstage.ap(0, 128, k * 512, 512), pso.ap(0, 128, 0, 512),
                        AF.Identity, bias=bot.ap(0, 128, co, 1)))
                piece_mm(mms, evac)
            wb_mm[wi] = self.mm.n
            B.wait("vector", self.sc, self.sc.n)
            if co >= 2:
                B.wait("vector", st_at[co - 2][0], st_at[co - 2][1])
            B.inc("vector", self.vc, lambda E, co=co: E.tensor_tensor(
                rstage.ap(0, 128, (co & 1) * S, S), tstage.ap(),
                xstage.ap(0, 128, (co & 1) * S, S), OP.add))
            ev2_at[co] = self.vc.n
            stp = self.sta if co % 2 == 0 else self.stb
            B.wait("gpsimd", self.vc, self.vc.n)
            B.wait("gpsimd", stp, stp.n)
            B.inc("gpsimd", stp, lambda E, co=co: E.dma_start(
                out=dram_ap(self.xspill2, co * 128 * S, [[S, 128], [1, S]]),
                in_=rstage.ap(0, 128, (co & 1) * S, S)), 16)
            st_at[co] = (stp, stp.n)
        B.wait("gpsimd", self.sta, self.sta.n)
        B.wait("gpsimd", self.stb, self.stb.n)
        B.wait("gpsimd", self.vc, self.vc.n)
        B.wait("gpsimd", self.mm, self.mm.n)
        B.wait("gpsimd", self.sc, self.sc.n)
        self.barrier()
        s_t.close()
        s_a.close()
        self.alloc_xf()
        for c in range(NCH):
            B.inc("gpsimd", self.gl, lambda E, c=c: E.dma_start(
                out=self.xf.ap(0, 128, c * S, S),
                in_=dram_ap(self.xspill2, c * 128 * S, [[S, 128], [1, S]])), 16)
        B.wait("vector", self.gl, self.gl.n)
        B.wait("scalar", self.gl, self.gl.n)
        B.wait("tensor", self.gl, self.gl.n)
        self.ln_phase()

    def l1_ffn(self):
        B = self.B
        NIC = 22
        QMAXI = 1.0 / 127.0
        RG = [[0, 1, 2, 3, 4, 5, 6, 7]]

        s_m = ExitStack()
        macc = B.sbuf("macc", 128, 4, F32, stack=s_m)
        gsc = B.sbuf("gsc", 1, 16, F32, stack=s_m)
        B.wait("vector", self.vc, self.vc.n)
        B.inc("vector", self.vc, lambda E: E.tensor_reduce(
            macc.ap(0, 128, 0, 1), self.xf.ap(), AX.X, OP.max, apply_absolute_value=True))
        B.wait("tensor", self.vc, self.vc.n)
        B.inc("tensor", self.mm, lambda E: E.transpose(
            self.ps_scr.ap(0, 1, 0, 128), macc.ap(0, 128, 0, 1), self.eye32.ap()))
        B.wait("vector", self.mm, self.mm.n)
        B.inc("vector", self.vc, lambda E: E.tensor_reduce(
            gsc.ap(0, 1, 0, 1), self.ps_scr.ap(0, 1, 0, 128), AX.X, OP.max))
        B.wait("gpsimd", self.vc, self.vc.n)
        B.inc("gpsimd", self.gb, lambda E: E.dma_start(
            out=dram_ap(self.ccxi, 0, [[1, 1]]), in_=gsc.ap(0, 1, 0, 1)), 16)
        B.wait("gpsimd", self.gb, self.gb.n)
        B.inc("gpsimd", self.ccs, lambda E: E.collective_compute(
            "AllReduce", OP.max, replica_groups=RG,
            ins=[self.ccxi.ap().opt()], outs=[self.ccxo.ap().opt()]), 1)
        B.wait("gpsimd", self.ccs, self.ccs.n)
        B.inc("gpsimd", self.gb, lambda E: E.dma_start(
            out=gsc.ap(0, 1, 1, 1), in_=dram_ap(self.ccxo, 0, [[1, 1]])), 16)
        B.wait("sync", self.ld, self.ld.n)
        B.inc("sync", self.ld, lambda E: E.dma_start(
            out=gsc.ap(0, 1, 2, 1), in_=dram_ap(self.win["l1f_clipx"], 0, [[1, 1]])), 16)
        B.wait("sync", self.ld, self.ld.n)
        B.inc("sync", self.ld, lambda E: E.dma_start(
            out=gsc.ap(0, 1, 3, 1), in_=dram_ap(self.win["l1f_clipd"], 0, [[1, 1]])), 16)
        B.wait("scalar", self.ld, self.ld.n)
        B.inc("scalar", self.sc, lambda E: E.activation(
            gsc.ap(0, 1, 4, 2), gsc.ap(0, 1, 2, 2), AF.Sigmoid))
        B.wait("vector", self.sc, self.sc.n)
        B.wait("vector", self.gb, self.gb.n)
        B.inc("vector", self.vc, lambda E: E.tensor_tensor(
            gsc.ap(0, 1, 6, 1), gsc.ap(0, 1, 4, 1), gsc.ap(0, 1, 1, 1), OP.mult))
        B.wait("vector", self.vc, self.vc.n)
        B.inc("vector", self.vc, lambda E: E.tensor_scalar(
            gsc.ap(0, 1, 6, 1), gsc.ap(0, 1, 6, 1), QMAXI, None, OP.mult))
        B.wait("vector", self.vc, self.vc.n)
        B.inc("vector", self.vc, lambda E: E.reciprocal(
            gsc.ap(0, 1, 7, 1), gsc.ap(0, 1, 6, 1)))
        B.wait("gpsimd", self.vc, self.vc.n)
        B.inc("gpsimd", self.gb, lambda E: E.dma_start(
            out=dram_ap(self.rowscr, 0, [[1, 8]]), in_=gsc.ap(0, 1, 0, 8)), 16)

        for c in range(NCH):
            B.inc("gpsimd", self.gl, lambda E, c=c: E.dma_start(
                out=dram_ap(self.xspill, c * 128 * S, [[S, 128], [1, S]]),
                in_=self.xf.ap(0, 128, c * S, S)), 16)
        B.wait("gpsimd", self.gl, self.gl.n)
        B.inc("gpsimd", self.gc, lambda E: E.nop())
        for e in ENGINES:
            B.wait(e, self.gc, self.gc.n)
        s_m.close()
        self.s_xf.close()

        s_f = ExitStack()
        wdqT = B.sbuf("wdqT", 128, NIC * 8 * 128, BF16, stack=s_f)
        hqc2 = [B.sbuf(f"hqc{i}", 128, NIC * 512, BF16, stack=s_f) for i in range(2)]
        hstg = [B.sbuf(f"hstg{i}", 128, S, F32, stack=s_f) for i in range(2)]
        swg = B.sbuf("swg", 128, 96, F32, stack=s_f)
        scl = B.sbuf("scl", 128, 12, F32, stack=s_f)
        hmc = B.sbuf("hmc", 128, NIC + 2, F32, stack=s_f)
        gsc2 = B.sbuf("gsc2", 1, 16, F32, stack=s_f)
        s_q = ExitStack()
        wstg = [B.sbuf(f"wstg{i}", 128, 2816, F32, stack=s_q) for i in range(2)]
        quf = B.sbuf("quf", 128, 2816, F32, stack=s_q)
        qf = B.sbuf("qf", 128, 2816, F32, stack=s_q)
        wglT = B.sbuf("wglT", 128, 1024, BF16, stack=s_q)
        wulT = B.sbuf("wulT", 128, 1024, BF16, stack=s_q)
        silg = B.sbuf("silg", 128, 512, F32, stack=s_q)

        B.wait("sync", self.gb, self.gb.n)
        B.wait("sync", self.ld, self.ld.n)
        B.inc("sync", self.ld, lambda E: E.dma_start(
            out=scl.ap(0, 128, 0, 8), in_=dram_ap(self.rowscr, 0, [[0, 128], [1, 8]])), 16)
        B.wait("scalar", self.ld, self.ld.n)
        B.wait("vector", self.ld, self.ld.n)

        for c in range(NCH):
            B.wait("scalar", self.vc, self.vc.n)
            B.inc("scalar", self.sc, lambda E, c=c: E.activation(
                quf.ap(0, 128, 0, S), self.xf.ap(0, 128, c * S, S),
                AF.Copy, scale=scl.ap(0, 128, 7, 1)))
            B.wait("vector", self.sc, self.sc.n)
            B.inc("vector", self.vc, lambda E: E.tensor_scalar(
                qf.ap(0, 128, 0, S), quf.ap(0, 128, 0, S), RNE_C, RNE_C, OP.add, OP.subtract))
            B.wait("vector", self.vc, self.vc.n)
            B.inc("vector", self.vc, lambda E: E.tensor_scalar(
                qf.ap(0, 128, 0, S), qf.ap(0, 128, 0, S), 127.0, -128.0, OP.min, OP.max))
            B.wait("vector", self.vc, self.vc.n)
            B.inc("vector", self.vc, lambda E, c=c: E.tensor_copy(
                self.xbf.ap(0, 128, c * S, S), qf.ap(0, 128, 0, S)))

        po = [self.ps_o1, self.ps_o2]
        psev = {id(self.ps_o1): (self.sc, 0), id(self.ps_o2): (self.sc, 0),
                id(self.ps_scr): (self.sc, 0)}
        po_i = [0]

        def piece(mms, evacs, eng="scalar"):
            i = po_i[0] & 1
            pso = po[i]
            sem, cnt = psev[id(pso)]
            B.wait("tensor", sem, cnt)
            mms(pso)
            B.wait(eng, self.mm, self.mm.n)
            evacs(pso)
            psev[id(pso)] = (self.sc, self.sc.n) if eng == "scalar" else (self.vc, self.vc.n)
            po_i[0] += 1

        par_ld = [self.lda, self.ldb]
        par_gw = [self.gwa, self.gwb]
        wld_gate = [0, 0]
        qf_mm = [0]
        silg_vc = [0]

        def quant_weight(wi, blk, dram, cols, scol):
            par = par_gw[wi & 1]
            B.wait("gpsimd", self.vc, wld_gate[wi & 1])
            B.wait("gpsimd", par, par.n)
            B.inc("gpsimd", par, lambda E, wi=wi, blk=blk, dram=dram, cols=cols: E.dma_start(
                out=wstg[wi & 1].ap(0, 128, 0, cols),
                in_=dram_ap(dram, blk * 128 * cols, [[cols, 128], [1, cols]])), 16)
            B.wait("vector", par, par.n)
            B.inc("vector", self.vc, lambda E, wi=wi, cols=cols, scol=scol: E.tensor_reduce(
                swg.ap(0, 128, scol, 1), wstg[wi & 1].ap(0, 128, 0, cols), AX.X, OP.max, apply_absolute_value=True))
            B.wait("vector", self.vc, self.vc.n)
            B.inc("vector", self.vc, lambda E, scol=scol: E.tensor_scalar(
                swg.ap(0, 128, scol, 1), swg.ap(0, 128, scol, 1), QMAXI, None, OP.mult))
            B.wait("vector", self.vc, self.vc.n)
            B.inc("vector", self.vc, lambda E, scol=scol: E.reciprocal(
                scl.ap(0, 128, 8, 1), swg.ap(0, 128, scol, 1)))
            B.wait("scalar", self.vc, self.vc.n)
            B.inc("scalar", self.sc, lambda E, wi=wi, cols=cols: E.activation(
                quf.ap(0, 128, 0, cols), wstg[wi & 1].ap(0, 128, 0, cols),
                AF.Copy, scale=scl.ap(0, 128, 8, 1)))
            B.wait("vector", self.sc, self.sc.n)
            B.wait("vector", self.mm, qf_mm[0])
            B.inc("vector", self.vc, lambda E, cols=cols: E.tensor_scalar(
                qf.ap(0, 128, 0, cols), quf.ap(0, 128, 0, cols), RNE_C, RNE_C,
                OP.add, OP.subtract))
            wld_gate[wi & 1] = self.vc.n
            B.wait("vector", self.vc, self.vc.n)
            B.inc("vector", self.vc, lambda E, cols=cols: E.tensor_scalar(
                qf.ap(0, 128, 0, cols), qf.ap(0, 128, 0, cols), 127.0, -128.0,
                OP.min, OP.max))

        tps = [self.ps_scr, self.ps_o1, self.ps_o2]

        def transpose_to(dstbuf, coloff, nblk, colstep):
            B.wait("tensor", self.vc, self.vc.n)
            for tb in range(nblk):
                ps = tps[tb % 3]
                sem, cnt = psev[id(ps)]
                B.wait("tensor", sem, cnt)
                B.inc("tensor", self.mm, lambda E, ps=ps, tb=tb: E.transpose(
                    ps.ap(0, 128, 0, 128), qf.ap(0, 128, tb * 128, 128), self.eye32.ap()))
                B.wait("vector", self.mm, self.mm.n)
                B.inc("vector", self.vc, lambda E, ps=ps, tb=tb, dstbuf=dstbuf,
                      coloff=coloff, colstep=colstep: E.tensor_copy(
                          dstbuf.ap(0, 128, coloff + tb * colstep, 128), ps.ap(0, 128, 0, 128)))
                psev[id(ps)] = (self.vc, self.vc.n)
            qf_mm[0] = self.mm.n

        hst_st = {}
        for ic in range(NIC):
            quant_weight(2 * ic, ic, self.win["l1f_wg"], D, ic)
            transpose_to(wglT, 0, 8, 128)
            B.wait("vector", self.ld, self.ld.n)
            B.inc("vector", self.vc, lambda E, ic=ic: E.tensor_scalar(
                swg.ap(0, 128, 44 + ic, 1), swg.ap(0, 128, ic, 1),
                scl.ap(0, 128, 6, 1), None, OP.mult))
            quant_weight(2 * ic + 1, ic, self.win["l1f_wu"], D, 22 + ic)
            transpose_to(wulT, 0, 8, 128)
            B.inc("vector", self.vc, lambda E, ic=ic: E.tensor_scalar(
                swg.ap(0, 128, 66 + ic, 1), swg.ap(0, 128, 22 + ic, 1),
                scl.ap(0, 128, 6, 1), None, OP.mult))
            fold_vc = self.vc.n
            hbuf = hstg[ic & 1]
            if hst_st.get(ic - 2) is not None:
                B.wait("vector", hst_st[ic - 2][0], hst_st[ic - 2][1])
            for np_ in range(4):
                def mms_g(pso, np_=np_):
                    for dc in range(NCH):
                        fn = lambda E, pso=pso, dc=dc, np_=np_: E.matmul(
                            pso.ap(0, 128, 0, 512), wglT.ap(0, 128, dc * 128, 128),
                            self.xbf.ap(0, 128, dc * S + np_ * 512, 512),
                            start=(dc == 0), stop=(dc == NCH - 1))
                        if dc == NCH - 1:
                            B.inc("tensor", self.mm, fn)
                        else:
                            B.run("tensor", fn)

                def ev_g(pso, ic=ic, fold_vc=fold_vc):
                    B.wait("scalar", self.vc, max(silg_vc[0], fold_vc))
                    B.inc("scalar", self.sc, lambda E, pso=pso, ic=ic: E.activation(
                        silg.ap(), pso.ap(0, 128, 0, 512),
                        AF.Silu, scale=swg.ap(0, 128, 44 + ic, 1)))
                piece(mms_g, ev_g)

                def mms_u(pso, np_=np_):
                    for dc in range(NCH):
                        fn = lambda E, pso=pso, dc=dc, np_=np_: E.matmul(
                            pso.ap(0, 128, 0, 512), wulT.ap(0, 128, dc * 128, 128),
                            self.xbf.ap(0, 128, dc * S + np_ * 512, 512),
                            start=(dc == 0), stop=(dc == NCH - 1))
                        if dc == NCH - 1:
                            B.inc("tensor", self.mm, fn)
                        else:
                            B.run("tensor", fn)

                def ev_u(pso, ic=ic, np_=np_, hbuf=hbuf):
                    B.wait("vector", self.sc, self.sc.n)
                    B.wait("vector", self.vc, self.vc.n)
                    B.inc("vector", self.vc, lambda E, pso=pso, ic=ic, np_=np_,
                          hbuf=hbuf: E.tensor_scalar(
                              hbuf.ap(0, 128, np_ * 512, 512), pso.ap(0, 128, 0, 512),
                              swg.ap(0, 128, 66 + ic, 1), None, OP.mult))
                    B.wait("vector", self.vc, self.vc.n)
                    B.inc("vector", self.vc, lambda E, np_=np_, hbuf=hbuf: E.tensor_tensor(
                        hbuf.ap(0, 128, np_ * 512, 512), silg.ap(),
                        hbuf.ap(0, 128, np_ * 512, 512), OP.mult))
                    silg_vc[0] = self.vc.n
                piece(mms_u, ev_u, eng="vector")
            B.wait("vector", self.vc, self.vc.n)
            B.inc("vector", self.vc, lambda E, ic=ic, hbuf=hbuf: E.tensor_reduce(
                hmc.ap(0, 128, ic, 1), hbuf.ap(), AX.X, OP.max, apply_absolute_value=True))
            stp = self.sta if ic % 2 == 0 else self.stb
            B.wait("gpsimd", self.vc, self.vc.n)
            B.wait("gpsimd", stp, stp.n)
            B.inc("gpsimd", stp, lambda E, ic=ic, hbuf=hbuf: E.dma_start(
                out=dram_ap(self.hspill, ic * 128 * S, [[S, 128], [1, S]]),
                in_=hbuf.ap()), 16)
            hst_st[ic] = (stp, stp.n)

        B.wait("vector", self.vc, self.vc.n)
        B.inc("vector", self.vc, lambda E: E.tensor_reduce(
            hmc.ap(0, 128, NIC, 1), hmc.ap(0, 128, 0, NIC), AX.X, OP.max))
        B.wait("tensor", self.vc, self.vc.n)
        B.wait("tensor", self.sc, self.sc.n)
        B.inc("tensor", self.mm, lambda E: E.transpose(
            self.ps_scr.ap(0, 1, 0, 128), hmc.ap(0, 128, NIC, 1), self.eye32.ap()))
        B.wait("vector", self.mm, self.mm.n)
        B.inc("vector", self.vc, lambda E: E.tensor_reduce(
            gsc2.ap(0, 1, 0, 1), self.ps_scr.ap(0, 1, 0, 128), AX.X, OP.max))
        B.wait("gpsimd", self.vc, self.vc.n)
        B.inc("gpsimd", self.gb, lambda E: E.dma_start(
            out=dram_ap(self.cchi, 0, [[1, 1]]), in_=gsc2.ap(0, 1, 0, 1)), 16)
        B.wait("gpsimd", self.gb, self.gb.n)
        B.inc("gpsimd", self.ccs, lambda E: E.collective_compute(
            "AllReduce", OP.max, replica_groups=RG,
            ins=[self.cchi.ap().opt()], outs=[self.ccho.ap().opt()]), 1)
        B.wait("gpsimd", self.ccs, self.ccs.n)
        B.inc("gpsimd", self.gb, lambda E: E.dma_start(
            out=gsc2.ap(0, 1, 1, 1), in_=dram_ap(self.ccho, 0, [[1, 1]])), 16)
        B.wait("vector", self.gb, self.gb.n)
        B.inc("vector", self.vc, lambda E: E.tensor_tensor(
            gsc2.ap(0, 1, 2, 1), gsc2.ap(0, 1, 1, 1), scl.ap(0, 1, 5, 1), OP.mult))
        B.wait("vector", self.vc, self.vc.n)
        B.inc("vector", self.vc, lambda E: E.tensor_scalar(
            gsc2.ap(0, 1, 2, 1), gsc2.ap(0, 1, 2, 1), QMAXI, None, OP.mult))
        B.wait("vector", self.vc, self.vc.n)
        B.inc("vector", self.vc, lambda E: E.reciprocal(
            gsc2.ap(0, 1, 3, 1), gsc2.ap(0, 1, 2, 1)))
        B.wait("gpsimd", self.vc, self.vc.n)
        B.inc("gpsimd", self.gb, lambda E: E.dma_start(
            out=dram_ap(self.rowscr, 8, [[1, 4]]), in_=gsc2.ap(0, 1, 0, 4)), 16)
        B.wait("sync", self.gb, self.gb.n)
        B.wait("sync", self.ld, self.ld.n)
        B.inc("sync", self.ld, lambda E: E.dma_start(
            out=scl.ap(0, 128, 9, 2), in_=dram_ap(self.rowscr, 10, [[0, 128], [1, 2]])), 16)
        B.wait("scalar", self.ld, self.ld.n)
        B.wait("vector", self.ld, self.ld.n)

        for dc in range(NCH):
            quant_weight(dc, dc, self.win["l1f_wd"], 2816, 88 + dc)
            transpose_to(wdqT, dc * 128, NIC, 8 * 128)

        B.wait("gpsimd", self.mm, self.mm.n)
        B.wait("gpsimd", self.sc, self.sc.n)
        B.wait("gpsimd", self.vc, self.vc.n)
        self.barrier()
        s_q.close()
        s_t = ExitStack()
        xstage = B.sbuf("xstg2", 128, 2 * S, F32, stack=s_t)
        tst2 = B.sbuf("tst2", 128, 512, F32, stack=s_t)
        rst2 = B.sbuf("rst2", 128, 2 * S, F32, stack=s_t)
        h16 = B.sbuf("h16", 128, 512, F32, stack=s_t)
        B.wait("sync", self.sta, self.sta.n)
        B.wait("sync", self.stb, self.stb.n)
        xst_at = {}
        st2_at = {}
        hq_vc = {}
        hq_mm = [0, 0]
        for np_ in range(4):
            hqc = hqc2[np_ & 1]
            B.wait("vector", self.mm, hq_mm[np_ & 1])
            for ic in range(NIC):
                par = par_ld[ic & 1]
                B.wait("sync", par, par.n)
                B.wait("sync", self.vc, hq_vc.get(ic - 2, 0))
                B.inc("sync", par, lambda E, ic=ic, np_=np_: E.dma_start(
                    out=hstg[0].ap(0, 128, (ic & 1) * 512, 512),
                    in_=dram_ap(self.hspill, ic * 128 * S + np_ * 512, [[S, 128], [1, 512]])), 16)
                B.wait("scalar", par, par.n)
                B.wait("scalar", self.vc, self.vc.n)
                B.inc("scalar", self.sc, lambda E, ic=ic: E.activation(
                    h16.ap(), hstg[0].ap(0, 128, (ic & 1) * 512, 512),
                    AF.Copy, scale=scl.ap(0, 128, 10, 1)))
                B.wait("vector", self.sc, self.sc.n)
                B.inc("vector", self.vc, lambda E: E.tensor_scalar(
                    h16.ap(), h16.ap(), RNE_C, RNE_C, OP.add, OP.subtract))
                B.wait("vector", self.vc, self.vc.n)
                B.inc("vector", self.vc, lambda E: E.tensor_scalar(
                    h16.ap(), h16.ap(), 127.0, -128.0, OP.min, OP.max))
                B.wait("vector", self.vc, self.vc.n)
                B.inc("vector", self.vc, lambda E, ic=ic, hqc=hqc: E.tensor_copy(
                    hqc.ap(0, 128, ic * 512, 512), h16.ap()))
                hq_vc[ic] = self.vc.n
            B.wait("tensor", self.vc, self.vc.n)
            for dc in range(NCH):
                parx = par_ld[dc & 1]
                if xst_at.get((np_, dc - 2)) is not None:
                    B.wait("sync", self.vc, xst_at[(np_, dc - 2)])
                B.wait("sync", parx, parx.n)
                B.inc("sync", parx, lambda E, dc=dc, np_=np_: E.dma_start(
                    out=xstage.ap(0, 128, (dc & 1) * S + np_ * 512, 512),
                    in_=dram_ap(self.xspill, dc * 128 * S + np_ * 512, [[S, 128], [1, 512]])), 16)

                def mms_d(pso, dc=dc, hqc=hqc):
                    for ic in range(NIC):
                        fn = lambda E, pso=pso, dc=dc, ic=ic, hqc=hqc: E.matmul(
                            pso.ap(0, 128, 0, 512),
                            wdqT.ap(0, 128, (ic * 8 + dc) * 128, 128),
                            hqc.ap(0, 128, ic * 512, 512),
                            start=(ic == 0), stop=(ic == NIC - 1))
                        if ic == NIC - 1:
                            B.inc("tensor", self.mm, fn)
                        else:
                            B.run("tensor", fn)

                def ev_d(pso, dc=dc):
                    B.wait("scalar", self.vc, self.vc.n)
                    B.inc("scalar", self.sc, lambda E, pso=pso, dc=dc: E.activation(
                        tst2.ap(), pso.ap(0, 128, 0, 512),
                        AF.Identity, scale=swg.ap(0, 128, 88 + dc, 1)))
                piece(mms_d, ev_d)
                B.wait("vector", self.sc, self.sc.n)
                B.wait("vector", parx, parx.n)
                if st2_at.get((np_, dc - 2)) is not None:
                    B.wait("vector", st2_at[(np_, dc - 2)][0], st2_at[(np_, dc - 2)][1])
                B.inc("vector", self.vc, lambda E: E.tensor_scalar(
                    tst2.ap(), tst2.ap(), scl.ap(0, 128, 9, 1), None, OP.mult))
                B.wait("vector", self.vc, self.vc.n)
                B.inc("vector", self.vc, lambda E, dc=dc, np_=np_: E.tensor_tensor(
                    rst2.ap(0, 128, (dc & 1) * S + np_ * 512, 512), tst2.ap(),
                    xstage.ap(0, 128, (dc & 1) * S + np_ * 512, 512), OP.add))
                xst_at[(np_, dc)] = self.vc.n
                stp = self.sta if dc % 2 == 0 else self.stb
                B.wait("gpsimd", self.vc, self.vc.n)
                B.wait("gpsimd", stp, stp.n)
                B.inc("gpsimd", stp, lambda E, dc=dc, np_=np_: E.dma_start(
                    out=dram_ap(self.xspill2, dc * 128 * S + np_ * 512, [[S, 128], [1, 512]]),
                    in_=rst2.ap(0, 128, (dc & 1) * S + np_ * 512, 512)), 16)
                st2_at[(np_, dc)] = (stp, stp.n)
            hq_mm[np_ & 1] = self.mm.n
        B.wait("gpsimd", self.sta, self.sta.n)
        B.wait("gpsimd", self.stb, self.stb.n)
        B.wait("gpsimd", self.vc, self.vc.n)
        B.wait("gpsimd", self.mm, self.mm.n)
        B.wait("gpsimd", self.sc, self.sc.n)
        self.barrier()
        s_t.close()
        s_f.close()
        self.alloc_xf()
        for c in range(NCH):
            B.inc("gpsimd", self.gl, lambda E, c=c: E.dma_start(
                out=self.xf.ap(0, 128, c * S, S),
                in_=dram_ap(self.xspill2, c * 128 * S, [[S, 128], [1, S]])), 16)
        B.wait("vector", self.gl, self.gl.n)
        B.wait("scalar", self.gl, self.gl.n)
        B.wait("tensor", self.gl, self.gl.n)
        self.ln_phase()

    def _bf_regather(self, srcbuf, ap_a):
        B = self.B
        hi, lo = ap_a
        sapA = type(srcbuf._base)(srcbuf._base.tensor, srcbuf.off,
                                  [[srcbuf.pitch, 32], list(hi), [3, 2], list(lo)])
        sapB = type(srcbuf._base)(srcbuf._base.tensor, srcbuf.off + 1,
                                  [[srcbuf.pitch, 32], list(hi), [1, 2], list(lo)])
        B.run("vector", lambda E, sapA=sapA: E.tensor_copy(
            self.ta.apd(0, [[self.ta.pitch, 32], [1, 128]]), sapA))
        B.inc("vector", self.vc, lambda E, sapB=sapB: E.tensor_copy(
            self.ta.apd(128, [[self.ta.pitch, 32], [1, 128]]), sapB))

    def _grid_change(self, src, dst, N, fwd):
        B = self.B
        W = 32 * N
        B.wait("sync", self.ev, self.ev.n)
        B.wait("sync", self.ld, self.ld.n)
        wsap = src.ap(0, 128, 0, W)
        if fwd:
            wdap = dram_ap(self.gcscr, 0, [[W, 4], [N, 32], [4 * W, 32], [1, N]])
        else:
            wdap = dram_ap(self.gcscr, 0, [[N, 32], [32 * W, 4], [W, 32], [1, N]])
        B.inc("sync", self.ld, lambda E, wsap=wsap, wdap=wdap: E.dma_start(out=wdap, in_=wsap), 16)
        B.wait("sync", self.ld, self.ld.n)
        rs = dst.ap(0, 128, 0, W)
        rd = dram_ap(self.gcscr, 0, [[W, 128], [1, W]])
        B.inc("sync", self.ld, lambda E, rs=rs, rd=rd: E.dma_start(out=rs, in_=rd), 16)
        B.wait("tensor", self.ld, self.ld.n)

    def _wcon_pipe(self, groups, evacs, tq, xstage, rstage, wcbuf):
        B = self.B
        NB = 4
        QT = 512
        B.wait("tensor", self.ev, self.ev.n)
        evcon2_at = {}
        mm_at = {}
        for g, pairs in enumerate(groups):
            m = g
            par = self.lda if m % 2 == 0 else self.ldb
            if m >= 2:
                B.wait("sync", self.ev, evcon2_at[m - 2])
                B.wait("sync", self.mm, mm_at[m - 2])
            B.wait("sync", par, par.n)
            B.inc("sync", par, lambda E, m=m, tq=tq, xstage=xstage: E.dma_start(
                out=xstage.ap(0, 128, (m % 2) * QT, QT),
                in_=dram_ap(self.xspill, m * 128 * S + tq * QT, [[S, 128], [1, QT]])), 16)
            B.inc("sync", par, lambda E, m=m, wcbuf=wcbuf: E.dma_start(
                out=wcbuf[m % 2].ap(),
                in_=dram_ap(self.wcscr, m * 128 * 32 * 128, [[32 * 128, 128], [1, 32 * 128]])), 16)
            pt = self.pa4[g % NB]
            bank = pt.offset // 512
            prev = self.bank_ev.get(bank)
            if prev:
                B.wait("tensor", self.ev, prev)
            B.wait("tensor", par, par.n)
            n = len(pairs)
            for i, (lh, rh) in enumerate(pairs):
                st_ = (i == 0)
                if i == n - 1:
                    B.inc("tensor", self.mm,
                          lambda E, lh=lh, rh=rh, pt=pt, st_=st_: E.matmul(pt, lh, rh, start=st_, stop=True))
                else:
                    B.run("tensor",
                          lambda E, lh=lh, rh=rh, pt=pt, st_=st_: E.matmul(pt, lh, rh, start=st_, stop=False))
            mm_at[m] = self.mm.n
            B.wait("vector", self.mm, self.mm.n)
            spar = self.sta if m % 2 == 0 else self.stb
            fns = evacs[g]
            B.wait("vector", spar, spar.n)
            B.wait("vector", self.ev, self.ev.n)
            B.inc("vector", self.vc, lambda E, f=fns[0], pt=pt: f(E, pt))
            B.wait("vector", self.vc, self.vc.n)
            B.wait("vector", par, par.n)
            B.inc("vector", self.ev, lambda E, f=fns[1], pt=pt: f(E, pt))
            evcon2_at[m] = self.ev.n
            self.bank_ev[bank] = self.ev.n
            B.wait("gpsimd", self.ev, self.ev.n)
            B.inc("gpsimd", spar, lambda E, m=m, tq=tq, rstage=rstage: E.dma_start(
                out=dram_ap(self.xspill2, m * 128 * S + tq * QT, [[S, 128], [1, QT]]),
                in_=rstage.ap(0, 128, (m % 2) * QT, QT)), 16)

    def emit_output(self):
        B = self.B
        s_o = ExitStack()
        otok = B.sbuf("otok", 128, D, F32, stack=s_o)
        for t in range(NT):
            for hf in range(2):
                if t > 0 or hf > 0:
                    B.wait("tensor", self.vc, self.vc.n)
                for j in range(4):
                    c = hf * 4 + j
                    fn = lambda E, c=c, j=j, hf=hf, t=t: E.transpose(
                        self.ps_a.ap(0, 128, hf * 1024 + j * 128, 128),
                        self.xf.ap(0, 128, c * S + t * 128, 128), self.eye32.ap())
                    if j == 3:
                        B.inc("tensor", self.mm, fn)
                    else:
                        B.run("tensor", fn)
                B.wait("vector", self.mm, self.mm.n)
                if t > 0:
                    B.wait("vector", self.st, self.st.n)
                B.inc("vector", self.vc, lambda E, hf=hf, otok=otok: E.tensor_copy(
                    otok.apd(hf * 512, [[D, 128], [128, 4], [1, 128]]),
                    self.ps_a.apd(hf * 1024, [[2048, 128], [128, 4], [1, 128]])))
            B.wait("gpsimd", self.vc, self.vc.n)
            B.inc("gpsimd", self.st, lambda E, t=t, otok=otok: E.dma_start(
                out=dram_ap(self.y_out, t * 128 * D, [[D, 128], [1, D]]), in_=otok.ap()), 16)
        B.wait("gpsimd", self.st, self.st.n)
        s_o.close()

    def build(self):
        self.alloc_xf()
        self.phase0()
        if self.n_stages >= 1:
            self.l0_attn()
        if self.n_stages >= 2:
            self.l0_ffn()
        if self.n_stages >= 3:
            self.l1_attn()
        if self.n_stages >= 4:
            self.l1_ffn()
        self.emit_output()
        self.s_xf.close()
        self.B.finalize()
        return self.nc


def build_program(n_stages=N_STAGES):
    return Prog(n_stages).build()


def _gsel_const():
    g = np.zeros((44, 23), dtype=ml_dtypes.bfloat16)
    for j in range(11):
        for tau in range(4):
            o = (1 << j) * tau
            slot = 4 * j + tau
            if o >= S:
                g[slot, 22] = 1.0
            else:
                g[slot, L1_OFFS.index(o)] = 1.0
    return g


def _stc_const():
    st = np.zeros((23, S), dtype=ml_dtypes.bfloat16)
    n = np.arange(S)
    for i, o in enumerate(L1_OFFS):
        st[i, n < o] = 1.0
    st[22] = 1.0
    return st


def _pmc_const():
    pm = np.zeros((128, 7 * 128), dtype=ml_dtypes.bfloat16)
    for i in range(7):
        st = 1 << i
        p = np.arange(128)
        pm[p, i * 128 + (p ^ st)] = 1.0
    return pm


def kernel(**inputs):
    nc = build_program(N_STAGES)
    x = np.ascontiguousarray(inputs["L0_attn_x"], dtype=np.float32)
    common = {
        "l0a_wq": np.ascontiguousarray(inputs["L0_attn_W_q"], dtype=np.float32),
        "l0a_wk": np.ascontiguousarray(inputs["L0_attn_W_k"], dtype=np.float32),
        "l0a_wv": np.ascontiguousarray(inputs["L0_attn_W_v"], dtype=np.float32),
        "l0a_wo": np.ascontiguousarray(inputs["L0_attn_W_o"], dtype=np.float32),
        "l0a_wdkv": np.ascontiguousarray(inputs["L0_attn_W_down_kv"], dtype=np.float32),
        "l0a_wupk": np.ascontiguousarray(inputs["L0_attn_W_up_k"], dtype=np.float32),
        "l0a_wupv": np.ascontiguousarray(inputs["L0_attn_W_up_v"], dtype=np.float32),
        "l0f_wexp": np.ascontiguousarray(inputs["L0_ffn_W_exp"], dtype=np.float32),
        "l0f_wcon": np.ascontiguousarray(inputs["L0_ffn_W_con"], dtype=np.float32),
        "l0f_bexp": np.ascontiguousarray(inputs["L0_ffn_b_exp"], dtype=np.float32).reshape(1, 4096),
        "l0f_upb": np.ascontiguousarray(inputs["L0_ffn_up_bias"], dtype=np.float32).reshape(1, 4096),
        "l0f_bcon": np.ascontiguousarray(inputs["L0_ffn_b_con"], dtype=np.float32).reshape(1, 1024),
        "l0f_dnb": np.ascontiguousarray(inputs["L0_ffn_down_bias"], dtype=np.float32).reshape(1, 1024),
        "l0f_upw": np.ascontiguousarray(inputs["L0_ffn_up_weights"], dtype=np.float32).reshape(12, 8192),
        "l0f_dnw": np.ascontiguousarray(inputs["L0_ffn_down_weights"], dtype=np.float32).reshape(12, 8192),
        "l0f_pmc": _pmc_const(),
        "l1a_wqkv": np.ascontiguousarray(inputs["L1_attn_w_qkv"], dtype=np.float32),
        "l1a_bqkv": np.ascontiguousarray(inputs["L1_attn_b_qkv"], dtype=np.float32).reshape(1, 3072),
        "l1a_wout": np.ascontiguousarray(inputs["L1_attn_w_out"], dtype=np.float32),
        "l1a_bout": np.ascontiguousarray(inputs["L1_attn_b_out"], dtype=np.float32).reshape(1, 1024),
        "l1a_wgate": np.ascontiguousarray(inputs["L1_attn_w_gate"], dtype=np.float32),
        "l1a_bgate": np.ascontiguousarray(inputs["L1_attn_b_gate"], dtype=np.float32).reshape(1, 1024),
        "l1a_pb": np.ascontiguousarray(inputs["L1_attn_pos_bias"], dtype=np.float32),
        "l1a_gsel": _gsel_const(),
        "l1a_stc": _stc_const(),
        "l1f_wg": np.ascontiguousarray(inputs["L1_ffn_w_gate"], dtype=np.float32),
        "l1f_wu": np.ascontiguousarray(inputs["L1_ffn_w_up"], dtype=np.float32),
        "l1f_wd": np.ascontiguousarray(inputs["L1_ffn_w_down"], dtype=np.float32),
        "l1f_clipx": np.ascontiguousarray(inputs["L1_ffn_clip_x"], dtype=np.float32).reshape(1, 1),
        "l1f_clipd": np.ascontiguousarray(inputs["L1_ffn_clip_down"], dtype=np.float32).reshape(1, 1),
    }
    in_maps = [dict(common, x=x[b]) for b in range(8)]
    res = run_bass_kernel_spmd(nc, in_maps, core_ids=list(range(8)))
    return np.stack([res.results[b]["y"] for b in range(8)], axis=0)


# version 07
# speedup vs baseline: 1.0483x; 1.0087x over previous
import math
import numpy as np
import ml_dtypes
from contextlib import ExitStack

import concourse.bass as bass
import concourse.mybir as mybir
from concourse.bass_utils import run_bass_kernel_spmd
from concourse.alu_op_type import AluOpType as OP

F32 = mybir.dt.float32
BF16 = mybir.dt.bfloat16
FP16 = mybir.dt.float16
I8 = mybir.dt.int8
U8 = mybir.dt.uint8
AF = mybir.ActivationFunctionType
AX = mybir.AxisListType

S = 2048
D = 1024
H = 16
HD = 64
DL = 256
NCH = 8
NT = 16
TB = 512
NTB = S // TB
LAM = 0.01
LN32 = math.log(math.sqrt(D))
EPS = 1e-5

ESIZE = {F32: 4, BF16: 2, FP16: 2, I8: 1, U8: 1}

L1_OFFS = [0, 1, 2, 3, 4, 6, 8, 12, 16, 24, 32, 48, 64, 96,
           128, 192, 256, 384, 512, 768, 1024, 1536]
L1_BAND = 1664
L1_WROW = 1792
RNE_C = 12582912.0

ENGINES = ("sync", "gpsimd", "tensor", "vector", "scalar")

N_STAGES = 4


class Sem:
    def __init__(self, h):
        self.h = h
        self.n = 0


class Buf:
    def __init__(self, parts, fs, base_ap, pitch=None, off=0):
        self.parts = parts
        self.fs = fs
        self.pitch = fs if pitch is None else pitch
        self.off = off
        self._base = base_ap

    def ap(self, p0=0, np_=None, f0=0, nf=None):
        np_ = (self.parts - p0) if np_ is None else np_
        nf = (self.fs - f0) if nf is None else nf
        return type(self._base)(self._base.tensor, p0 * self.pitch + self.off + f0,
                                [[self.pitch, np_], [1, nf]])

    def apd(self, offset, dims):
        return type(self._base)(self._base.tensor, self.off + offset, dims)


class Builder:
    def __init__(self, nc):
        self.nc = nc
        self.prog = {e: [] for e in ENGINES}
        self.es = ExitStack()
        self._i = 0

    def sem(self, name=None):
        h = self.es.enter_context(self.nc.semaphore(name or f"s{self._i}"))
        self._i += 1
        return Sem(h)

    def sbuf(self, name, parts, fs, dt, stack=None):
        h = (stack or self.es).enter_context(self.nc.sbuf_tensor(name, [parts, fs], dt))
        return Buf(parts, fs, h.ap())

    def psum(self, name, parts, fs, dt):
        h = self.es.enter_context(self.nc.psum_tensor(name, [parts, fs], dt))
        return Buf(parts, fs, h.ap())

    def em(self, e, f):
        self.prog[e].append(f)

    def wait(self, e, s, v=None):
        v = s.n if v is None else v
        if v > 0:
            self.em(e, lambda E, h=s.h, v=v: E.wait_ge(h, v))

    def run(self, e, f):
        self.em(e, f)

    def inc(self, e, s, f, k=1):
        s.n += k
        self.em(e, lambda E, f=f, h=s.h, k=k: f(E).then_inc(h, k))

    def barrier(self, pb, dummies=None):
        for e in ENGINES:
            pb.n += 1
            if dummies and e in dummies:
                self.em(e, lambda E, h=pb.h, f=dummies[e]: f(E).then_inc(h, 1))
            else:
                self.em(e, lambda E, h=pb.h: E.nop().then_inc(h, 1))
        tgt = pb.n
        for e in ENGINES:
            self.em(e, lambda E, h=pb.h, v=tgt: E.wait_ge(h, v))

    def finalize(self):
        with self.nc.Block() as block:
            for name in ENGINES:
                dec = getattr(block, name)
                prog = self.prog[name]

                def body(E, prog=prog):
                    for f in prog:
                        f(E)
                dec(body)
        self.es.close()


def dram_ap(t, offset, dims):
    base = t if hasattr(t, "tensor") else t.ap()
    return type(base)(base.tensor, offset, dims)


class Prog:
    def __init__(self, n_stages):
        self.n_stages = n_stages
        nc = bass.Bass("TRN2", target_bir_lowering=False)
        self.nc = nc
        B = Builder(nc)
        self.B = B

        self.x_in = nc.dram_tensor("x", [S, D], F32, kind="ExternalInput").ap()
        self.y_out = nc.dram_tensor("y", [S, D], F32, kind="ExternalOutput").ap()
        self.win = {}
        for nm, shp in [
            ("l0a_wq", [D, D]), ("l0a_wk", [D, D]), ("l0a_wv", [D, D]),
            ("l0a_wo", [D, D]), ("l0a_wdkv", [D, DL]),
            ("l0a_wupk", [DL, D]), ("l0a_wupv", [DL, D]),
            ("l0f_wexp", [D, 4 * D]), ("l0f_wcon", [4 * D, D]),
            ("l0f_bexp", [1, 4 * D]), ("l0f_upb", [1, 4 * D]),
            ("l0f_bcon", [1, D]), ("l0f_dnb", [1, D]),
            ("l0f_upw", [12, 8192]), ("l0f_dnw", [12, 8192]),
        ]:
            self.win[nm] = nc.dram_tensor(nm, shp, F32, kind="ExternalInput").ap()
        self.win["l0f_pmc"] = nc.dram_tensor("l0f_pmc", [128, 7 * 128], BF16,
                                             kind="ExternalInput").ap()
        for nm, shp in [
            ("l1a_wqkv", [D, 3 * D]), ("l1a_bqkv", [1, 3 * D]),
            ("l1a_wout", [D, D]), ("l1a_bout", [1, D]),
            ("l1a_wgate", [D, D]), ("l1a_bgate", [1, D]),
            ("l1a_pb", [44, H]),
        ]:
            self.win[nm] = nc.dram_tensor(nm, shp, F32, kind="ExternalInput").ap()
        self.win["l1a_gsel"] = nc.dram_tensor("l1a_gsel", [44, 23], BF16,
                                              kind="ExternalInput").ap()
        self.win["l1a_stc"] = nc.dram_tensor("l1a_stc", [23, S], BF16,
                                             kind="ExternalInput").ap()
        self.c2scr = nc.dram_tensor("c2scr", [H, S], BF16)
        for nm, shp in [
            ("l1f_wg", [2816, D]), ("l1f_wu", [2816, D]), ("l1f_wd", [D, 2816]),
            ("l1f_clipx", [1, 1]), ("l1f_clipd", [1, 1]),
        ]:
            self.win[nm] = nc.dram_tensor(nm, shp, F32, kind="ExternalInput").ap()
        self.hspill = nc.dram_tensor("hspill", [2816, S], F32)
        self.ccxi = nc.dram_tensor("ccxi", [1, 1], F32)
        self.ccxo = nc.dram_tensor("ccxo", [1, 1], F32)
        self.cchi = nc.dram_tensor("cchi", [1, 1], F32)
        self.ccho = nc.dram_tensor("ccho", [1, 1], F32)
        self.wrowscr = nc.dram_tensor("wrowscr", [H, L1_WROW], BF16)
        self.wscr23 = nc.dram_tensor("wscr23", [23 * 16], BF16)
        self.zscr2 = nc.dram_tensor("zscr2", [H, S], BF16)
        self.rowscr = nc.dram_tensor("rowscr", [8, S], F32)
        self.zscr = nc.dram_tensor("zscr", [2, H * S], BF16)
        self.xspill = nc.dram_tensor("xspill", [NCH * 128, S], F32)
        self.xspill2 = nc.dram_tensor("xspill2", [NCH * 128, S], F32)
        self.wescr = nc.dram_tensor("wescr", [8, 128 * 8 * 512], BF16)
        self.gcscr = nc.dram_tensor("gcscr", [128, 32 * 512], BF16)
        self.wcscr = nc.dram_tensor("wcscr", [8, 128 * 32 * 128], BF16)

        self.PB = B.sem("pb")
        self.ld = B.sem("ld")
        self.gl = B.sem("gl")
        self.mm = B.sem("mm")
        self.ev = B.sem("ev")
        self.vc = B.sem("vc")
        self.sc = B.sem("sc")
        self.gc = B.sem("gc")
        self.xs = B.sem("xs")
        self.pv = B.sem("pv")
        self.st = B.sem("st")
        self.lda = B.sem("lda")
        self.ldb = B.sem("ldb")
        self.gb = B.sem("gb")
        self.sta = B.sem("sta")
        self.stb = B.sem("stb")
        self.gwa = B.sem("gwa")
        self.gwb = B.sem("gwb")
        self.ccs = B.sem("ccs")
        self.wsa = B.sem("wsa")
        self.wsb = B.sem("wsb")

        self.xbf = B.sbuf("xbf", 128, NCH * S, BF16)
        self.eye32 = B.sbuf("eye32", 128, 128, F32)
        self.maskU = B.sbuf("masku", 128, 128, U8)
        self.pidx = B.sbuf("pidx", 128, 1, F32)
        self.fidx = B.sbuf("fidx", 128, 128, F32)
        self.bias1 = B.sbuf("bias1", 1, 4, F32)
        self.ones128 = B.sbuf("ones128", 128, 1, BF16)
        self.dume = B.sbuf("dume", 1, 4, F32)
        self.ta = B.sbuf("ta", 32, 256, F32)

        self.ps_a = B.psum("ps_a", 128, 2048, F32)
        self.ps_scr = B.psum("ps_scr", 128, 512, F32)
        self.ps_o1 = B.psum("ps_o1", 128, 512, F32)
        self.ps_o2 = B.psum("ps_o2", 128, 512, F32)
        self.ps_s = B.psum("ps_s", 128, 512, F32)
        self.pa4 = [self.ps_a.ap(0, 128, b * 512, 512) for b in range(4)]
        self._lni = 0
        self._xfi = 0
        self.bank_ev = {}

    def carve(self, byte_off, parts, nf, dt):
        bc = self.xf._base.bitcast(dt)
        row = bc.ap[0][0]
        return Buf(parts, nf, bc, pitch=row, off=byte_off // ESIZE[dt])

    def alloc_xf(self):
        self.s_xf = ExitStack()
        self.xf = self.B.sbuf(f"xf{self._xfi}", 128, NCH * S, F32, stack=self.s_xf)
        self._xfi += 1

    def barrier(self):
        B = self.B
        dummies = {
            "tensor": lambda E: E.matmul(self.ps_s.ap(0, 1, 0, 1),
                                         self.ones128.ap(0, 1, 0, 1),
                                         self.ones128.ap(0, 1, 0, 1), start=True, stop=True),
            "vector": lambda E: E.memset(self.dume.ap(0, 1, 0, 1), 0.0),
            "scalar": lambda E: E.activation(self.dume.ap(0, 1, 1, 1), self.dume.ap(0, 1, 1, 1), AF.Copy),
            "gpsimd": lambda E: E.memset(self.dume.ap(0, 1, 2, 1), 0.0),
        }
        B.wait("gpsimd", self.gc, self.gc.n)
        B.wait("gpsimd", self.gl, self.gl.n)
        B.wait("gpsimd", self.gb, self.gb.n)
        B.wait("gpsimd", self.st, self.st.n)
        B.wait("gpsimd", self.sta, self.sta.n)
        B.wait("gpsimd", self.stb, self.stb.n)
        B.wait("gpsimd", self.gwa, self.gwa.n)
        B.wait("gpsimd", self.gwb, self.gwb.n)
        B.wait("sync", self.ld, self.ld.n)
        B.wait("sync", self.lda, self.lda.n)
        B.wait("sync", self.ldb, self.ldb.n)
        B.wait("sync", self.wsa, self.wsa.n)
        B.wait("sync", self.wsb, self.wsb.n)
        B.barrier(self.PB, dummies)

    def mm_pipe(self, groups, psum_aps, evacs, evac_engine="scalar"):
        B = self.B
        NB = len(psum_aps)
        B.wait("tensor", self.ev, self.ev.n)
        for g, pairs in enumerate(groups):
            pt = psum_aps[g % NB]
            bank = pt.offset // 512
            prev = self.bank_ev.get(bank)
            if prev:
                B.wait("tensor", self.ev, prev)
            n = len(pairs)
            for i, (lh, rh) in enumerate(pairs):
                st_ = (i == 0)
                if i == n - 1:
                    B.inc("tensor", self.mm,
                          lambda E, lh=lh, rh=rh, pt=pt, st_=st_: E.matmul(pt, lh, rh, start=st_, stop=True))
                else:
                    B.run("tensor",
                          lambda E, lh=lh, rh=rh, pt=pt, st_=st_: E.matmul(pt, lh, rh, start=st_, stop=False))
            B.wait(evac_engine, self.mm)
            fns = evacs[g]
            for j, f in enumerate(fns):
                if j == len(fns) - 1:
                    B.inc(evac_engine, self.ev, lambda E, f=f, pt=pt: f(E, pt))
                else:
                    B.run(evac_engine, lambda E, f=f, pt=pt: f(E, pt))
            self.bank_ev[bank] = self.ev.n

    def cast_load_w(self, dst, dram, rows, cols):
        B = self.B
        for c in range(rows // 128):
            B.inc("gpsimd", self.gl, lambda E, c=c, dst=dst, dram=dram, cols=cols: E.dma_start(
                out=dst.ap(0, 128, c * cols, cols),
                in_=dram_ap(dram, c * 128 * cols, [[cols, 128], [1, cols]])), 16)

    def row_bcast(self, dram_off, n_elems, dst):
        B = self.B
        B.inc("sync", self.ld, lambda E, dram_off=dram_off, n_elems=n_elems, dst=dst: E.dma_start(
            out=dst.ap(0, 128, 0, n_elems),
            in_=dram_ap(self.rowscr, dram_off, [[0, 128], [1, n_elems]])), 16)

    def phase0(self):
        B = self.B
        s_p0 = ExitStack()
        xtok = [B.sbuf(f"xtok{i}", 128, D, F32, stack=s_p0) for i in range(2)]
        B.inc("gpsimd", self.gc, lambda E: E.iota(self.pidx.ap(), [[0, 1]], base=0,
                                         channel_multiplier=1, allow_small_or_imprecise_dtypes=True))
        B.inc("gpsimd", self.gc, lambda E: E.iota(self.fidx.ap(), [[1, 128]], base=0,
                                         channel_multiplier=0, allow_small_or_imprecise_dtypes=True))
        B.wait("vector", self.gc, 2)
        B.run("vector", lambda E: E.tensor_scalar(self.eye32.ap(), self.fidx.ap(), self.pidx.ap(), None, OP.is_equal))
        B.run("vector", lambda E: E.tensor_scalar(self.maskU.ap(), self.fidx.ap(), self.pidx.ap(), None, OP.is_lt))
        B.run("vector", lambda E: E.memset(self.dume.ap(0, 1, 1, 1), 0.0))
        B.run("vector", lambda E: E.memset(self.bias1.ap(0, 1, 0, 1), -LN32))
        B.run("vector", lambda E: E.memset(self.bias1.ap(0, 1, 1, 1), EPS))
        B.inc("vector", self.vc, lambda E: E.memset(self.ones128.ap(), 1.0))

        for t in range(NT):
            buf = xtok[t % 2]
            par = self.lda if t % 2 == 0 else self.ldb
            if t >= 2:
                B.wait("sync", self.vc, self.vc.n)
            B.inc("sync", par, lambda E, buf=buf, t=t: E.dma_start(
                out=buf.ap(), in_=dram_ap(self.x_in, t * 128 * D, [[D, 128], [1, D]])), 16)
            B.wait("tensor", par, par.n)
            if t == 0:
                B.wait("tensor", self.vc, 1)
            for hf in range(2):
                if t > 0 or hf > 0:
                    B.wait("tensor", self.vc, self.vc.n)
                for j in range(4):
                    fn = lambda E, buf=buf, hf=hf, j=j: E.transpose(
                        self.ps_a.ap(0, 128, hf * 1024 + j * 128, 128),
                        buf.ap(0, 128, (hf * 4 + j) * 128, 128), self.eye32.ap())
                    if j == 3:
                        B.inc("tensor", self.mm, fn)
                    else:
                        B.run("tensor", fn)
                B.wait("vector", self.mm, self.mm.n)

                def evx(E, t=t, hf=hf):
                    srcap = self.ps_a.apd(hf * 1024, [[2048, 128], [128, 4], [1, 128]])
                    dstap = self.xf.apd(4 * hf * S + t * 128, [[NCH * S, 128], [S, 4], [1, 128]])
                    return E.tensor_copy(dstap, srcap)
                B.inc("vector", self.vc, evx)
        B.wait("scalar", self.vc, self.vc.n)
        for c in range(NCH):
            B.inc("scalar", self.sc, lambda E, c=c: E.activation(
                self.xbf.ap(0, 128, c * S, S), self.xf.ap(0, 128, c * S, S), AF.Copy))
        self.barrier()
        s_p0.close()

    def ln_phase(self):
        B = self.B
        self._lni += 1
        s_ln = ExitStack()
        sqb = B.sbuf(f"sqb{self._lni}", 128, S, BF16, stack=s_ln)
        repa = B.sbuf(f"repa{self._lni}", 128, S, F32, stack=s_ln)
        repb = B.sbuf(f"repb{self._lni}", 128, S, F32, stack=s_ln)
        srow = B.sbuf(f"srow{self._lni}", 1, 4 * S, F32, stack=s_ln)
        for c in range(NCH):
            if c > 0:
                B.wait("scalar", self.mm, self.mm.n)
            B.inc("scalar", self.sc, lambda E, c=c: E.activation(
                self.xbf.ap(0, 128, c * S, S), self.xf.ap(0, 128, c * S, S), AF.Copy))
            B.inc("scalar", self.sc, lambda E, c=c, sqb=sqb: E.activation(
                sqb.ap(), self.xf.ap(0, 128, c * S, S), AF.Square))
            B.wait("tensor", self.sc, self.sc.n)
            for tb in range(NTB):
                B.run("tensor", lambda E, c=c, tb=tb: E.matmul(
                    self.ps_a.ap(0, 1, tb * TB, TB), self.ones128.ap(),
                    self.xbf.ap(0, 128, c * S + tb * TB, TB),
                    start=(c == 0), stop=(c == NCH - 1)))
                fn = lambda E, c=c, tb=tb, sqb=sqb: E.matmul(
                    self.ps_a.ap(32, 1, tb * TB, TB), self.ones128.ap(),
                    sqb.ap(0, 128, tb * TB, TB),
                    start=(c == 0), stop=(c == NCH - 1))
                if tb == NTB - 1:
                    B.inc("tensor", self.mm, fn)
                else:
                    B.run("tensor", fn)
        sr = srow
        B.wait("vector", self.mm, self.mm.n)
        B.inc("vector", self.vc, lambda E, sr=sr: E.tensor_scalar(
            sr.ap(0, 1, 2 * S, S), self.ps_a.apd(0, [[2048, 1], [512, 4], [1, 512]]),
            1.0 / D, None, OP.mult))
        B.inc("vector", self.vc, lambda E, sr=sr: E.tensor_scalar(
            sr.ap(0, 1, 3 * S, S), self.ps_a.apd(32 * 2048, [[2048, 1], [512, 4], [1, 512]]),
            1.0 / D, None, OP.mult))
        B.wait("vector", self.vc, self.vc.n)
        B.inc("vector", self.vc, lambda E, sr=sr: E.tensor_tensor(
            sr.ap(0, 1, 0, S), sr.ap(0, 1, 2 * S, S), sr.ap(0, 1, 2 * S, S), OP.mult))
        B.wait("vector", self.vc, self.vc.n)
        B.inc("vector", self.vc, lambda E, sr=sr: E.tensor_tensor(
            sr.ap(0, 1, 3 * S, S), sr.ap(0, 1, 3 * S, S), sr.ap(0, 1, 0, S), OP.subtract))
        B.wait("scalar", self.vc, self.vc.n)
        B.inc("scalar", self.sc, lambda E, sr=sr: E.activation(
            sr.ap(0, 1, 0, S), sr.ap(0, 1, 3 * S, S), AF.Sqrt,
            bias=self.bias1.ap(0, 1, 1, 1)))
        B.wait("vector", self.sc, self.sc.n)
        B.inc("vector", self.vc, lambda E, sr=sr: E.reciprocal(sr.ap(0, 1, 0, S), sr.ap(0, 1, 0, S)))
        B.wait("vector", self.vc, self.vc.n)
        B.inc("vector", self.vc, lambda E, sr=sr: E.tensor_tensor(
            sr.ap(0, 1, 1 * S, S), sr.ap(0, 1, 2 * S, S), sr.ap(0, 1, 0, S), OP.mult))
        B.wait("vector", self.vc, self.vc.n)
        B.inc("vector", self.vc, lambda E, sr=sr: E.tensor_scalar(
            sr.ap(0, 1, 1 * S, S), sr.ap(0, 1, 1 * S, S), -1.0, None, OP.mult))
        B.wait("gpsimd", self.vc, self.vc.n)
        B.inc("gpsimd", self.gb, lambda E, sr=sr: E.dma_start(
            out=dram_ap(self.rowscr, 0, [[1, 2 * S]]), in_=sr.ap(0, 1, 0, 2 * S)), 16)
        B.wait("sync", self.gb, self.gb.n)
        self.row_bcast(0, S, repa)
        self.row_bcast(S, S, repb)
        B.wait("vector", self.ld, self.ld.n)
        for c in range(NCH):
            B.inc("vector", self.vc, lambda E, c=c, repa=repa: E.tensor_tensor(
                self.xf.ap(0, 128, c * S, S), self.xf.ap(0, 128, c * S, S),
                repa.ap(), OP.mult))
            B.wait("vector", self.vc, self.vc.n)
            B.inc("vector", self.vc, lambda E, c=c, repb=repb: E.tensor_tensor(
                self.xf.ap(0, 128, c * S, S), self.xf.ap(0, 128, c * S, S),
                repb.ap(), OP.add))
            B.wait("scalar", self.vc, self.vc.n)
            B.inc("scalar", self.sc, lambda E, c=c: E.activation(
                self.xbf.ap(0, 128, c * S, S), self.xf.ap(0, 128, c * S, S), AF.Copy))
        self.barrier()
        s_ln.close()

    def l0_attn(self):
        B = self.B
        s_ot = ExitStack()
        OTu = B.sbuf("otu", 128, NCH * S, BF16, stack=s_ot)
        s1 = ExitStack()
        wupk = B.sbuf("wupk", 128, 2 * D, BF16, stack=s1)
        wupv = B.sbuf("wupv", 128, 2 * D, BF16, stack=s1)
        etile = [B.sbuf(f"etile{i}", 128, S, BF16, stack=s1) for i in range(4)]
        latT = B.sbuf("latT", 128, 2 * S, BF16, stack=s1)
        self.cast_load_w(wupk, self.win["l0a_wupk"], DL, D)
        self.cast_load_w(wupv, self.win["l0a_wupv"], DL, D)

        s_ef = ExitStack()
        irow = B.sbuf("irow", 1, S, F32, stack=s_ef)
        erow = B.sbuf("erow", 1, S, F32, stack=s_ef)
        etf = B.sbuf("etf", 128, S, F32, stack=s_ef)
        B.inc("gpsimd", self.gc, lambda E, irow=irow: E.iota(irow.ap(0, 1, 0, S), [[1, S]], base=0,
                                                    channel_multiplier=0, allow_small_or_imprecise_dtypes=True))
        B.wait("scalar", self.vc, 1)
        B.wait("scalar", self.gc, self.gc.n)
        for i, (sgn, use_b) in enumerate([(+1, True), (-1, True), (+1, False), (-1, False)]):
            if i > 0:
                B.wait("scalar", self.gb, self.gb.n)
            if use_b:
                B.inc("scalar", self.sc, lambda E, sgn=sgn, erow=erow: E.activation(
                    erow.ap(0, 1, 0, S), irow.ap(0, 1, 0, S), AF.Exp,
                    bias=self.bias1.ap(0, 1, 0, 1), scale=sgn * LAM))
            else:
                B.inc("scalar", self.sc, lambda E, sgn=sgn, erow=erow: E.activation(
                    erow.ap(0, 1, 0, S), irow.ap(0, 1, 0, S), AF.Exp, scale=sgn * LAM))
            B.wait("gpsimd", self.sc, self.sc.n)
            B.inc("gpsimd", self.gb, lambda E, i=i, erow=erow: E.dma_start(
                out=dram_ap(self.rowscr, 2 * S + i * S, [[1, S]]), in_=erow.ap(0, 1, 0, S)), 16)
        B.wait("sync", self.gb, self.gb.n)
        for i in range(4):
            self.row_bcast(2 * S + i * S, S, etf)
            B.wait("vector", self.ld, self.ld.n)
            B.inc("vector", self.vc, lambda E, i=i, etile=etile, etf=etf: E.tensor_copy(
                etile[i].ap(), etf.ap()))
            B.wait("sync", self.vc, self.vc.n)
        self.barrier()
        s_ef.close()

        s_w1 = ExitStack()
        wk = B.sbuf("wk", 128, NCH * D, BF16, stack=s_w1)
        wv = B.sbuf("wv", 128, NCH * D, BF16, stack=s_w1)
        wdkv = B.sbuf("wdkv", 128, NCH * DL, BF16, stack=s_w1)
        self.cast_load_w(wk, self.win["l0a_wk"], D, D)
        self.cast_load_w(wv, self.win["l0a_wv"], D, D)
        self.cast_load_w(wdkv, self.win["l0a_wdkv"], D, DL)
        s_kv = ExitStack()
        QTB = 256
        kQ = B.sbuf("kQ", 128, NCH * QTB, BF16, stack=s_kv)
        vQ = B.sbuf("vQ", 128, NCH * QTB, BF16, stack=s_kv)
        B.wait("tensor", self.gl, self.gl.n)
        for tq in range(S // QTB):
            tok0 = tq * QTB
            for dstbuf, wbuf in ((kQ, wk), (vQ, wv)):
                groups, evacs = [], []
                for m in range(NCH):
                    groups.append([(wbuf.ap(0, 128, c * D + m * 128, 128),
                                    self.xbf.ap(0, 128, c * S + tok0, QTB)) for c in range(NCH)])
                    evacs.append([lambda E, pt, dstbuf=dstbuf, m=m: E.activation(
                        dstbuf.ap(0, 128, m * QTB, QTB), pt, AF.Copy)])
                self.mm_pipe(groups, [self.ps_a.ap(0, 128, b * 512, QTB) for b in range(4)], evacs)
            groups, evacs = [], []
            for m in range(2):
                pairs = [(wdkv.ap(0, 128, c * DL + m * 128, 128),
                          kQ.ap(0, 128, c * QTB, QTB)) for c in range(NCH)]
                pairs += [(wdkv.ap(0, 128, c * DL + m * 128, 128),
                           vQ.ap(0, 128, c * QTB, QTB)) for c in range(NCH)]
                groups.append(pairs)
                evacs.append([lambda E, pt, m=m, tok0=tok0, latT=latT: E.activation(
                    latT.ap(0, 128, m * S + tok0, QTB), pt, AF.Copy)])
            self.mm_pipe(groups, [self.ps_a.ap(0, 128, b * 512, QTB) for b in range(4)], evacs)
        self.barrier()
        s_kv.close()
        s_w1.close()

        s_qT = ExitStack()
        qT = B.sbuf("qT", 128, 4 * S, BF16, stack=s_qT)
        wq = B.sbuf("wq", 128, 4 * D, BF16, stack=s_qT)
        zrow = B.sbuf("zrowr", 65, 1024, BF16, stack=s_qT)

        for c in range(NCH):
            B.inc("gpsimd", self.gl, lambda E, c=c: E.dma_start(
                out=dram_ap(self.xspill, c * 128 * S, [[S, 128], [1, S]]),
                in_=self.xf.ap(0, 128, c * S, S)), 16)
        B.wait("gpsimd", self.gl, self.gl.n)
        B.inc("gpsimd", self.gc, lambda E: E.nop())
        for e in ENGINES:
            B.wait(e, self.gc, self.gc.n)

        KB = 1024
        krecT = self.carve(0, 128, 4 * S, BF16)
        vaug = self.carve(16 * KB, 128, NT * KB, BF16)
        k1c = self.carve(48 * KB, 128, S, BF16)
        k2c = self.carve(52 * KB, 128, S, BF16)
        q1b = self.carve(56 * KB, 128, TB, BF16)
        q2b = self.carve(57 * KB, 128, TB, BF16)
        ptile = [self.carve(58 * KB, 128, KB, BF16), self.carve(60 * KB, 128, KB, BF16)]

        for t in range(NT):
            B.run("vector", lambda E, t=t, vaug=vaug: E.memset(
                vaug.apd(t * KB + 64, [[vaug.pitch, 128], [256, 4], [1, 64]]), 1.0))
            fn = lambda E, t=t, vaug=vaug: E.memset(
                vaug.apd(t * KB + 128, [[vaug.pitch, 128], [256, 4], [1, 64]]), 1.0)
            if t == NT - 1:
                B.inc("vector", self.vc, fn)
            else:
                B.run("vector", fn)

        for half in range(2):
            B.wait("gpsimd", self.mm, self.mm.n)
            for c in range(NCH):
                B.inc("gpsimd", self.gl, lambda E, c=c, half=half, wq=wq: E.dma_start(
                    out=wq.ap(0, 128, c * 512, 512),
                    in_=dram_ap(self.win["l0a_wq"], c * 128 * D + half * 512, [[D, 128], [1, 512]])), 16)
            B.wait("tensor", self.gl, self.gl.n)
            groups, evacs = [], []
            for m in range(4):
                for tb in range(NTB):
                    groups.append([(wq.ap(0, 128, c * 512 + m * 128, 128),
                                    self.xbf.ap(0, 128, c * S + tb * TB, TB)) for c in range(NCH)])
                    evacs.append([lambda E, pt, m=m, tb=tb, qT=qT: E.activation(
                        qT.ap(0, 128, m * S + tb * TB, TB), pt, AF.Copy)])
            self.mm_pipe(groups, self.pa4, evacs)
            groups, evacs = [], []
            for m in range(4):
                C = 4 * half + m
                for tb in range(NTB):
                    groups.append([(wupk.ap(0, 128, dl * D + C * 128, 128),
                                    latT.ap(0, 128, dl * S + tb * TB, TB)) for dl in range(2)])
                    evacs.append([lambda E, pt, m=m, tb=tb, krecT=krecT: E.activation(
                        krecT.ap(0, 128, m * S + tb * TB, TB), pt, AF.Copy)])
            self.mm_pipe(groups, self.pa4, evacs)

            B.wait("tensor", self.vc, self.vc.n)
            groups, evacs = [], []
            for t in range(NT):
                groups.append([(latT.ap(0, 128, dl * S + t * 128, 128),
                                wupv.ap(0, 128, dl * D + half * 512, 512)) for dl in range(2)])

                def ev_even(E, pt, t=t, vaug=vaug):
                    src = type(pt)(pt.tensor, pt.offset, [[2048, 128], [128, 4], [1, 64]])
                    dst = vaug.apd(t * KB, [[vaug.pitch, 128], [256, 4], [1, 64]])
                    return E.tensor_copy(dst, src)

                def ev_odd(E, pt, t=t, vaug=vaug):
                    src = type(pt)(pt.tensor, pt.offset + 64, [[2048, 128], [128, 4], [1, 64]])
                    dst = vaug.apd(t * KB + 192, [[vaug.pitch, 128], [256, 4], [1, 64]])
                    return E.tensor_copy(dst, src)
                evacs.append([ev_even, ev_odd])
            self.mm_pipe(groups, self.pa4, evacs, evac_engine="vector")

            for mc in range(4):
                C = 4 * half + mc
                B.wait("vector", self.ev, self.ev.n)
                B.wait("vector", self.pv, self.pv.n)
                B.run("vector", lambda E, mc=mc, k1c=k1c, krecT=krecT, etile=etile: E.tensor_tensor(
                    k1c.ap(), krecT.ap(0, 128, mc * S, S), etile[0].ap(), OP.mult))
                B.inc("vector", self.vc, lambda E, mc=mc, k2c=k2c, krecT=krecT, etile=etile: E.tensor_tensor(
                    k2c.ap(), krecT.ap(0, 128, mc * S, S), etile[1].ap(), OP.mult))
                for h2 in range(2):
                    hg = 2 * C + h2
                    pr = 64 * h2
                    hh = hg & 7
                    for qb in range(NTB):
                        B.wait("vector", self.pv, self.pv.n)
                        B.run("vector", lambda E, pr=pr, mc=mc, qb=qb, q1b=q1b, qT=qT, etile=etile: E.tensor_tensor(
                            q1b.ap(pr, 64, 0, TB), qT.ap(pr, 64, mc * S + qb * TB, TB),
                            etile[3].ap(pr, 64, qb * TB, TB), OP.mult))
                        B.inc("vector", self.vc, lambda E, pr=pr, mc=mc, qb=qb, q2b=q2b, qT=qT, etile=etile: E.tensor_tensor(
                            q2b.ap(pr, 64, 0, TB), qT.ap(pr, 64, mc * S + qb * TB, TB),
                            etile[2].ap(pr, 64, qb * TB, TB), OP.mult))
                        B.wait("tensor", self.vc, self.vc.n)
                        xs_base = self.xs.n
                        pv_base = self.pv.n
                        for g in range(8):
                            hf = g % 2
                            if g >= 2:
                                B.wait("tensor", self.xs, xs_base + (g - 1))
                            diag_slots = []
                            mmq = []
                            for kk in range(2):
                                kc = 2 * g + kk
                                kcr = kc - 4 * qb
                                pslice = self.ps_a.ap(0, 128, hf * 1024 + kk * 512, 512)
                                if kcr < 0:
                                    mmq.append((None, lambda E, pr=pr, kc=kc, pslice=pslice, k1c=k1c, q1b=q1b: E.matmul(
                                        pslice, k1c.ap(pr, 64, kc * 128, 128), q1b.ap(pr, 64, 0, TB),
                                        start=True, stop=True)))
                                elif kcr > 3:
                                    mmq.append((None, lambda E, pr=pr, kc=kc, pslice=pslice, k2c=k2c, q2b=q2b: E.matmul(
                                        pslice, k2c.ap(pr, 64, kc * 128, 128), q2b.ap(pr, 64, 0, TB),
                                        start=True, stop=True)))
                                else:
                                    for j in range(4):
                                        psub = self.ps_a.ap(0, 128, hf * 1024 + kk * 512 + j * 128, 128)
                                        if kcr < j:
                                            mmq.append((None, lambda E, pr=pr, kc=kc, j=j, psub=psub, k1c=k1c, q1b=q1b: E.matmul(
                                                psub, k1c.ap(pr, 64, kc * 128, 128),
                                                q1b.ap(pr, 64, j * 128, 128), start=True, stop=True)))
                                        elif kcr > j:
                                            mmq.append((None, lambda E, pr=pr, kc=kc, j=j, psub=psub, k2c=k2c, q2b=q2b: E.matmul(
                                                psub, k2c.ap(pr, 64, kc * 128, 128),
                                                q2b.ap(pr, 64, j * 128, 128), start=True, stop=True)))
                                        else:
                                            nds = len(diag_slots)
                                            mmq.append(("dwait", None))
                                            mmq.append((None, lambda E, pr=pr, kc=kc, j=j, psub=psub, k1c=k1c, q1b=q1b: E.matmul(
                                                psub, k1c.ap(pr, 64, kc * 128, 128),
                                                q1b.ap(pr, 64, j * 128, 128), start=True, stop=True)))
                                            mmq.append((None, lambda E, pr=pr, kc=kc, j=j, nds=nds, k2c=k2c, q2b=q2b: E.matmul(
                                                self.ps_scr.ap(0, 128, nds * 128, 128),
                                                k2c.ap(pr, 64, kc * 128, 128),
                                                q2b.ap(pr, 64, j * 128, 128), start=True, stop=True)))
                                            diag_slots.append(psub)
                            last_i = max(i for i, (tag, f) in enumerate(mmq) if f is not None)
                            for i, (tag, f) in enumerate(mmq):
                                if tag == "dwait":
                                    if self.vc.n > 0:
                                        B.wait("tensor", self.vc, self.vc.n)
                                elif i == last_i:
                                    B.inc("tensor", self.mm, f)
                                else:
                                    B.run("tensor", f)
                            if diag_slots:
                                B.wait("vector", self.mm, self.mm.n)
                                for nds, psub in enumerate(diag_slots):
                                    B.inc("vector", self.vc, lambda E, nds=nds, psub=psub: E.copy_predicated(
                                        psub, self.maskU.ap(), self.ps_scr.ap(0, 128, nds * 128, 128)))
                                B.wait("scalar", self.vc, self.vc.n)
                            else:
                                B.wait("scalar", self.mm, self.mm.n)
                            if g >= 2:
                                B.wait("scalar", self.pv, pv_base + (g - 1))
                            B.inc("scalar", self.xs, lambda E, hf=hf, ptile=ptile: E.activation(
                                ptile[hf].ap(), self.ps_a.ap(0, 128, hf * 1024, 1024), AF.Exp))
                            B.wait("tensor", self.xs, self.xs.n)
                            for kk in range(2):
                                kc = 2 * g + kk
                                pso = self.ps_o1 if h2 == 0 else self.ps_o2
                                if h2 == 0:
                                    lh = vaug.ap(0, 128, kc * KB + hh * 128, 65)
                                else:
                                    lh = vaug.ap(0, 128, kc * KB + hh * 128, 128)
                                nout = 65 if h2 == 0 else 128
                                if kk == 1:
                                    B.inc("tensor", self.pv, lambda E, lh=lh, hf=hf, kk=kk, pso=pso, kc=kc, nout=nout, ptile=ptile: E.matmul(
                                        pso.ap(0, nout, 0, 512), lh, ptile[hf].ap(0, 128, kk * 512, 512),
                                        start=(kc == 0), stop=(kc == 15)))
                                else:
                                    B.run("tensor", lambda E, lh=lh, hf=hf, kk=kk, pso=pso, kc=kc, nout=nout, ptile=ptile: E.matmul(
                                        pso.ap(0, nout, 0, 512), lh, ptile[hf].ap(0, 128, kk * 512, 512),
                                        start=(kc == 0), stop=(kc == 15)))
                        pso = self.ps_o1 if h2 == 0 else self.ps_o2
                        zslot = qb % 2
                        zpar = self.sta if zslot == 0 else self.stb
                        B.wait("vector", self.pv, self.pv.n)
                        B.wait("vector", zpar, zpar.n)
                        zr = 64 if h2 == 0 else 0
                        B.inc("vector", self.vc, lambda E, pso=pso, zslot=zslot, zr=zr, zrow=zrow: E.tensor_copy(
                            zrow.ap(zr, 1, zslot * 512, 512), pso.ap(zr, 1, 0, 512)))
                        B.wait("scalar", self.pv, self.pv.n)
                        if h2 == 0:
                            B.inc("scalar", self.sc, lambda E, pso=pso, C=C, qb=qb, OTu=OTu: E.activation(
                                OTu.ap(0, 64, C * S + qb * TB, TB), pso.ap(0, 64, 0, 512), AF.Copy))
                            zoff = hg // 2 * S + qb * 512
                        else:
                            B.inc("scalar", self.sc, lambda E, pso=pso, C=C, qb=qb, OTu=OTu: E.activation(
                                OTu.ap(64, 64, C * S + qb * TB, TB), pso.ap(64, 64, 0, 512), AF.Copy))
                            zoff = H * S + hg // 2 * S + qb * 512
                        B.wait("gpsimd", self.vc, self.vc.n)
                        B.inc("gpsimd", zpar, lambda E, zr=zr, zslot=zslot, zoff=zoff, zrow=zrow: E.dma_start(
                            out=dram_ap(self.zscr, zoff, [[1, 512]]),
                            in_=zrow.ap(zr, 1, zslot * 512, 512)), 16)
                        B.wait("tensor", self.sc, self.sc.n)
                        B.wait("tensor", self.vc, self.vc.n)
        self.barrier()
        s_qT.close()
        s1.close()

        s_n = ExitStack()
        zrep = B.sbuf("zrep", 128, S, BF16, stack=s_n)
        rzt = B.sbuf("rzt", 128, S, F32, stack=s_n)
        wo = B.sbuf("wo", 128, NCH * D, BF16, stack=s_n)
        B.wait("sync", self.sta, self.sta.n)
        B.wait("sync", self.stb, self.stb.n)
        for C in range(NCH):
            B.inc("sync", self.ld, lambda E, C=C, zrep=zrep: E.dma_start(
                out=zrep.ap(), in_=dram_ap(self.zscr, C * S, [[H * S, 2], [0, 64], [1, S]])), 16)
            B.wait("vector", self.ld, self.ld.n)
            B.inc("vector", self.vc, lambda E, zrep=zrep, rzt=rzt: E.reciprocal(rzt.ap(), zrep.ap()))
            B.wait("vector", self.vc, self.vc.n)
            B.inc("vector", self.vc, lambda E, C=C, OTu=OTu, rzt=rzt: E.tensor_tensor(
                OTu.ap(0, 128, C * S, S), OTu.ap(0, 128, C * S, S), rzt.ap(), OP.mult))
            B.wait("sync", self.vc, self.vc.n)

        B.wait("gpsimd", self.vc, self.vc.n)
        for c in range(NCH):
            B.inc("gpsimd", self.gl, lambda E, c=c: E.dma_start(
                out=self.xf.ap(0, 128, c * S, S),
                in_=dram_ap(self.xspill, c * 128 * S, [[S, 128], [1, S]])), 16)
        self.cast_load_w(wo, self.win["l0a_wo"], D, D)
        B.wait("tensor", self.gl, self.gl.n)
        B.wait("tensor", self.vc, self.vc.n)
        groups, evacs = [], []
        for m in range(NCH):
            for tb in range(NTB):
                groups.append([(wo.ap(0, 128, c * D + m * 128, 128),
                                OTu.ap(0, 128, c * S + tb * TB, TB)) for c in range(NCH)])
                evacs.append([lambda E, pt, m=m, tb=tb: E.tensor_tensor(
                    self.xf.ap(0, 128, m * S + tb * TB, TB), pt,
                    self.xf.ap(0, 128, m * S + tb * TB, TB), OP.add)])
        self.mm_pipe(groups, self.pa4, evacs, evac_engine="vector")
        self.barrier()
        s_n.close()
        s_ot.close()

        self.ln_phase()


    def l0_ffn(self):
        B = self.B
        FD = 4 * D
        NFC = 32
        QT = 512
        for c in range(NCH):
            B.inc("gpsimd", self.gl, lambda E, c=c: E.dma_start(
                out=dram_ap(self.xspill, c * 128 * S, [[S, 128], [1, S]]),
                in_=self.xf.ap(0, 128, c * S, S)), 16)
        B.wait("gpsimd", self.gl, self.gl.n)
        B.inc("gpsimd", self.gc, lambda E: E.nop())
        for e in ENGINES:
            B.wait(e, self.gc, self.gc.n)
        self.s_xf.close()

        s_f = ExitStack()
        bsets = [B.sbuf(f"bset{i}", 128, NFC * 128, BF16, stack=s_f) for i in range(4)]
        pmats = B.sbuf("pmats", 128, 7 * 128, BF16, stack=s_f)
        eyebf = B.sbuf("eyebf", 128, 128, BF16, stack=s_f)
        vab = B.sbuf("vab", 128, 64, BF16, stack=s_f)
        wsl = B.sbuf("wsl", 32, 512, F32, stack=s_f)
        bexpL1 = B.sbuf("bexpL1", 128, 32, F32, stack=s_f)
        upbL2 = B.sbuf("upbL2", 128, 32, F32, stack=s_f)
        b2t = B.sbuf("b2t", 128, 16, F32, stack=s_f)
        pcol = B.sbuf("pcol", 128, 4, F32, stack=s_f)
        xstage = B.sbuf("xstage", 128, 2 * QT, F32, stack=s_f)
        rstage = B.sbuf("rstage", 128, 2 * QT, F32, stack=s_f)
        tstage = B.sbuf("tstage", 128, QT, F32, stack=s_f)
        s_fb = ExitStack()
        tsc = B.sbuf("tsc", 128, NFC * 128, BF16, stack=s_fb)
        da_all = B.sbuf("da_all", 128, NFC * 128, BF16, stack=s_fb)
        db_all = B.sbuf("db_all", 128, NFC * 128, BF16, stack=s_fb)
        ws2 = B.sbuf("ws2", 32, 8068, F32, stack=s_fb)

        STRIDES = [1, 2, 4, 8, 16, 32, 64]
        PIDX = {st: i for i, st in enumerate(STRIDES)}

        B.inc("sync", self.ld, lambda E: E.dma_start(
            out=wsl.ap(0, 32, 0, 128), in_=dram_ap(self.win["l0f_bexp"], 0, [[128, 32], [1, 128]])), 16)
        B.inc("sync", self.ld, lambda E: E.dma_start(
            out=upbL2.ap(0, 128, 0, 32), in_=dram_ap(self.win["l0f_upb"], 0, [[32, 128], [1, 32]])), 16)
        B.inc("sync", self.ld, lambda E: E.dma_start(
            out=wsl.ap(0, 8, 128, 128), in_=dram_ap(self.win["l0f_bcon"], 0, [[128, 8], [1, 128]])), 16)
        B.inc("sync", self.ld, lambda E: E.dma_start(
            out=wsl.ap(0, 8, 256, 128), in_=dram_ap(self.win["l0f_dnb"], 0, [[128, 8], [1, 128]])), 16)
        B.wait("tensor", self.ld, self.ld.n)
        B.wait("tensor", self.ev, self.ev.n)
        B.run("tensor", lambda E: E.transpose(
            self.ps_scr.ap(0, 128, 0, 32), wsl.ap(0, 32, 0, 128), self.eye32.ap(0, 32, 0, 32)))
        B.run("tensor", lambda E: E.transpose(
            self.ps_scr.ap(0, 128, 32, 8), wsl.ap(0, 8, 128, 128), self.eye32.ap(0, 8, 0, 8)))
        B.inc("tensor", self.mm, lambda E: E.transpose(
            self.ps_scr.ap(0, 128, 40, 8), wsl.ap(0, 8, 256, 128), self.eye32.ap(0, 8, 0, 8)))
        B.wait("vector", self.mm, self.mm.n)
        B.run("vector", lambda E: E.tensor_copy(bexpL1.ap(), self.ps_scr.ap(0, 128, 0, 32)))
        B.inc("vector", self.vc, lambda E: E.tensor_copy(
            b2t.ap(0, 128, 0, 8), self.ps_scr.ap(0, 128, 32, 8)))
        B.wait("vector", self.vc, self.vc.n)
        B.inc("vector", self.vc, lambda E: E.tensor_tensor(
            b2t.ap(0, 128, 0, 8), b2t.ap(0, 128, 0, 8), self.ps_scr.ap(0, 128, 40, 8), OP.add))

        B.run("vector", lambda E: E.tensor_scalar(eyebf.ap(), self.fidx.ap(), self.pidx.ap(), None, OP.is_equal))
        B.wait("sync", self.ld, self.ld.n)
        B.inc("sync", self.ld, lambda E: E.dma_start(
            out=pmats.ap(), in_=dram_ap(self.win["l0f_pmc"], 0, [[7 * 128, 128], [1, 7 * 128]])), 16)
        B.wait("vector", self.ld, self.ld.n)

        for set_i, (wdram, stages, grid) in enumerate([
                (self.win["l0f_upw"], list(range(7)), "L1"),
                (self.win["l0f_upw"], list(range(7, 12)), "L2"),
                (self.win["l0f_dnw"], list(range(7)), "L1"),
                (self.win["l0f_dnw"], list(range(7, 12)), "L2")]):
            dest = bsets[set_i]
            n_st = len(stages)
            for idx, s in enumerate(reversed(stages)):
                B.wait("sync", self.vc, self.vc.n)
                B.wait("sync", self.ld, self.ld.n)
                if grid == "L1":
                    B.inc("sync", self.ld, lambda E, s=s, wdram=wdram: E.dma_start(
                        out=wsl.ap(0, 32, 0, 256),
                        in_=dram_ap(wdram, s * 8192, [[256, 32], [1, 256]])), 16)
                    stt = 2 ** s
                    ap_a = ([4 * stt, 2 ** (6 - s)], [4, stt])
                else:
                    B.inc("sync", self.ld, lambda E, s=s, wdram=wdram, ws2=ws2: E.dma_start(
                        out=ws2.ap(0, 32, 0, 8068),
                        in_=dram_ap(wdram, s * 8192, [[4, 32], [1, 8068]])), 16)
                    stt = 2 ** s
                    ap_a = ([4 * stt, 2 ** (11 - s)], [128, 2 ** (s - 5)])
                B.wait("vector", self.ld, self.ld.n)
                srcbuf = wsl if grid == "L1" else ws2
                self._bf_regather(srcbuf, ap_a)
                B.wait("tensor", self.vc, self.vc.n)
                B.wait("tensor", self.ev, self.ev.n)
                B.run("tensor", lambda E: E.transpose(
                    self.ps_scr.ap(0, 128, 64, 32), self.ta.ap(0, 32, 0, 128), self.eye32.ap(0, 32, 0, 32)))
                B.inc("tensor", self.mm, lambda E: E.transpose(
                    self.ps_scr.ap(0, 128, 96, 32), self.ta.ap(0, 32, 128, 128), self.eye32.ap(0, 32, 0, 32)))
                B.wait("vector", self.mm, self.mm.n)
                B.inc("vector", self.vc, lambda E, vab=vab: E.tensor_copy(
                    vab.ap(0, 128, 0, 64), self.ps_scr.ap(0, 128, 64, 64)))
                B.wait("vector", self.vc, self.vc.n)
                eye_rep = eyebf.apd(0, [[128, 128], [0, 32], [1, 128]])
                va_rep = vab.apd(0, [[64, 128], [1, 32], [0, 128]])
                pm = pmats.apd(PIDX[stt if grid == "L1" else stt // 32] * 128,
                               [[7 * 128, 128], [0, 32], [1, 128]])
                vb_rep = vab.apd(32, [[64, 128], [1, 32], [0, 128]])
                da_ap = da_all.apd(0, [[NFC * 128, 128], [128, 32], [1, 128]])
                db_ap = db_all.apd(0, [[NFC * 128, 128], [128, 32], [1, 128]])
                B.run("vector", lambda E, eye_rep=eye_rep, va_rep=va_rep, da_ap=da_ap: E.tensor_tensor(
                    da_ap, eye_rep, va_rep, OP.mult))
                B.inc("vector", self.vc, lambda E, pm=pm, vb_rep=vb_rep, db_ap=db_ap: E.tensor_tensor(
                    db_ap, pm, vb_rep, OP.mult))
                B.wait("tensor", self.vc, self.vc.n)
                B.wait("tensor", self.ev, self.ev.n)
                out_is_dest = ((n_st - 1 - idx) % 2 == 0)
                outbuf = dest if out_is_dest else tsc
                if idx == 0:
                    rhsbuf = None
                else:
                    rhsbuf = tsc if out_is_dest else dest
                stage_ev = self.ev.n
                for c in range(NFC):
                    slot = c % 16
                    if c >= 16 and c % 4 == 0:
                        B.wait("tensor", self.ev, stage_ev + (c // 4 - 4) + 1)
                    if rhsbuf is None:
                        rhs = eyebf.ap()
                    else:
                        rhs = rhsbuf.ap(0, 128, c * 128, 128)
                    B.run("tensor", lambda E, c=c, slot=slot, da_all=da_all, rhs=rhs: E.matmul(
                        self.ps_a.ap(0, 128, slot * 128, 128),
                        da_all.ap(0, 128, c * 128, 128), rhs, start=True, stop=False))
                    fn = lambda E, c=c, slot=slot, db_all=db_all, rhs=rhs: E.matmul(
                        self.ps_a.ap(0, 128, slot * 128, 128),
                        db_all.ap(0, 128, c * 128, 128), rhs, start=False, stop=True)
                    if c % 4 == 3:
                        B.inc("tensor", self.mm, fn)
                        B.wait("vector", self.mm, self.mm.n)
                        grp = c // 4
                        B.inc("vector", self.ev, lambda E, grp=grp, outbuf=outbuf: E.tensor_copy(
                            outbuf.ap(0, 128, grp * 512, 512),
                            self.ps_a.ap(0, 128, (grp % 4) * 512, 512)))
                    else:
                        B.run("tensor", fn)
        B.wait("tensor", self.ev, self.ev.n)
        self.barrier()
        s_fb.close()

        s_ws = ExitStack()
        tmpw = [B.sbuf(f"tmpw{i}", 128, FD, BF16, stack=s_ws) for i in range(2)]
        gw = [self.gwa, self.gwb]
        stw = [self.wsa, self.wsb]
        for c in range(NCH):
            par = gw[c % 2]
            spar = stw[c % 2]
            buf = tmpw[c % 2]
            B.wait("gpsimd", spar, spar.n)
            B.wait("gpsimd", par, par.n)
            B.inc("gpsimd", par, lambda E, c=c, buf=buf: E.dma_start(
                out=buf.ap(), in_=dram_ap(self.win["l0f_wexp"], c * 128 * FD, [[FD, 128], [1, FD]])), 16)
            B.wait("sync", par, par.n)
            B.wait("sync", spar, spar.n)
            B.inc("sync", spar, lambda E, c=c, buf=buf: E.dma_start(
                out=dram_ap(self.wescr, c * 512, [[8 * 512, 128], [128 * 8 * 512, 8], [1, 512]]),
                in_=buf.ap(0, 128, 0, FD)), 16)
        for fc in range(NFC):
            par = gw[fc % 2]
            spar = stw[fc % 2]
            buf = tmpw[fc % 2]
            B.wait("gpsimd", spar, spar.n)
            B.wait("gpsimd", par, par.n)
            B.inc("gpsimd", par, lambda E, fc=fc, buf=buf: E.dma_start(
                out=buf.ap(0, 128, 0, D), in_=dram_ap(self.win["l0f_wcon"], fc * D, [[32 * D, 128], [1, D]])), 16)
            B.wait("sync", par, par.n)
            B.wait("sync", spar, spar.n)
            B.inc("sync", spar, lambda E, fc=fc, buf=buf: E.dma_start(
                out=dram_ap(self.wcscr, fc * 128, [[32 * 128, 128], [128 * 32 * 128, 8], [1, 128]]),
                in_=buf.ap(0, 128, 0, D)), 16)
        B.wait("sync", self.wsa, self.wsa.n)
        B.wait("sync", self.wsb, self.wsb.n)
        B.wait("gpsimd", self.gwa, self.gwa.n)
        B.wait("gpsimd", self.gwb, self.gwb.n)
        self.barrier()
        s_ws.close()

        s_fh = ExitStack()
        hA = B.sbuf("hA", 128, NFC * QT, BF16, stack=s_fh)
        hB = B.sbuf("hB", 128, NFC * QT, BF16, stack=s_fh)
        webuf = [B.sbuf(f"webuf{i}", 128, 8 * 512, BF16, stack=s_fh) for i in range(2)]
        wcbuf = [B.sbuf(f"wcbuf{i}", 128, NFC * 128, BF16, stack=s_fh) for i in range(2)]

        for tq in range(4):
            tok0 = tq * QT
            for g8 in range(8):
                buf = webuf[g8 % 2]
                B.wait("sync", self.mm, self.mm.n)
                spar = self.lda if g8 % 2 == 0 else self.ldb
                B.wait("sync", spar, spar.n)
                B.inc("sync", spar, lambda E, g8=g8, buf=buf: E.dma_start(
                    out=buf.ap(), in_=dram_ap(self.wescr, g8 * 128 * 8 * 512, [[8 * 512, 128], [1, 8 * 512]])), 16)
                B.wait("tensor", spar, spar.n)
                groups, evacs = [], []
                for i4 in range(4):
                    fc = g8 * 4 + i4
                    groups.append([(buf.ap(0, 128, c * 512 + i4 * 128, 128),
                                    self.xbf.ap(0, 128, c * S + tok0, QT)) for c in range(NCH)])
                    evacs.append([lambda E, pt, fc=fc, hA=hA: E.activation(
                        hA.ap(0, 128, fc * QT, QT), pt, AF.Identity, bias=bexpL1.ap(0, 128, fc, 1))])
                self.mm_pipe(groups, self.pa4, evacs)
            groups, evacs = [], []
            for fc in range(NFC):
                groups.append([(bsets[0].ap(0, 128, fc * 128, 128), hA.ap(0, 128, fc * QT, QT))])
                evacs.append([lambda E, pt, fc=fc, hA=hA: E.tensor_copy(
                    hA.ap(0, 128, fc * QT, QT), pt)])
            self.mm_pipe(groups, self.pa4, evacs, evac_engine="vector")
            self._grid_change(hA, hB, QT, fwd=True)
            groups, evacs = [], []
            for fc in range(NFC):
                groups.append([(bsets[1].ap(0, 128, fc * 128, 128), hB.ap(0, 128, fc * QT, QT))])
                evacs.append([lambda E, pt, fc=fc, hB=hB: E.activation(
                    hB.ap(0, 128, fc * QT, QT), pt, AF.Gelu, bias=upbL2.ap(0, 128, fc, 1))])
            self.mm_pipe(groups, self.pa4, evacs)
            self._grid_change(hB, hA, QT, fwd=False)
            groups, evacs = [], []
            for fc in range(NFC):
                groups.append([(bsets[2].ap(0, 128, fc * 128, 128), hA.ap(0, 128, fc * QT, QT))])
                evacs.append([lambda E, pt, fc=fc, hA=hA: E.tensor_copy(
                    hA.ap(0, 128, fc * QT, QT), pt)])
            self.mm_pipe(groups, self.pa4, evacs, evac_engine="vector")
            self._grid_change(hA, hB, QT, fwd=True)
            groups, evacs = [], []
            for fc in range(NFC):
                groups.append([(bsets[3].ap(0, 128, fc * 128, 128), hB.ap(0, 128, fc * QT, QT))])
                evacs.append([lambda E, pt, fc=fc, hB=hB: E.tensor_copy(
                    hB.ap(0, 128, fc * QT, QT), pt)])
            self.mm_pipe(groups, self.pa4, evacs, evac_engine="vector")
            groups, evacs = [], []
            for m in range(NCH):
                buf = wcbuf[m % 2]
                groups.append([(buf.ap(0, 128, fc * 128, 128),
                                hB.ap(0, 128, fc * QT, QT)) for fc in range(NFC)])

                def ev_con(E, pt, m=m, tstage=tstage):
                    return E.tensor_scalar(tstage.ap(0, 128, 0, QT), pt,
                                           b2t.ap(0, 128, m, 1), None, OP.add)

                def ev_con2(E, pt, m=m, xstage=xstage, rstage=rstage, tstage=tstage):
                    return E.tensor_tensor(rstage.ap(0, 128, (m % 2) * QT, QT),
                                           tstage.ap(0, 128, 0, QT),
                                           xstage.ap(0, 128, (m % 2) * QT, QT), OP.add)
                evacs.append([ev_con, ev_con2])
            self._wcon_pipe(groups, evacs, tq, xstage, rstage, wcbuf)
        B.wait("gpsimd", self.sta, self.sta.n)
        B.wait("gpsimd", self.stb, self.stb.n)
        B.wait("gpsimd", self.vc, self.vc.n)
        self.barrier()
        s_fh.close()
        s_f.close()
        self.alloc_xf()
        for c in range(NCH):
            B.inc("gpsimd", self.gl, lambda E, c=c: E.dma_start(
                out=self.xf.ap(0, 128, c * S, S),
                in_=dram_ap(self.xspill2, c * 128 * S, [[S, 128], [1, S]])), 16)
        B.wait("vector", self.gl, self.gl.n)
        B.wait("scalar", self.gl, self.gl.n)
        B.wait("tensor", self.gl, self.gl.n)
        self.ln_phase()

    def l1_attn(self):
        B = self.B
        AFE = AF.Exp
        for c in range(NCH):
            B.inc("gpsimd", self.gl, lambda E, c=c: E.dma_start(
                out=dram_ap(self.xspill, c * 128 * S, [[S, 128], [1, S]]),
                in_=self.xf.ap(0, 128, c * S, S)), 16)
        B.wait("gpsimd", self.gl, self.gl.n)
        B.inc("gpsimd", self.gc, lambda E: E.nop())
        for e in ENGINES:
            B.wait(e, self.gc, self.gc.n)
        self.s_xf.close()

        s_a = ExitStack()
        ogbf = B.sbuf("ogbf", 128, NCH * S, BF16, stack=s_a)
        qpair = B.sbuf("qpair", 128, S, BF16, stack=s_a)
        kpair = B.sbuf("kpair", 128, S, BF16, stack=s_a)
        gatep = B.sbuf("gatep", 128, S, BF16, stack=s_a)
        vtmp = B.sbuf("vtmp", 128, S, F32, stack=s_a)
        vtokp = B.sbuf("vtokp", 128, 16 * 130, BF16, stack=s_a)
        gathp = B.sbuf("gathp", 128, S, BF16, stack=s_a)
        wbuf = [B.sbuf(f"l1wb{i}", 128, 1024, BF16, stack=s_a) for i in range(2)]
        mkraw = B.sbuf("mkraw", 128, L1_BAND, BF16, stack=s_a)
        mk2 = B.sbuf("mk2", 128, L1_BAND, BF16, stack=s_a)
        ptil = [B.sbuf(f"ptil{i}", 128, S, BF16, stack=s_a) for i in range(2)]
        zsb = B.sbuf("zsb", 1, 2 * S, F32, stack=s_a)
        zrep = B.sbuf("zrep2", 128, S, BF16, stack=s_a)
        rzf = B.sbuf("rzf", 128, S, F32, stack=s_a)
        og1 = B.sbuf("og1", 128, S, F32, stack=s_a)
        c2l = B.sbuf("c2l", 1, 2 * S, BF16, stack=s_a)
        bqt = B.sbuf("bqt", 128, 32, F32, stack=s_a)
        bot = B.sbuf("bot", 128, 8, F32, stack=s_a)
        revb = B.sbuf("revb", 128, 128, BF16, stack=s_a)
        p127 = B.sbuf("p127", 128, 1, F32, stack=s_a)
        s_prep = ExitStack()
        pbx = B.sbuf("pbx", 44, 16, F32, stack=s_prep)
        epbb = B.sbuf("epbb", 44, 16, BF16, stack=s_prep)
        gselb = B.sbuf("gselb", 44, 23, BF16, stack=s_prep)
        wsb = B.sbuf("wsb", 23, 16, F32, stack=s_prep)
        wsbb = B.sbuf("wsbb", 23, 16, BF16, stack=s_prep)
        stcb = B.sbuf("stcb", 23, S, BF16, stack=s_prep)
        mz = B.sbuf("mz", 16, L1_WROW, BF16, stack=s_prep)

        B.wait("sync", self.ld, self.ld.n)
        B.inc("sync", self.ld, lambda E: E.dma_start(
            out=pbx.ap(), in_=dram_ap(self.win["l1a_pb"], 0, [[16, 44], [1, 16]])), 16)
        B.wait("sync", self.ld, self.ld.n)
        B.inc("sync", self.ld, lambda E: E.dma_start(
            out=gselb.ap(), in_=dram_ap(self.win["l1a_gsel"], 0, [[23, 44], [1, 23]])), 16)
        B.wait("sync", self.ld, self.ld.n)
        B.inc("sync", self.ld, lambda E: E.dma_start(
            out=bqt.ap(0, 128, 0, 24),
            in_=dram_ap(self.win["l1a_bqkv"], 0, [[1, 128], [128, 24]]), allow_slow_non_contiguous=True), 16)
        B.wait("sync", self.ld, self.ld.n)
        B.inc("sync", self.ld, lambda E: E.dma_start(
            out=bqt.ap(0, 128, 24, 8),
            in_=dram_ap(self.win["l1a_bgate"], 0, [[1, 128], [128, 8]]), allow_slow_non_contiguous=True), 16)
        B.wait("sync", self.ld, self.ld.n)
        B.inc("sync", self.ld, lambda E: E.dma_start(
            out=bot.ap(), in_=dram_ap(self.win["l1a_bout"], 0, [[1, 128], [128, 8]]), allow_slow_non_contiguous=True), 16)

        B.inc("vector", self.vc, lambda E: E.tensor_scalar(
            p127.ap(), self.pidx.ap(), -1.0, 127.0, OP.mult, OP.add))
        B.wait("vector", self.vc, self.vc.n)
        B.inc("vector", self.vc, lambda E: E.tensor_scalar(
            revb.ap(), self.fidx.ap(), p127.ap(), None, OP.is_equal))
        B.run("vector", lambda E: E.memset(mz.ap(), 0.0))
        B.inc("vector", self.vc, lambda E: E.memset(
            vtokp.apd(64, [[16 * 130, 128], [65, 32], [1, 1]]), 1.0))

        B.wait("scalar", self.ld, self.ld.n)
        B.inc("scalar", self.sc, lambda E: E.activation(epbb.ap(), pbx.ap(), AF.Exp))
        B.wait("tensor", self.sc, self.sc.n)
        B.wait("tensor", self.ld, self.ld.n)
        B.wait("tensor", self.vc, self.vc.n)
        B.inc("tensor", self.mm, lambda E: E.matmul(
            self.ps_scr.ap(0, 23, 0, 16), gselb.ap(), epbb.ap(), start=True, stop=True))
        B.wait("vector", self.mm, self.mm.n)
        B.inc("vector", self.vc, lambda E: E.tensor_copy(wsb.ap(), self.ps_scr.ap(0, 23, 0, 16)))
        B.wait("vector", self.vc, self.vc.n)
        B.inc("vector", self.vc, lambda E: E.tensor_copy(wsbb.ap(), wsb.ap()))
        B.wait("scalar", self.vc, self.vc.n)
        B.wait("sync", self.ld, self.ld.n)
        B.inc("sync", self.ld, lambda E: E.dma_start(
            out=stcb.ap(), in_=dram_ap(self.win["l1a_stc"], 0, [[S, 23], [1, S]])), 16)
        B.wait("tensor", self.ld, self.ld.n)
        B.wait("tensor", self.vc, self.vc.n)
        for bb in range(4):
            B.inc("tensor", self.mm, lambda E, bb=bb: E.matmul(
                self.ps_a.ap(0, 16, bb * 512, 512), wsbb.ap(),
                stcb.ap(0, 23, bb * 512, 512), start=True, stop=True))
        B.wait("vector", self.mm, self.mm.n)
        B.inc("vector", self.vc, lambda E: E.tensor_copy(
            stcb.ap(0, 16, 0, S), self.ps_a.ap(0, 16, 0, S)))
        B.wait("gpsimd", self.vc, self.vc.n)
        B.inc("gpsimd", self.gb, lambda E: E.dma_start(
            out=dram_ap(self.c2scr, 0, [[S, 16], [1, S]]),
            in_=stcb.ap(0, 16, 0, S)), 16)
        B.wait("gpsimd", self.gb, self.gb.n)
        B.wait("gpsimd", self.vc, self.vc.n)
        B.inc("gpsimd", self.gb, lambda E: E.dma_start(
            out=dram_ap(self.wrowscr, 0, [[L1_WROW, 16], [1, L1_WROW]]), in_=mz.ap()), 16)
        B.wait("gpsimd", self.gb, self.gb.n)
        B.inc("gpsimd", self.gb, lambda E: E.dma_start(
            out=dram_ap(self.wscr23, 0, [[16, 23], [1, 16]]), in_=wsb.ap()), 16)
        B.wait("gpsimd", self.gb, self.gb.n)
        for oi, o in enumerate(L1_OFFS):
            B.inc("gpsimd", self.gb, lambda E, oi=oi, o=o: E.dma_start(
                out=dram_ap(self.wrowscr, 127 + o, [[L1_WROW, 16], [1, 1]]),
                in_=dram_ap(self.wscr23, oi * 16, [[1, 16]]),
                allow_slow_non_contiguous=True), 16)
            B.wait("gpsimd", self.gb, self.gb.n)
        B.inc("gpsimd", self.gc, lambda E: E.nop())
        for e in ENGINES:
            B.wait(e, self.gc, self.gc.n)
        s_prep.close()

        po = [self.ps_o1, self.ps_o2, self.ps_scr]
        po_ev = [0, 0, 0]
        po_i = [0]

        def piece_mm(mms, evac):
            i = po_i[0] % 3
            pso = po[i]
            B.wait("tensor", self.sc, po_ev[i])
            mms(pso)
            B.wait("scalar", self.mm, self.mm.n)
            evac(pso)
            po_ev[i] = self.sc.n
            po_i[0] += 1

        psa_sc = [0]
        psa_vc = [0]
        gath_vc = [0]
        mk_vc = [0]
        vtok_mm = [0]
        wb_mm = [0, 0]
        par_ld = [self.lda, self.ldb]
        par_gw = [self.gwa, self.gwb]

        WQKV = self.win["l1a_wqkv"]
        WGATE = self.win["l1a_wgate"]

        zvc = [0]
        for pr in range(NCH):
            hA, hB2 = 2 * pr, 2 * pr + 1
            B.wait("sync", self.vc, zvc[0])
            B.wait("sync", self.ld, self.ld.n)
            B.inc("sync", self.ld, lambda E, pr=pr: E.dma_start(
                out=c2l.ap(0, 1, 0, 2 * S),
                in_=dram_ap(self.c2scr, 2 * pr * S, [[1, 2 * S]])), 16)
            projs = [
                (WQKV, 3 * D, pr * 128, pr, AF.Identity, qpair, BF16),
                (WQKV, 3 * D, D + pr * 128, 8 + pr, AF.Identity, kpair, BF16),
                (WQKV, 3 * D, 2 * D + pr * 128, 16 + pr, AF.Identity, vtmp, F32),
                (WGATE, D, pr * 128, 24 + pr, AF.Sigmoid, gatep, BF16),
            ]
            for pi, (wd, wpitch, c0, bcol, func, dst, ddt) in enumerate(projs):
                wi = pi & 1
                par = par_gw[wi]
                B.wait("gpsimd", self.mm, wb_mm[wi])
                B.wait("gpsimd", par, par.n)
                B.inc("gpsimd", par, lambda E, wd=wd, wpitch=wpitch, c0=c0, wi=wi: E.dma_start(
                    out=wbuf[wi].ap(),
                    in_=dram_ap(wd, c0, [[wpitch, 128], [128 * wpitch, 8], [1, 128]])), 16)
                B.wait("tensor", par, par.n)
                if pi == 2:
                    B.wait("scalar", self.vc, gath_vc[0])
                if pi == 3:
                    B.wait("scalar", self.vc, gath_vc[0])
                for k in range(4):
                    def mms(pso, k=k, wi=wi):
                        for t in range(NCH):
                            fn = lambda E, pso=pso, t=t, k=k, wi=wi: E.matmul(
                                pso.ap(0, 128, 0, 512),
                                wbuf[wi].ap(0, 128, t * 128, 128),
                                self.xbf.ap(0, 128, t * S + k * 512, 512),
                                start=(t == 0), stop=(t == NCH - 1))
                            if t == NCH - 1:
                                B.inc("tensor", self.mm, fn)
                            else:
                                B.run("tensor", fn)

                    def evac(pso, k=k, func=func, dst=dst, bcol=bcol):
                        B.inc("scalar", self.sc, lambda E, pso=pso, k=k, func=func,
                              dst=dst, bcol=bcol: E.activation(
                                  dst.ap(0, 128, k * 512, 512), pso.ap(0, 128, 0, 512),
                                  func, bias=bqt.ap(0, 128, bcol, 1)))
                    piece_mm(mms, evac)
                wb_mm[wi] = self.mm.n

            B.wait("tensor", self.sc, self.sc.n)
            B.wait("tensor", self.vc, self.vc.n)
            B.wait("tensor", self.mm, vtok_mm[0])
            tp_ev = {}
            tps = [self.ps_scr, self.ps_o1, self.ps_o2]
            for mb in range(16):
                ps = tps[mb % 3]
                if mb >= 3:
                    B.wait("tensor", self.vc, tp_ev[mb - 3])
                B.inc("tensor", self.mm, lambda E, ps=ps, mb=mb: E.transpose(
                    ps.ap(0, 128, 0, 128), vtmp.ap(0, 128, mb * 128, 128),
                    self.eye32.ap()))
                B.wait("vector", self.mm, self.mm.n)
                B.run("vector", lambda E, ps=ps, mb=mb: E.tensor_copy(
                    vtokp.ap(0, 128, mb * 130, 64), ps.ap(0, 128, 0, 64)))
                B.inc("vector", self.vc, lambda E, ps=ps, mb=mb: E.tensor_copy(
                    vtokp.ap(0, 128, mb * 130 + 65, 64), ps.ap(0, 128, 64, 64)))
                tp_ev[mb] = self.vc.n
            B.wait("tensor", self.vc, self.vc.n)
            po_ev[0] = self.sc.n
            po_ev[1] = self.sc.n
            po_ev[2] = self.sc.n

            for hi, h in enumerate((hA, hB2)):
                B.wait("sync", self.mm, self.mm.n)
                par = par_ld[hi]
                B.wait("sync", par, par.n)
                B.inc("sync", par, lambda E, h=h: E.dma_start(
                    out=mkraw.ap(),
                    in_=dram_ap(self.wrowscr, h * L1_WROW, [[1, 128], [1, L1_BAND]])), 16)
                B.wait("tensor", par, par.n)
                B.wait("scalar", self.vc, mk_vc[0])
                for k in range(4):
                    w = min(512, L1_BAND - k * 512)

                    def mms(pso, k=k, w=w):
                        B.inc("tensor", self.mm, lambda E, pso=pso, k=k, w=w: E.matmul(
                            pso.ap(0, 128, 0, w), revb.ap(),
                            mkraw.ap(0, 128, k * 512, w), start=True, stop=True))

                    def evac(pso, k=k, w=w):
                        B.inc("scalar", self.sc, lambda E, pso=pso, k=k, w=w: E.activation(
                            mk2.ap(0, 128, k * 512, w), pso.ap(0, 128, 0, w), AF.Copy))
                    piece_mm(mms, evac)

                B.wait("tensor", self.sc, psa_sc[0])
                B.wait("tensor", self.vc, psa_vc[0])
                pt_mm = [0, 0]
                for mb in range(16):
                    m0 = 128 * mb
                    W = min(L1_BAND, S - m0)
                    pb_ = mb & 1
                    ptb = ptil[pb_]
                    npz = (W + 511) // 512
                    for k in range(npz):
                        n0 = m0 + k * 512
                        w = min(512, m0 + W - n0)

                        def mms(pso, h=h, hi_=h & 1, m0=m0, n0=n0, w=w):
                            B.wait("tensor", self.mm, pt_mm[pb_])
                            B.inc("tensor", self.mm, lambda E, pso=pso, hi_=hi_, m0=m0,
                                  n0=n0, w=w: E.matmul(
                                      pso.ap(0, 128, 0, w),
                                      kpair.ap(64 * hi_, 64, m0, 128),
                                      qpair.ap(64 * hi_, 64, n0, w),
                                      start=True, stop=True))

                        def evac(pso, n0=n0, w=w, ptb=ptb):
                            B.inc("scalar", self.sc, lambda E, pso=pso, n0=n0, w=w,
                                  ptb=ptb: E.activation(
                                      ptb.ap(0, 128, n0, w), pso.ap(0, 128, 0, w),
                                      AF.Exp, scale=0.125))
                        piece_mm(mms, evac)
                    if mb == 0:
                        B.inc("vector", self.vc, lambda E, ptb=ptb: E.memset(
                            ptb.ap(0, 128, L1_BAND, S - L1_BAND), 0.0))
                    B.wait("vector", self.sc, self.sc.n)
                    B.inc("vector", self.vc, lambda E, ptb=ptb, m0=m0, W=W: E.tensor_tensor(
                        ptb.ap(0, 128, m0, W), mk2.ap(0, 128, 0, W),
                        ptb.ap(0, 128, m0, W), OP.mult))
                    mk_vc[0] = self.vc.n
                    B.wait("tensor", self.vc, self.vc.n)
                    for b in range(4):
                        if mb == 0:
                            nlo, nhi = 512 * b, 512 * b + 512
                        else:
                            nlo = max(512 * b, m0)
                            nhi = min(512 * b + 512, m0 + W)
                            if nlo >= nhi:
                                continue
                        last = (mb == min(15, (512 * (b + 1) - 1) // 128))
                        fn = lambda E, b=b, nlo=nlo, nhi=nhi, mb=mb, hi_=h & 1, last=last: E.matmul(
                            self.ps_a.ap(0, 65, nlo, nhi - nlo),
                            vtokp.ap(0, 128, mb * 130 + 65 * hi_, 65),
                            ptil[mb & 1].ap(0, 128, nlo, nhi - nlo),
                            start=(mb == 0), stop=last)
                        if b == 3:
                            B.inc("tensor", self.mm, fn)
                        else:
                            B.run("tensor", fn)
                    pt_mm[pb_] = self.mm.n
                vtok_mm[0] = self.mm.n
                B.wait("scalar", self.mm, self.mm.n)
                for b in range(4):
                    B.inc("scalar", self.sc, lambda E, b=b, hi=hi: E.activation(
                        gathp.ap(64 * hi, 64, b * 512, 512),
                        self.ps_a.ap(0, 64, b * 512, 512), AF.Copy))
                B.wait("vector", self.mm, self.mm.n)
                B.wait("vector", self.ld, self.ld.n)
                B.inc("vector", self.vc, lambda E, hi=hi: E.tensor_tensor(
                    zsb.ap(0, 1, hi * S, S), self.ps_a.ap(64, 1, 0, S),
                    c2l.ap(0, 1, hi * S, S), OP.add))
                zvc[0] = self.vc.n
                psa_sc[0] = self.sc.n
                psa_vc[0] = self.vc.n

            stp = self.sta if pr % 2 == 0 else self.stb
            B.wait("gpsimd", self.sc, self.sc.n)
            B.wait("gpsimd", self.vc, self.vc.n)
            B.wait("gpsimd", stp, stp.n)
            B.inc("gpsimd", stp, lambda E, pr=pr: E.dma_start(
                out=dram_ap(self.zscr2, 2 * pr * S, [[S, 2], [1, S]]),
                in_=zsb.ap(0, 1, 0, 2 * S)), 16)
            B.wait("sync", stp, stp.n)
            B.wait("sync", self.ld, self.ld.n)
            B.inc("sync", self.ld, lambda E, pr=pr: E.dma_start(
                out=zrep.ap(),
                in_=dram_ap(self.zscr2, 2 * pr * S, [[S, 2], [0, 64], [1, S]])), 16)
            B.wait("vector", self.ld, self.ld.n)
            B.wait("vector", self.sc, self.sc.n)
            B.inc("vector", self.vc, lambda E: E.reciprocal(rzf.ap(), zrep.ap()))
            B.wait("vector", self.vc, self.vc.n)
            B.inc("vector", self.vc, lambda E: E.tensor_tensor(
                og1.ap(), gathp.ap(), gatep.ap(), OP.mult))
            B.wait("vector", self.vc, self.vc.n)
            B.inc("vector", self.vc, lambda E, pr=pr: E.tensor_tensor(
                ogbf.ap(0, 128, pr * S, S), og1.ap(), rzf.ap(), OP.mult))
            gath_vc[0] = self.vc.n

        s_t = ExitStack()
        xstage = B.sbuf("xstg1", 128, 2 * S, F32, stack=s_t)
        rstage = B.sbuf("rstg1", 128, 2 * S, F32, stack=s_t)
        tstage = B.sbuf("tstg1", 128, S, F32, stack=s_t)
        WOUT = self.win["l1a_wout"]
        ev2_at = {}
        st_at = {}
        B.wait("tensor", self.vc, self.vc.n)
        for co in range(NCH):
            wi = co & 1
            par = par_gw[wi]
            parx = par_ld[wi]
            B.wait("gpsimd", self.mm, wb_mm[wi])
            B.wait("gpsimd", par, par.n)
            B.inc("gpsimd", par, lambda E, co=co: E.dma_start(
                out=wbuf[co & 1].ap(),
                in_=dram_ap(WOUT, co * 128, [[D, 128], [128 * D, 8], [1, 128]])), 16)
            if co >= 2:
                B.wait("sync", self.vc, ev2_at[co - 2])
            B.wait("sync", parx, parx.n)
            B.inc("sync", parx, lambda E, co=co: E.dma_start(
                out=xstage.ap(0, 128, (co & 1) * S, S),
                in_=dram_ap(self.xspill, co * 128 * S, [[S, 128], [1, S]])), 16)
            B.wait("tensor", par, par.n)
            B.wait("vector", parx, parx.n)
            for k in range(4):
                def mms(pso, co=co, k=k):
                    for t in range(NCH):
                        fn = lambda E, pso=pso, co=co, t=t, k=k: E.matmul(
                            pso.ap(0, 128, 0, 512),
                            wbuf[co & 1].ap(0, 128, t * 128, 128),
                            ogbf.ap(0, 128, t * S + k * 512, 512),
                            start=(t == 0), stop=(t == NCH - 1))
                        if t == NCH - 1:
                            B.inc("tensor", self.mm, fn)
                        else:
                            B.run("tensor", fn)

                def evac(pso, co=co, k=k):
                    B.inc("scalar", self.sc, lambda E, pso=pso, co=co, k=k: E.activation(
                        tstage.ap(0, 128, k * 512, 512), pso.ap(0, 128, 0, 512),
                        AF.Identity, bias=bot.ap(0, 128, co, 1)))
                piece_mm(mms, evac)
            wb_mm[wi] = self.mm.n
            B.wait("vector", self.sc, self.sc.n)
            if co >= 2:
                B.wait("vector", st_at[co - 2][0], st_at[co - 2][1])
            B.inc("vector", self.vc, lambda E, co=co: E.tensor_tensor(
                rstage.ap(0, 128, (co & 1) * S, S), tstage.ap(),
                xstage.ap(0, 128, (co & 1) * S, S), OP.add))
            ev2_at[co] = self.vc.n
            stp = self.sta if co % 2 == 0 else self.stb
            B.wait("gpsimd", self.vc, self.vc.n)
            B.wait("gpsimd", stp, stp.n)
            B.inc("gpsimd", stp, lambda E, co=co: E.dma_start(
                out=dram_ap(self.xspill2, co * 128 * S, [[S, 128], [1, S]]),
                in_=rstage.ap(0, 128, (co & 1) * S, S)), 16)
            st_at[co] = (stp, stp.n)
        B.wait("gpsimd", self.sta, self.sta.n)
        B.wait("gpsimd", self.stb, self.stb.n)
        B.wait("gpsimd", self.vc, self.vc.n)
        B.wait("gpsimd", self.mm, self.mm.n)
        B.wait("gpsimd", self.sc, self.sc.n)
        self.barrier()
        s_t.close()
        s_a.close()
        self.alloc_xf()
        for c in range(NCH):
            B.inc("gpsimd", self.gl, lambda E, c=c: E.dma_start(
                out=self.xf.ap(0, 128, c * S, S),
                in_=dram_ap(self.xspill2, c * 128 * S, [[S, 128], [1, S]])), 16)
        B.wait("vector", self.gl, self.gl.n)
        B.wait("scalar", self.gl, self.gl.n)
        B.wait("tensor", self.gl, self.gl.n)
        self.ln_phase()

    def l1_ffn(self):
        B = self.B
        NIC = 22
        QMAXI = 1.0 / 127.0
        RG = [[0, 1, 2, 3, 4, 5, 6, 7]]

        s_m = ExitStack()
        macc = B.sbuf("macc", 128, 4, F32, stack=s_m)
        gsc = B.sbuf("gsc", 1, 16, F32, stack=s_m)
        B.wait("vector", self.vc, self.vc.n)
        B.inc("vector", self.vc, lambda E: E.tensor_reduce(
            macc.ap(0, 128, 0, 1), self.xf.ap(), AX.X, OP.max, apply_absolute_value=True))
        B.wait("tensor", self.vc, self.vc.n)
        B.inc("tensor", self.mm, lambda E: E.transpose(
            self.ps_scr.ap(0, 1, 0, 128), macc.ap(0, 128, 0, 1), self.eye32.ap()))
        B.wait("vector", self.mm, self.mm.n)
        B.inc("vector", self.vc, lambda E: E.tensor_reduce(
            gsc.ap(0, 1, 0, 1), self.ps_scr.ap(0, 1, 0, 128), AX.X, OP.max))
        B.wait("gpsimd", self.vc, self.vc.n)
        B.inc("gpsimd", self.gb, lambda E: E.dma_start(
            out=dram_ap(self.ccxi, 0, [[1, 1]]), in_=gsc.ap(0, 1, 0, 1)), 16)
        B.wait("gpsimd", self.gb, self.gb.n)
        B.inc("gpsimd", self.ccs, lambda E: E.collective_compute(
            "AllReduce", OP.max, replica_groups=RG,
            ins=[self.ccxi.ap().opt()], outs=[self.ccxo.ap().opt()]), 1)
        B.wait("gpsimd", self.ccs, self.ccs.n)
        B.inc("gpsimd", self.gb, lambda E: E.dma_start(
            out=gsc.ap(0, 1, 1, 1), in_=dram_ap(self.ccxo, 0, [[1, 1]])), 16)
        B.wait("sync", self.ld, self.ld.n)
        B.inc("sync", self.ld, lambda E: E.dma_start(
            out=gsc.ap(0, 1, 2, 1), in_=dram_ap(self.win["l1f_clipx"], 0, [[1, 1]])), 16)
        B.wait("sync", self.ld, self.ld.n)
        B.inc("sync", self.ld, lambda E: E.dma_start(
            out=gsc.ap(0, 1, 3, 1), in_=dram_ap(self.win["l1f_clipd"], 0, [[1, 1]])), 16)
        B.wait("scalar", self.ld, self.ld.n)
        B.inc("scalar", self.sc, lambda E: E.activation(
            gsc.ap(0, 1, 4, 2), gsc.ap(0, 1, 2, 2), AF.Sigmoid))
        B.wait("vector", self.sc, self.sc.n)
        B.wait("vector", self.gb, self.gb.n)
        B.inc("vector", self.vc, lambda E: E.tensor_tensor(
            gsc.ap(0, 1, 6, 1), gsc.ap(0, 1, 4, 1), gsc.ap(0, 1, 1, 1), OP.mult))
        B.wait("vector", self.vc, self.vc.n)
        B.inc("vector", self.vc, lambda E: E.tensor_scalar(
            gsc.ap(0, 1, 6, 1), gsc.ap(0, 1, 6, 1), QMAXI, None, OP.mult))
        B.wait("vector", self.vc, self.vc.n)
        B.inc("vector", self.vc, lambda E: E.reciprocal(
            gsc.ap(0, 1, 7, 1), gsc.ap(0, 1, 6, 1)))
        B.wait("gpsimd", self.vc, self.vc.n)
        B.inc("gpsimd", self.gb, lambda E: E.dma_start(
            out=dram_ap(self.rowscr, 0, [[1, 8]]), in_=gsc.ap(0, 1, 0, 8)), 16)

        for c in range(NCH):
            B.inc("gpsimd", self.gl, lambda E, c=c: E.dma_start(
                out=dram_ap(self.xspill, c * 128 * S, [[S, 128], [1, S]]),
                in_=self.xf.ap(0, 128, c * S, S)), 16)
        B.wait("gpsimd", self.gl, self.gl.n)
        B.inc("gpsimd", self.gc, lambda E: E.nop())
        for e in ENGINES:
            B.wait(e, self.gc, self.gc.n)
        s_m.close()
        self.s_xf.close()

        s_f = ExitStack()
        wdqT = B.sbuf("wdqT", 128, NIC * 8 * 128, BF16, stack=s_f)
        hqc2 = [B.sbuf(f"hqc{i}", 128, NIC * 512, BF16, stack=s_f) for i in range(2)]
        hstg = [B.sbuf(f"hstg{i}", 128, S, F32, stack=s_f) for i in range(2)]
        swg = B.sbuf("swg", 128, 96, F32, stack=s_f)
        scl = B.sbuf("scl", 128, 12, F32, stack=s_f)
        hmc = B.sbuf("hmc", 128, NIC + 2, F32, stack=s_f)
        gsc2 = B.sbuf("gsc2", 1, 16, F32, stack=s_f)
        s_q = ExitStack()
        wstg = [B.sbuf(f"wstg{i}", 128, 2816, F32, stack=s_q) for i in range(2)]
        quf = B.sbuf("quf", 128, 2816, F32, stack=s_q)
        qf = B.sbuf("qf", 128, 2816, F32, stack=s_q)
        wglT = B.sbuf("wglT", 128, 1024, BF16, stack=s_q)
        wulT = B.sbuf("wulT", 128, 1024, BF16, stack=s_q)
        silg = B.sbuf("silg", 128, 512, F32, stack=s_q)

        B.wait("sync", self.gb, self.gb.n)
        B.wait("sync", self.ld, self.ld.n)
        B.inc("sync", self.ld, lambda E: E.dma_start(
            out=scl.ap(0, 128, 0, 8), in_=dram_ap(self.rowscr, 0, [[0, 128], [1, 8]])), 16)
        B.wait("scalar", self.ld, self.ld.n)
        B.wait("vector", self.ld, self.ld.n)

        for c in range(NCH):
            B.wait("scalar", self.vc, self.vc.n)
            B.inc("scalar", self.sc, lambda E, c=c: E.activation(
                quf.ap(0, 128, 0, S), self.xf.ap(0, 128, c * S, S),
                AF.Copy, scale=scl.ap(0, 128, 7, 1)))
            B.wait("vector", self.sc, self.sc.n)
            B.inc("vector", self.vc, lambda E: E.tensor_scalar(
                qf.ap(0, 128, 0, S), quf.ap(0, 128, 0, S), RNE_C, RNE_C, OP.add, OP.subtract))
            B.wait("vector", self.vc, self.vc.n)
            B.inc("vector", self.vc, lambda E: E.tensor_scalar(
                qf.ap(0, 128, 0, S), qf.ap(0, 128, 0, S), 127.0, -128.0, OP.min, OP.max))
            B.wait("vector", self.vc, self.vc.n)
            B.inc("vector", self.vc, lambda E, c=c: E.tensor_copy(
                self.xbf.ap(0, 128, c * S, S), qf.ap(0, 128, 0, S)))

        po = [self.ps_o1, self.ps_o2, self.ps_scr]
        psev = {id(self.ps_o1): (self.sc, 0), id(self.ps_o2): (self.sc, 0),
                id(self.ps_scr): (self.sc, 0)}
        po_i = [0]

        def piece(mms, evacs, eng="scalar"):
            i = po_i[0] % 3
            pso = po[i]
            sem, cnt = psev[id(pso)]
            B.wait("tensor", sem, cnt)
            mms(pso)
            B.wait(eng, self.mm, self.mm.n)
            evacs(pso)
            psev[id(pso)] = (self.sc, self.sc.n) if eng == "scalar" else (self.vc, self.vc.n)
            po_i[0] += 1

        par_ld = [self.lda, self.ldb]
        par_gw = [self.gwa, self.gwb]
        wld_gate = [0, 0]
        qf_mm = [0]
        silg_vc = [0]

        def quant_weight(wi, blk, dram, cols, scol):
            par = par_gw[wi & 1]
            B.wait("gpsimd", self.vc, wld_gate[wi & 1])
            B.wait("gpsimd", par, par.n)
            B.inc("gpsimd", par, lambda E, wi=wi, blk=blk, dram=dram, cols=cols: E.dma_start(
                out=wstg[wi & 1].ap(0, 128, 0, cols),
                in_=dram_ap(dram, blk * 128 * cols, [[cols, 128], [1, cols]])), 16)
            B.wait("vector", par, par.n)
            B.inc("vector", self.vc, lambda E, wi=wi, cols=cols, scol=scol: E.tensor_reduce(
                swg.ap(0, 128, scol, 1), wstg[wi & 1].ap(0, 128, 0, cols), AX.X, OP.max, apply_absolute_value=True))
            B.wait("vector", self.vc, self.vc.n)
            B.inc("vector", self.vc, lambda E, scol=scol: E.tensor_scalar(
                swg.ap(0, 128, scol, 1), swg.ap(0, 128, scol, 1), QMAXI, None, OP.mult))
            B.wait("vector", self.vc, self.vc.n)
            B.inc("vector", self.vc, lambda E, scol=scol: E.reciprocal(
                scl.ap(0, 128, 8, 1), swg.ap(0, 128, scol, 1)))
            B.wait("scalar", self.vc, self.vc.n)
            B.inc("scalar", self.sc, lambda E, wi=wi, cols=cols: E.activation(
                quf.ap(0, 128, 0, cols), wstg[wi & 1].ap(0, 128, 0, cols),
                AF.Copy, scale=scl.ap(0, 128, 8, 1)))
            B.wait("vector", self.sc, self.sc.n)
            B.wait("vector", self.mm, qf_mm[0])
            B.inc("vector", self.vc, lambda E, cols=cols: E.tensor_scalar(
                qf.ap(0, 128, 0, cols), quf.ap(0, 128, 0, cols), RNE_C, RNE_C,
                OP.add, OP.subtract))
            wld_gate[wi & 1] = self.vc.n
            B.wait("vector", self.vc, self.vc.n)
            B.inc("vector", self.vc, lambda E, cols=cols: E.tensor_scalar(
                qf.ap(0, 128, 0, cols), qf.ap(0, 128, 0, cols), 127.0, -128.0,
                OP.min, OP.max))

        tps = [self.ps_scr, self.ps_o1, self.ps_o2]

        def transpose_to(dstbuf, coloff, nblk, colstep):
            B.wait("tensor", self.vc, self.vc.n)
            for tb in range(nblk):
                ps = tps[tb % 3]
                sem, cnt = psev[id(ps)]
                B.wait("tensor", sem, cnt)
                B.inc("tensor", self.mm, lambda E, ps=ps, tb=tb: E.transpose(
                    ps.ap(0, 128, 0, 128), qf.ap(0, 128, tb * 128, 128), self.eye32.ap()))
                B.wait("vector", self.mm, self.mm.n)
                B.inc("vector", self.vc, lambda E, ps=ps, tb=tb, dstbuf=dstbuf,
                      coloff=coloff, colstep=colstep: E.tensor_copy(
                          dstbuf.ap(0, 128, coloff + tb * colstep, 128), ps.ap(0, 128, 0, 128)))
                psev[id(ps)] = (self.vc, self.vc.n)
            qf_mm[0] = self.mm.n

        hst_st = {}
        for ic in range(NIC):
            quant_weight(2 * ic, ic, self.win["l1f_wg"], D, ic)
            transpose_to(wglT, 0, 8, 128)
            B.wait("vector", self.ld, self.ld.n)
            B.inc("vector", self.vc, lambda E, ic=ic: E.tensor_scalar(
                swg.ap(0, 128, 44 + ic, 1), swg.ap(0, 128, ic, 1),
                scl.ap(0, 128, 6, 1), None, OP.mult))
            quant_weight(2 * ic + 1, ic, self.win["l1f_wu"], D, 22 + ic)
            transpose_to(wulT, 0, 8, 128)
            B.inc("vector", self.vc, lambda E, ic=ic: E.tensor_scalar(
                swg.ap(0, 128, 66 + ic, 1), swg.ap(0, 128, 22 + ic, 1),
                scl.ap(0, 128, 6, 1), None, OP.mult))
            fold_vc = self.vc.n
            hbuf = hstg[ic & 1]
            if hst_st.get(ic - 2) is not None:
                B.wait("vector", hst_st[ic - 2][0], hst_st[ic - 2][1])
            for np_ in range(4):
                def mms_g(pso, np_=np_):
                    for dc in range(NCH):
                        fn = lambda E, pso=pso, dc=dc, np_=np_: E.matmul(
                            pso.ap(0, 128, 0, 512), wglT.ap(0, 128, dc * 128, 128),
                            self.xbf.ap(0, 128, dc * S + np_ * 512, 512),
                            start=(dc == 0), stop=(dc == NCH - 1))
                        if dc == NCH - 1:
                            B.inc("tensor", self.mm, fn)
                        else:
                            B.run("tensor", fn)

                def ev_g(pso, ic=ic, fold_vc=fold_vc):
                    B.wait("scalar", self.vc, max(silg_vc[0], fold_vc))
                    B.inc("scalar", self.sc, lambda E, pso=pso, ic=ic: E.activation(
                        silg.ap(), pso.ap(0, 128, 0, 512),
                        AF.Silu, scale=swg.ap(0, 128, 44 + ic, 1)))
                piece(mms_g, ev_g)

                def mms_u(pso, np_=np_):
                    for dc in range(NCH):
                        fn = lambda E, pso=pso, dc=dc, np_=np_: E.matmul(
                            pso.ap(0, 128, 0, 512), wulT.ap(0, 128, dc * 128, 128),
                            self.xbf.ap(0, 128, dc * S + np_ * 512, 512),
                            start=(dc == 0), stop=(dc == NCH - 1))
                        if dc == NCH - 1:
                            B.inc("tensor", self.mm, fn)
                        else:
                            B.run("tensor", fn)

                def ev_u(pso, ic=ic, np_=np_, hbuf=hbuf):
                    B.wait("vector", self.sc, self.sc.n)
                    B.wait("vector", self.vc, self.vc.n)
                    B.inc("vector", self.vc, lambda E, pso=pso, ic=ic, np_=np_,
                          hbuf=hbuf: E.tensor_scalar(
                              hbuf.ap(0, 128, np_ * 512, 512), pso.ap(0, 128, 0, 512),
                              swg.ap(0, 128, 66 + ic, 1), None, OP.mult))
                    B.wait("vector", self.vc, self.vc.n)
                    B.inc("vector", self.vc, lambda E, np_=np_, hbuf=hbuf: E.tensor_tensor(
                        hbuf.ap(0, 128, np_ * 512, 512), silg.ap(),
                        hbuf.ap(0, 128, np_ * 512, 512), OP.mult))
                    silg_vc[0] = self.vc.n
                piece(mms_u, ev_u, eng="vector")
            B.wait("vector", self.vc, self.vc.n)
            B.inc("vector", self.vc, lambda E, ic=ic, hbuf=hbuf: E.tensor_reduce(
                hmc.ap(0, 128, ic, 1), hbuf.ap(), AX.X, OP.max, apply_absolute_value=True))
            stp = self.sta if ic % 2 == 0 else self.stb
            B.wait("gpsimd", self.vc, self.vc.n)
            B.wait("gpsimd", stp, stp.n)
            B.inc("gpsimd", stp, lambda E, ic=ic, hbuf=hbuf: E.dma_start(
                out=dram_ap(self.hspill, ic * 128 * S, [[S, 128], [1, S]]),
                in_=hbuf.ap()), 16)
            hst_st[ic] = (stp, stp.n)

        B.wait("vector", self.vc, self.vc.n)
        B.inc("vector", self.vc, lambda E: E.tensor_reduce(
            hmc.ap(0, 128, NIC, 1), hmc.ap(0, 128, 0, NIC), AX.X, OP.max))
        B.wait("tensor", self.vc, self.vc.n)
        B.wait("tensor", self.sc, self.sc.n)
        B.inc("tensor", self.mm, lambda E: E.transpose(
            self.ps_scr.ap(0, 1, 0, 128), hmc.ap(0, 128, NIC, 1), self.eye32.ap()))
        B.wait("vector", self.mm, self.mm.n)
        B.inc("vector", self.vc, lambda E: E.tensor_reduce(
            gsc2.ap(0, 1, 0, 1), self.ps_scr.ap(0, 1, 0, 128), AX.X, OP.max))
        B.wait("gpsimd", self.vc, self.vc.n)
        B.inc("gpsimd", self.gb, lambda E: E.dma_start(
            out=dram_ap(self.cchi, 0, [[1, 1]]), in_=gsc2.ap(0, 1, 0, 1)), 16)
        B.wait("gpsimd", self.gb, self.gb.n)
        B.inc("gpsimd", self.ccs, lambda E: E.collective_compute(
            "AllReduce", OP.max, replica_groups=RG,
            ins=[self.cchi.ap().opt()], outs=[self.ccho.ap().opt()]), 1)
        B.wait("gpsimd", self.ccs, self.ccs.n)
        B.inc("gpsimd", self.gb, lambda E: E.dma_start(
            out=gsc2.ap(0, 1, 1, 1), in_=dram_ap(self.ccho, 0, [[1, 1]])), 16)
        B.wait("vector", self.gb, self.gb.n)
        B.inc("vector", self.vc, lambda E: E.tensor_tensor(
            gsc2.ap(0, 1, 2, 1), gsc2.ap(0, 1, 1, 1), scl.ap(0, 1, 5, 1), OP.mult))
        B.wait("vector", self.vc, self.vc.n)
        B.inc("vector", self.vc, lambda E: E.tensor_scalar(
            gsc2.ap(0, 1, 2, 1), gsc2.ap(0, 1, 2, 1), QMAXI, None, OP.mult))
        B.wait("vector", self.vc, self.vc.n)
        B.inc("vector", self.vc, lambda E: E.reciprocal(
            gsc2.ap(0, 1, 3, 1), gsc2.ap(0, 1, 2, 1)))
        B.wait("gpsimd", self.vc, self.vc.n)
        B.inc("gpsimd", self.gb, lambda E: E.dma_start(
            out=dram_ap(self.rowscr, 8, [[1, 4]]), in_=gsc2.ap(0, 1, 0, 4)), 16)
        B.wait("sync", self.gb, self.gb.n)
        B.wait("sync", self.ld, self.ld.n)
        B.inc("sync", self.ld, lambda E: E.dma_start(
            out=scl.ap(0, 128, 9, 2), in_=dram_ap(self.rowscr, 10, [[0, 128], [1, 2]])), 16)
        B.wait("scalar", self.ld, self.ld.n)
        B.wait("vector", self.ld, self.ld.n)

        for dc in range(NCH):
            quant_weight(dc, dc, self.win["l1f_wd"], 2816, 88 + dc)
            transpose_to(wdqT, dc * 128, NIC, 8 * 128)

        B.wait("gpsimd", self.mm, self.mm.n)
        B.wait("gpsimd", self.sc, self.sc.n)
        B.wait("gpsimd", self.vc, self.vc.n)
        self.barrier()
        s_q.close()
        s_t = ExitStack()
        xstage = B.sbuf("xstg2", 128, 2 * S, F32, stack=s_t)
        tst2 = B.sbuf("tst2", 128, 512, F32, stack=s_t)
        rst2 = B.sbuf("rst2", 128, 2 * S, F32, stack=s_t)
        h16 = B.sbuf("h16", 128, 512, F32, stack=s_t)
        B.wait("sync", self.sta, self.sta.n)
        B.wait("sync", self.stb, self.stb.n)
        xst_at = {}
        st2_at = {}
        hq_vc = {}
        hq_mm = [0, 0]
        for np_ in range(4):
            hqc = hqc2[np_ & 1]
            B.wait("vector", self.mm, hq_mm[np_ & 1])
            for ic in range(NIC):
                par = par_ld[ic & 1]
                B.wait("sync", par, par.n)
                B.wait("sync", self.vc, hq_vc.get(ic - 2, 0))
                B.inc("sync", par, lambda E, ic=ic, np_=np_: E.dma_start(
                    out=hstg[0].ap(0, 128, (ic & 1) * 512, 512),
                    in_=dram_ap(self.hspill, ic * 128 * S + np_ * 512, [[S, 128], [1, 512]])), 16)
                B.wait("scalar", par, par.n)
                B.wait("scalar", self.vc, self.vc.n)
                B.inc("scalar", self.sc, lambda E, ic=ic: E.activation(
                    h16.ap(), hstg[0].ap(0, 128, (ic & 1) * 512, 512),
                    AF.Copy, scale=scl.ap(0, 128, 10, 1)))
                B.wait("vector", self.sc, self.sc.n)
                B.inc("vector", self.vc, lambda E: E.tensor_scalar(
                    h16.ap(), h16.ap(), RNE_C, RNE_C, OP.add, OP.subtract))
                B.wait("vector", self.vc, self.vc.n)
                B.inc("vector", self.vc, lambda E: E.tensor_scalar(
                    h16.ap(), h16.ap(), 127.0, -128.0, OP.min, OP.max))
                B.wait("vector", self.vc, self.vc.n)
                B.inc("vector", self.vc, lambda E, ic=ic, hqc=hqc: E.tensor_copy(
                    hqc.ap(0, 128, ic * 512, 512), h16.ap()))
                hq_vc[ic] = self.vc.n
            B.wait("tensor", self.vc, self.vc.n)
            for dc in range(NCH):
                parx = par_ld[dc & 1]
                if xst_at.get((np_, dc - 2)) is not None:
                    B.wait("sync", self.vc, xst_at[(np_, dc - 2)])
                B.wait("sync", parx, parx.n)
                B.inc("sync", parx, lambda E, dc=dc, np_=np_: E.dma_start(
                    out=xstage.ap(0, 128, (dc & 1) * S + np_ * 512, 512),
                    in_=dram_ap(self.xspill, dc * 128 * S + np_ * 512, [[S, 128], [1, 512]])), 16)

                def mms_d(pso, dc=dc, hqc=hqc):
                    for ic in range(NIC):
                        fn = lambda E, pso=pso, dc=dc, ic=ic, hqc=hqc: E.matmul(
                            pso.ap(0, 128, 0, 512),
                            wdqT.ap(0, 128, (ic * 8 + dc) * 128, 128),
                            hqc.ap(0, 128, ic * 512, 512),
                            start=(ic == 0), stop=(ic == NIC - 1))
                        if ic == NIC - 1:
                            B.inc("tensor", self.mm, fn)
                        else:
                            B.run("tensor", fn)

                def ev_d(pso, dc=dc):
                    B.wait("scalar", self.vc, self.vc.n)
                    B.inc("scalar", self.sc, lambda E, pso=pso, dc=dc: E.activation(
                        tst2.ap(), pso.ap(0, 128, 0, 512),
                        AF.Identity, scale=swg.ap(0, 128, 88 + dc, 1)))
                piece(mms_d, ev_d)
                B.wait("vector", self.sc, self.sc.n)
                B.wait("vector", parx, parx.n)
                if st2_at.get((np_, dc - 2)) is not None:
                    B.wait("vector", st2_at[(np_, dc - 2)][0], st2_at[(np_, dc - 2)][1])
                B.inc("vector", self.vc, lambda E: E.tensor_scalar(
                    tst2.ap(), tst2.ap(), scl.ap(0, 128, 9, 1), None, OP.mult))
                B.wait("vector", self.vc, self.vc.n)
                B.inc("vector", self.vc, lambda E, dc=dc, np_=np_: E.tensor_tensor(
                    rst2.ap(0, 128, (dc & 1) * S + np_ * 512, 512), tst2.ap(),
                    xstage.ap(0, 128, (dc & 1) * S + np_ * 512, 512), OP.add))
                xst_at[(np_, dc)] = self.vc.n
                stp = self.sta if dc % 2 == 0 else self.stb
                B.wait("gpsimd", self.vc, self.vc.n)
                B.wait("gpsimd", stp, stp.n)
                B.inc("gpsimd", stp, lambda E, dc=dc, np_=np_: E.dma_start(
                    out=dram_ap(self.xspill2, dc * 128 * S + np_ * 512, [[S, 128], [1, 512]]),
                    in_=rst2.ap(0, 128, (dc & 1) * S + np_ * 512, 512)), 16)
                st2_at[(np_, dc)] = (stp, stp.n)
            hq_mm[np_ & 1] = self.mm.n
        B.wait("gpsimd", self.sta, self.sta.n)
        B.wait("gpsimd", self.stb, self.stb.n)
        B.wait("gpsimd", self.vc, self.vc.n)
        B.wait("gpsimd", self.mm, self.mm.n)
        B.wait("gpsimd", self.sc, self.sc.n)
        self.barrier()
        s_t.close()
        s_f.close()
        self.alloc_xf()
        for c in range(NCH):
            B.inc("gpsimd", self.gl, lambda E, c=c: E.dma_start(
                out=self.xf.ap(0, 128, c * S, S),
                in_=dram_ap(self.xspill2, c * 128 * S, [[S, 128], [1, S]])), 16)
        B.wait("vector", self.gl, self.gl.n)
        B.wait("scalar", self.gl, self.gl.n)
        B.wait("tensor", self.gl, self.gl.n)
        self.ln_phase()

    def _bf_regather(self, srcbuf, ap_a):
        B = self.B
        hi, lo = ap_a
        sapA = type(srcbuf._base)(srcbuf._base.tensor, srcbuf.off,
                                  [[srcbuf.pitch, 32], list(hi), [3, 2], list(lo)])
        sapB = type(srcbuf._base)(srcbuf._base.tensor, srcbuf.off + 1,
                                  [[srcbuf.pitch, 32], list(hi), [1, 2], list(lo)])
        B.run("vector", lambda E, sapA=sapA: E.tensor_copy(
            self.ta.apd(0, [[self.ta.pitch, 32], [1, 128]]), sapA))
        B.inc("vector", self.vc, lambda E, sapB=sapB: E.tensor_copy(
            self.ta.apd(128, [[self.ta.pitch, 32], [1, 128]]), sapB))

    def _grid_change(self, src, dst, N, fwd):
        B = self.B
        W = 32 * N
        B.wait("sync", self.ev, self.ev.n)
        B.wait("sync", self.ld, self.ld.n)
        wsap = src.ap(0, 128, 0, W)
        if fwd:
            wdap = dram_ap(self.gcscr, 0, [[W, 4], [N, 32], [4 * W, 32], [1, N]])
        else:
            wdap = dram_ap(self.gcscr, 0, [[N, 32], [32 * W, 4], [W, 32], [1, N]])
        B.inc("sync", self.ld, lambda E, wsap=wsap, wdap=wdap: E.dma_start(out=wdap, in_=wsap), 16)
        B.wait("sync", self.ld, self.ld.n)
        rs = dst.ap(0, 128, 0, W)
        rd = dram_ap(self.gcscr, 0, [[W, 128], [1, W]])
        B.inc("sync", self.ld, lambda E, rs=rs, rd=rd: E.dma_start(out=rs, in_=rd), 16)
        B.wait("tensor", self.ld, self.ld.n)

    def _wcon_pipe(self, groups, evacs, tq, xstage, rstage, wcbuf):
        B = self.B
        NB = 4
        QT = 512
        B.wait("tensor", self.ev, self.ev.n)
        evcon2_at = {}
        mm_at = {}
        for g, pairs in enumerate(groups):
            m = g
            par = self.lda if m % 2 == 0 else self.ldb
            if m >= 2:
                B.wait("sync", self.ev, evcon2_at[m - 2])
                B.wait("sync", self.mm, mm_at[m - 2])
            B.wait("sync", par, par.n)
            B.inc("sync", par, lambda E, m=m, tq=tq, xstage=xstage: E.dma_start(
                out=xstage.ap(0, 128, (m % 2) * QT, QT),
                in_=dram_ap(self.xspill, m * 128 * S + tq * QT, [[S, 128], [1, QT]])), 16)
            B.inc("sync", par, lambda E, m=m, wcbuf=wcbuf: E.dma_start(
                out=wcbuf[m % 2].ap(),
                in_=dram_ap(self.wcscr, m * 128 * 32 * 128, [[32 * 128, 128], [1, 32 * 128]])), 16)
            pt = self.pa4[g % NB]
            bank = pt.offset // 512
            prev = self.bank_ev.get(bank)
            if prev:
                B.wait("tensor", self.ev, prev)
            B.wait("tensor", par, par.n)
            n = len(pairs)
            for i, (lh, rh) in enumerate(pairs):
                st_ = (i == 0)
                if i == n - 1:
                    B.inc("tensor", self.mm,
                          lambda E, lh=lh, rh=rh, pt=pt, st_=st_: E.matmul(pt, lh, rh, start=st_, stop=True))
                else:
                    B.run("tensor",
                          lambda E, lh=lh, rh=rh, pt=pt, st_=st_: E.matmul(pt, lh, rh, start=st_, stop=False))
            mm_at[m] = self.mm.n
            B.wait("vector", self.mm, self.mm.n)
            spar = self.sta if m % 2 == 0 else self.stb
            fns = evacs[g]
            B.wait("vector", spar, spar.n)
            B.wait("vector", self.ev, self.ev.n)
            B.inc("vector", self.vc, lambda E, f=fns[0], pt=pt: f(E, pt))
            B.wait("vector", self.vc, self.vc.n)
            B.wait("vector", par, par.n)
            B.inc("vector", self.ev, lambda E, f=fns[1], pt=pt: f(E, pt))
            evcon2_at[m] = self.ev.n
            self.bank_ev[bank] = self.ev.n
            B.wait("gpsimd", self.ev, self.ev.n)
            B.inc("gpsimd", spar, lambda E, m=m, tq=tq, rstage=rstage: E.dma_start(
                out=dram_ap(self.xspill2, m * 128 * S + tq * QT, [[S, 128], [1, QT]]),
                in_=rstage.ap(0, 128, (m % 2) * QT, QT)), 16)

    def emit_output(self):
        B = self.B
        s_o = ExitStack()
        otok = B.sbuf("otok", 128, D, F32, stack=s_o)
        for t in range(NT):
            for hf in range(2):
                if t > 0 or hf > 0:
                    B.wait("tensor", self.vc, self.vc.n)
                for j in range(4):
                    c = hf * 4 + j
                    fn = lambda E, c=c, j=j, hf=hf, t=t: E.transpose(
                        self.ps_a.ap(0, 128, hf * 1024 + j * 128, 128),
                        self.xf.ap(0, 128, c * S + t * 128, 128), self.eye32.ap())
                    if j == 3:
                        B.inc("tensor", self.mm, fn)
                    else:
                        B.run("tensor", fn)
                B.wait("vector", self.mm, self.mm.n)
                if t > 0:
                    B.wait("vector", self.st, self.st.n)
                B.inc("vector", self.vc, lambda E, hf=hf, otok=otok: E.tensor_copy(
                    otok.apd(hf * 512, [[D, 128], [128, 4], [1, 128]]),
                    self.ps_a.apd(hf * 1024, [[2048, 128], [128, 4], [1, 128]])))
            B.wait("gpsimd", self.vc, self.vc.n)
            B.inc("gpsimd", self.st, lambda E, t=t, otok=otok: E.dma_start(
                out=dram_ap(self.y_out, t * 128 * D, [[D, 128], [1, D]]), in_=otok.ap()), 16)
        B.wait("gpsimd", self.st, self.st.n)
        s_o.close()

    def build(self):
        self.alloc_xf()
        self.phase0()
        if self.n_stages >= 1:
            self.l0_attn()
        if self.n_stages >= 2:
            self.l0_ffn()
        if self.n_stages >= 3:
            self.l1_attn()
        if self.n_stages >= 4:
            self.l1_ffn()
        self.emit_output()
        self.s_xf.close()
        self.B.finalize()
        return self.nc


def build_program(n_stages=N_STAGES):
    return Prog(n_stages).build()


def _gsel_const():
    g = np.zeros((44, 23), dtype=ml_dtypes.bfloat16)
    for j in range(11):
        for tau in range(4):
            o = (1 << j) * tau
            slot = 4 * j + tau
            if o >= S:
                g[slot, 22] = 1.0
            else:
                g[slot, L1_OFFS.index(o)] = 1.0
    return g


def _stc_const():
    st = np.zeros((23, S), dtype=ml_dtypes.bfloat16)
    n = np.arange(S)
    for i, o in enumerate(L1_OFFS):
        st[i, n < o] = 1.0
    st[22] = 1.0
    return st


def _pmc_const():
    pm = np.zeros((128, 7 * 128), dtype=ml_dtypes.bfloat16)
    for i in range(7):
        st = 1 << i
        p = np.arange(128)
        pm[p, i * 128 + (p ^ st)] = 1.0
    return pm


def kernel(**inputs):
    nc = build_program(N_STAGES)
    x = np.ascontiguousarray(inputs["L0_attn_x"], dtype=np.float32)
    common = {
        "l0a_wq": np.ascontiguousarray(inputs["L0_attn_W_q"], dtype=np.float32),
        "l0a_wk": np.ascontiguousarray(inputs["L0_attn_W_k"], dtype=np.float32),
        "l0a_wv": np.ascontiguousarray(inputs["L0_attn_W_v"], dtype=np.float32),
        "l0a_wo": np.ascontiguousarray(inputs["L0_attn_W_o"], dtype=np.float32),
        "l0a_wdkv": np.ascontiguousarray(inputs["L0_attn_W_down_kv"], dtype=np.float32),
        "l0a_wupk": np.ascontiguousarray(inputs["L0_attn_W_up_k"], dtype=np.float32),
        "l0a_wupv": np.ascontiguousarray(inputs["L0_attn_W_up_v"], dtype=np.float32),
        "l0f_wexp": np.ascontiguousarray(inputs["L0_ffn_W_exp"], dtype=np.float32),
        "l0f_wcon": np.ascontiguousarray(inputs["L0_ffn_W_con"], dtype=np.float32),
        "l0f_bexp": np.ascontiguousarray(inputs["L0_ffn_b_exp"], dtype=np.float32).reshape(1, 4096),
        "l0f_upb": np.ascontiguousarray(inputs["L0_ffn_up_bias"], dtype=np.float32).reshape(1, 4096),
        "l0f_bcon": np.ascontiguousarray(inputs["L0_ffn_b_con"], dtype=np.float32).reshape(1, 1024),
        "l0f_dnb": np.ascontiguousarray(inputs["L0_ffn_down_bias"], dtype=np.float32).reshape(1, 1024),
        "l0f_upw": np.ascontiguousarray(inputs["L0_ffn_up_weights"], dtype=np.float32).reshape(12, 8192),
        "l0f_dnw": np.ascontiguousarray(inputs["L0_ffn_down_weights"], dtype=np.float32).reshape(12, 8192),
        "l0f_pmc": _pmc_const(),
        "l1a_wqkv": np.ascontiguousarray(inputs["L1_attn_w_qkv"], dtype=np.float32),
        "l1a_bqkv": np.ascontiguousarray(inputs["L1_attn_b_qkv"], dtype=np.float32).reshape(1, 3072),
        "l1a_wout": np.ascontiguousarray(inputs["L1_attn_w_out"], dtype=np.float32),
        "l1a_bout": np.ascontiguousarray(inputs["L1_attn_b_out"], dtype=np.float32).reshape(1, 1024),
        "l1a_wgate": np.ascontiguousarray(inputs["L1_attn_w_gate"], dtype=np.float32),
        "l1a_bgate": np.ascontiguousarray(inputs["L1_attn_b_gate"], dtype=np.float32).reshape(1, 1024),
        "l1a_pb": np.ascontiguousarray(inputs["L1_attn_pos_bias"], dtype=np.float32),
        "l1a_gsel": _gsel_const(),
        "l1a_stc": _stc_const(),
        "l1f_wg": np.ascontiguousarray(inputs["L1_ffn_w_gate"], dtype=np.float32),
        "l1f_wu": np.ascontiguousarray(inputs["L1_ffn_w_up"], dtype=np.float32),
        "l1f_wd": np.ascontiguousarray(inputs["L1_ffn_w_down"], dtype=np.float32),
        "l1f_clipx": np.ascontiguousarray(inputs["L1_ffn_clip_x"], dtype=np.float32).reshape(1, 1),
        "l1f_clipd": np.ascontiguousarray(inputs["L1_ffn_clip_down"], dtype=np.float32).reshape(1, 1),
    }
    in_maps = [dict(common, x=x[b]) for b in range(8)]
    res = run_bass_kernel_spmd(nc, in_maps, core_ids=list(range(8)))
    return np.stack([res.results[b]["y"] for b in range(8)], axis=0)
